# version 05
# baseline (speedup 1.0000x reference)
import contextlib
import math
import numpy as np
import ml_dtypes

import concourse.bass as bass
import concourse.mybir as mybir
from concourse.bass_utils import run_bass_kernel_spmd

F32 = mybir.dt.float32
BF16 = mybir.dt.bfloat16
AF = mybir.ActivationFunctionType
ALU = mybir.AluOpType
AX = mybir.AxisListType
BF = ml_dtypes.bfloat16

N_CORES = 8
B_FULL = 16
B_CORE = B_FULL // N_CORES
T = 2048
C = 1024
NCH = C // 128
NTC = T // 512

ENGINES = ("sync", "gpsimd", "tensor", "scalar", "vector")


class Prog:

    def __init__(self, nc, ctx):
        self.nc = nc
        self.ctx = ctx
        self.ops = {e: [] for e in ENGINES}
        self.semcnt = {}

    def sem(self, name):
        s = self.ctx.enter_context(self.nc.semaphore(name))
        self.semcnt[id(s)] = 0
        return s

    def op(self, eng, fn, waits=(), inc=None, amount=1):
        tok = None
        if inc is not None:
            self.semcnt[id(inc)] += amount
            tok = (inc, self.semcnt[id(inc)])
        self.ops[eng].append(
            (fn, tuple(w for w in waits if w is not None), inc, amount))
        return tok

    def dma(self, eng, out, in_, waits=(), inc=None, **kw):
        if inc is None:
            inc = self.dma_sem[eng]
        return self.op(eng, lambda e: e.dma_start(out=out, in_=in_, **kw),
                       waits=waits, inc=inc, amount=16)

    def barrier(self):
        toks = []
        for sem in self.dma_sem.values():
            if self.semcnt[id(sem)]:
                toks.append((sem, self.semcnt[id(sem)]))
        for sem in self.eng_sems:
            if self.semcnt[id(sem)]:
                toks.append((sem, self.semcnt[id(sem)]))
        for eng in ENGINES:
            self.op(eng, lambda e: e.nop(), waits=toks)

    def emit(self, block):
        def mk(name):
            def run(eng):
                waited = {}
                for fn, waits, inc, amount in self.ops[name]:
                    for (sem, cnt) in waits:
                        if waited.get(id(sem), 0) < cnt:
                            eng.wait_ge(sem, cnt)
                            waited[id(sem)] = cnt
                    inst = fn(eng)
                    if inc is not None:
                        inst.then_inc(inc, amount)
            return run
        block.sync(mk("sync"))
        block.gpsimd(mk("gpsimd"))
        block.tensor(mk("tensor"))
        block.scalar(mk("scalar"))
        block.vector(mk("vector"))


def _rope_tables_l0attn():
    inv_freq = 1.0 / (10000.0 ** (np.arange(0, 64, 2, dtype=np.float64) / 64))
    t = np.arange(T, dtype=np.float64)
    freqs = np.outer(t, inv_freq)
    emb = np.concatenate([freqs, freqs], axis=-1)
    cos = np.cos(emb).T.astype(np.float32)
    sin = np.sin(emb).T.astype(np.float32)
    return (np.concatenate([cos, cos], 0).astype(BF),
            np.concatenate([sin, sin], 0).astype(BF))


def _perm_rotate_half(dim_head):
    h = dim_head // 2
    P = np.zeros((128, 128), dtype=np.float32)
    for base in range(0, 128, dim_head):
        for m in range(h):
            P[base + m + h, base + m] = -1.0
            P[base + m, base + m + h] = 1.0
    return P.astype(BF)


def _tri_masks():
    k = np.arange(128)[:, None]
    i = np.arange(896)[None, :]
    return (i >= 384 + k).astype(np.float32).astype(BF)


def _consts_common():
    return {
        "ones128": np.ones((128, 1), dtype=BF),
        "ones64": np.ones((64, 1), dtype=BF),
        "onesr1": np.ones((1, 128), dtype=np.float32),
        "negr1": -np.ones((1, 128), dtype=np.float32),
        "onesr64": np.ones((1, 64), dtype=np.float32),
        "masks": _tri_masks(),
    }


class NS:
    def __init__(self, **kw):
        self.__dict__.update(kw)


def _slot8(w):
    return np.ascontiguousarray(
        np.stack([_chunked(w[:, j * 128:(j + 1) * 128]) for j in range(8)]))


def _chunked(w):
    w = np.asarray(w, dtype=np.float32)
    K, N = w.shape
    return np.ascontiguousarray(w.reshape(K // 128, 128, N).transpose(1, 0, 2)
                                .reshape(128, -1))


def tokmax(toks):
    best = {}
    for t in toks:
        if t is None:
            continue
        sem, cnt = t
        if best.get(id(sem), (None, -1))[1] < cnt:
            best[id(sem)] = (sem, cnt)
    return list(best.values())


GA = 0.044715
GS = 0.7978845608028654

SOFF_SUM = 0
SOFF_SQ = T
SOFF_MU2 = 2 * T
SOFF_REC = 2 * T
SOFF_MAXK = 2 * T + 512
STAT_W = 2 * T + 528


def emit_ln_rows(p, s, get_chunk, sqch, n_t=T):
    ntc = n_t // 512
    ST = s.stat
    z1 = p.op("vector", lambda e: e.memset(ST[0:1, 0:2 * n_t], 0.0),
              waits=tokmax(s.stat_users), inc=s.dve)
    s.stat_users = []
    acc_toks = [z1]
    s.chunk_consumed = {}
    for c in range(NCH):
        ap, ready = get_chunk(c)
        sq_tok = p.op("scalar", lambda e, ap=ap: e.activation(sqch, ap, AF.Square),
                      waits=list(ready) + ([s.sqch_free] if s.sqch_free else []),
                      inc=s.act)
        mm1 = mm2 = None
        for tc in range(ntc):
            tsl = slice(tc * 512, tc * 512 + 512)
            mm1 = p.op("tensor", lambda e, ap=ap, tsl=tsl: e.matmul(
                s.ps_s1[:], s.ones128[:], ap[:, tsl],
                start=True, stop=True, skip_group_check=True),
                waits=list(ready) + s.cgate + ([s.psfree[id(s.ps_s1)]] if s.psfree.get(id(s.ps_s1)) else []),
                inc=s.pe)
            a1 = p.op("vector", lambda e, tsl=tsl: e.tensor_add(
                ST[0:1, SOFF_SUM:SOFF_SUM + n_t][:, tsl],
                ST[0:1, SOFF_SUM:SOFF_SUM + n_t][:, tsl], s.ps_s1[:]),
                waits=[mm1, z1], inc=s.dve)
            s.psfree[id(s.ps_s1)] = a1
            mm2 = p.op("tensor", lambda e, tsl=tsl, sqch=sqch: e.matmul(
                s.ps_s2[:], s.ones128[:], sqch[:, tsl],
                start=True, stop=True, skip_group_check=True),
                waits=[sq_tok] + ([s.psfree[id(s.ps_s2)]] if s.psfree.get(id(s.ps_s2)) else []),
                inc=s.pe)
            a2 = p.op("vector", lambda e, tsl=tsl: e.tensor_add(
                ST[0:1, SOFF_SQ:SOFF_SQ + n_t][:, tsl],
                ST[0:1, SOFF_SQ:SOFF_SQ + n_t][:, tsl], s.ps_s2[:]),
                waits=[mm2, z1], inc=s.dve)
            s.psfree[id(s.ps_s2)] = a2
            acc_toks += [a1, a2]
        s.sqch_free = acc_toks[-1]
        s.chunk_consumed[c] = [mm1, sq_tok]
    bc_toks = []
    for tc in range(ntc):
        mu = ST[0:1, SOFF_SUM + tc * 512:SOFF_SUM + tc * 512 + 512]
        rs = ST[0:1, SOFF_SQ + tc * 512:SOFF_SQ + tc * 512 + 512]
        mu2 = ST[0:1, SOFF_MU2:SOFF_MU2 + 512]
        t_mu = p.op("vector", lambda e, mu=mu: e.tensor_scalar_mul(mu, mu, 1.0 / C),
                    waits=acc_toks, inc=s.dve)
        t_mu2 = p.op("scalar", lambda e, mu=mu, mu2=mu2: e.activation(mu2, mu, AF.Square),
                     waits=[t_mu] + ([s.mu2_free] if getattr(s, "mu2_free", None) else []),
                     inc=s.act)
        t_var = p.op("vector", lambda e, rs=rs, mu2=mu2: e.scalar_tensor_tensor(
            rs, rs, 1.0 / C, mu2, ALU.mult, ALU.subtract),
            waits=[t_mu2] + acc_toks, inc=s.dve)
        s.mu2_free = t_var
        t_ln = p.op("scalar", lambda e, rs=rs: e.activation(rs, rs, AF.Ln, bias=s.eps[0:1, 0:1]),
                    waits=[t_var, s.eps_tok], inc=s.act)
        t_rstd = p.op("scalar", lambda e, rs=rs: e.activation(rs, rs, AF.Exp, scale=-0.5),
                      waits=[t_ln], inc=s.act)
        t_mr = p.op("vector", lambda e, mu=mu, rs=rs: e.tensor_mul(mu, mu, rs),
                    waits=[t_rstd], inc=s.dve)
        b1 = p.op("tensor", lambda e, rs=rs: e.matmul(
            s.ps_c[:], s.onesr1[:], rs, start=True, stop=True, skip_group_check=True),
            waits=[t_rstd] + s.cgate + ([s.psfree[id(s.ps_c)]] if s.psfree.get(id(s.ps_c)) else []), inc=s.pe)
        b2 = p.op("tensor", lambda e, mu=mu: e.matmul(
            s.ps_d[:], s.negr1[:], mu, start=True, stop=True, skip_group_check=True),
            waits=[t_mr] + s.cgate + ([s.psfree[id(s.ps_d)]] if s.psfree.get(id(s.ps_d)) else []), inc=s.pe)
        c1 = p.op("scalar", lambda e, tc=tc: e.copy(
            s.rstdB[:, tc * 512:tc * 512 + 512], s.ps_c[:]), waits=[b1], inc=s.act)
        c2 = p.op("scalar", lambda e, tc=tc: e.copy(
            s.nmrB[:, tc * 512:tc * 512 + 512], s.ps_d[:]), waits=[b2], inc=s.act)
        s.psfree[id(s.ps_c)] = c1
        s.psfree[id(s.ps_d)] = c2
        bc_toks += [c1, c2]
    s.stat_users += bc_toks
    return bc_toks




def emit_lnin_streamed(p, s, xt_ap, bsl, dst, XC, sqch, TMPF, bgate, dst_of=None):
    if dst_of is None:
        dst_of = lambda c: dst[:, c * T:(c + 1) * T]
    def get_chunk(cc):
        slot = cc % 2
        if cc < 2:
            w = list(bgate) + s.cgate
        else:
            w = s.chunk_consumed[cc - 2]
        ld = p.dma("gpsimd", XC[:, slot * T:(slot + 1) * T],
                   xt_ap[cc, :, bsl], waits=w)
        return XC[:, slot * T:(slot + 1) * T], [ld]
    bc = emit_ln_rows(p, s, get_chunk, sqch)
    xln_toks = []
    for c in range(NCH):
        slot = c % 2
        w = bc + ([xln_toks[c - 2]] if c >= 2 else [])
        ld = p.dma("gpsimd", XC[:, slot * T:(slot + 1) * T],
                   xt_ap[c, :, bsl], waits=w)
        t2 = None
        for tc in range(NTC):
            tsl = slice(tc * 512, tc * 512 + 512)
            t1 = p.op("vector", lambda e, slot=slot, tsl=tsl: e.tensor_mul(
                TMPF[:], XC[:, slot * T:(slot + 1) * T][:, tsl], s.rstdB[:, tsl]),
                waits=[ld] + bc + ([s.tmpf_free] if s.tmpf_free else []),
                inc=s.dve)
            t2 = p.op("vector", lambda e, c=c, tsl=tsl: e.tensor_add(
                dst_of(c)[:, tsl], TMPF[:], s.nmrB[:, tsl]),
                waits=[t1], inc=s.dve)
            s.tmpf_free = t2
        xln_toks.append(t2)
    return xln_toks


def emit_lnout_add_store(p, s, yb, yb_toks, sqch, xt_ap, xout_ap, bsl, XS, TMPF,
                         yb_of=None):
    if yb_of is None:
        yb_of = lambda c: yb[:, c * T:(c + 1) * T]
    def get_chunk(cc):
        return yb_of(cc), yb_toks[cc]
    bc2 = emit_ln_rows(p, s, get_chunk, sqch)
    st_toks = []
    for c in range(NCH):
        ld = p.dma("sync", XS[:], xt_ap[c, :, bsl],
                   waits=([st_toks[c - 1]] if c >= 1 else
                          ([s.xs_gate] if getattr(s, "xs_gate", None) else [])))
        t3 = None
        for tc in range(NTC):
            tsl = slice(tc * 512, tc * 512 + 512)
            t1 = p.op("vector", lambda e, c=c, tsl=tsl: e.tensor_mul(
                TMPF[:], yb_of(c)[:, tsl], s.rstdB[:, tsl]),
                waits=bc2 + ([s.tmpf_free] if s.tmpf_free else []), inc=s.dve)
            t2 = p.op("vector", lambda e, tsl=tsl: e.tensor_add(
                TMPF[:], TMPF[:], s.nmrB[:, tsl]), waits=[t1], inc=s.dve)
            t3 = p.op("vector", lambda e, tsl=tsl: e.tensor_add(
                XS[:, tsl], XS[:, tsl], TMPF[:]), waits=[t2, ld], inc=s.dve)
            s.tmpf_free = t3
        st = p.dma("sync", xout_ap[c, :, bsl], XS[:], waits=[t3])
        st_toks.append(st)
    return st_toks


def _emit_stage_a(nc, ctx, p, xt, wqt, wkt, wvt, wot, cos0, sin0, perm_in, masks_in, xout):
    sb = lambda name, shape, dt: ctx.enter_context(nc.sbuf_tensor('a_' + name, shape, dt))
    psumt = lambda name, shape: ctx.enter_context(nc.psum_tensor('a_' + name, shape, F32))

    SCA = sb("SCA", [128, NCH * T], BF16)
    SCB = sb("SCB", [128, NCH * T], BF16)
    V65 = sb("V65", [128, 16 * 1040], BF16)
    QR2 = sb("QR2", [128, T], BF16)
    KR2 = sb("KR2", [128, T], BF16)
    QH = sb("QH", [65, T], BF16)
    KH = sb("KH", [65, T], BF16)
    PT = sb("PT", [128, 4 * 512], BF16)
    OT2 = sb("OT2", [64, 512], BF16)
    OTN = sb("OTN", [64, 512], BF16)
    XC = sb("XC", [128, 2 * T], BF16)
    XS = sb("XS", [128, T], F32)
    WBIG = sb("WBIG", [128, NCH * C], BF16)
    WSM = sb("WSM", [128, 2 * C], BF16)
    COS = sb("COS", [128, T], BF16)
    SIN = sb("SIN", [128, T], BF16)
    PERM = sb("PERM", [128, 128], BF16)
    ONES128 = sb("ONES128", [128, 1], BF16)
    ONES64 = sb("ONES64", [64, 1], BF16)
    ONESR1 = sb("ONESR1", [1, 128], F32)
    NEGR1 = sb("NEGR1", [1, 128], F32)
    ONESR64 = sb("ONESR64", [1, 64], F32)
    MASKS = sb("MASKS", [128, 896], BF16)
    STAT = sb("STAT", [1, STAT_W], F32)
    EPS = sb("EPS", [1, 8], F32)
    RSTDB = sb("RSTDB", [128, T], F32)
    NMRB = sb("NMRB", [128, T], BF16)
    TMPF = sb("TMPF", [128, 512], F32)
    TMPG = sb("TMPG", [128, 512], F32)

    ps_a = psumt("ps_a", [128, 512])
    ps_b = psumt("ps_b", [128, 512])
    ps_c = psumt("ps_c", [128, 512])
    ps_d = psumt("ps_d", [128, 512])
    ps_o = psumt("ps_o", [65, 512])
    ps_r = psumt("ps_r", [64, 512])
    ps_s1 = psumt("ps_s1", [1, 512])
    ps_s2 = psumt("ps_s2", [1, 512])

    s = NS()
    s.pe, s.act, s.dve = p.eng_sems

    s.ones128, s.onesr1, s.negr1 = ONES128, ONESR1, NEGR1
    s.stat = STAT
    s.rstdB, s.nmrB = RSTDB, NMRB
    s.ps_s1, s.ps_s2, s.ps_c, s.ps_d = ps_s1, ps_s2, ps_c, ps_d
    s.psfree = {}
    s.sqch_free = None
    s.stat_users = []
    s.tmpf_free = s.tmpg_free = None

    const_toks = []
    for dst, srcap in [(COS, cos0), (SIN, sin0), (PERM, perm_in),
                       (MASKS, masks_in)]:
        const_toks.append(p.dma("sync", dst[:], srcap))
    for tns, val in [(ONES128, 1.0), (ONES64, 1.0), (ONESR1, 1.0),
                     (NEGR1, -1.0), (ONESR64, 1.0)]:
        const_toks.append(p.op("vector", lambda e, tns=tns, val=val: e.memset(tns[:], val),
                               inc=s.dve))
    s.cgate = tokmax(const_toks)
    s.eps = EPS
    s.eps_tok = p.op("vector", lambda e: e.memset(EPS[:], 1e-5), inc=s.dve)
    khm1 = p.op("vector", lambda e: e.memset(KH[64:65, :], -1.0), inc=s.dve)
    ones_v = p.op("vector", lambda e: e.memset(
        V65.ap().rearrange("p (tt h o) -> p tt h o", tt=16, h=16)[:, :, :, 64:65],
        1.0), inc=s.dve)

    BPAD = 1.02
    SM_SCALE = 1.0 / math.sqrt(64)
    store_last = None

    for b in range(B_CORE):
        bsl = slice(b * T, (b + 1) * T)
        bgate = [store_last] if store_last else []

        wv_tok = p.dma("gpsimd", WBIG[:], wvt,
                       waits=bgate + ([s.psfree.get("wbig")] if s.psfree.get("wbig") else []))

        xln_toks = emit_lnin_streamed(p, s, xt, bsl, SCA, XC, KR2[:], TMPF, bgate)
        xln_last = xln_toks[-1]

        for tt in range(16):
            for half in range(2):
                pst = ps_a if half == 0 else ps_b
                mm_tok = None
                for cch in range(NCH):
                    w = []
                    if cch == 0:
                        w = [wv_tok, xln_last, ones_v]
                        ft = s.psfree.get(id(pst))
                        if ft is not None:
                            w.append(ft)
                    mm_tok = p.op("tensor", lambda e, cch=cch, tt=tt, half=half, pst=pst: e.matmul(
                        pst[:], SCA[:, cch * T + tt * 128: cch * T + tt * 128 + 128],
                        WBIG[:, cch * C + half * 512: cch * C + half * 512 + 512],
                        start=(cch == 0), stop=(cch == NCH - 1), skip_group_check=True),
                        waits=w, inc=(s.pe if cch == NCH - 1 else None))

                def evac_v(e, tt=tt, half=half, pst=pst):
                    src_ap = pst.ap().rearrange("p (h d) -> p h d", h=8)
                    dst = V65.ap().rearrange("p (t2 h o) -> p t2 h o", t2=16, h=16)[
                        :, tt, half * 8:(half + 1) * 8, 0:64]
                    return e.activation(dst, src_ap, AF.Copy)
                ev = p.op("scalar", evac_v, waits=[mm_tok], inc=s.act)
                s.psfree[id(pst)] = ev
                vlast = ev
        s.psfree["wbig"] = vlast

        o_toks = []
        prev_qn = None
        prerope = XC[:, T:2 * T]
        for hp in range(8):
            pr_toks = {}
            for (which, wdram, slot) in (("q", wqt, 0), ("k", wkt, 1)):
                wtok = p.dma("gpsimd", WSM[:, slot * C:(slot + 1) * C], wdram[hp],
                             waits=[s.psfree.get(("wsm", slot))] if s.psfree.get(("wsm", slot)) else [])
                banks = [ps_a, ps_b, ps_c, ps_d]
                mm_last = {}
                for tc in range(NTC):
                    for cch in range(NCH):
                        pst = banks[tc]
                        w = []
                        if cch == 0:
                            w = [wtok, xln_last]
                            ft = s.psfree.get(id(pst))
                            if ft is not None:
                                w.append(ft)
                        mm_last[tc] = p.op("tensor", lambda e, cch=cch, tc=tc, pst=pst, slot=slot: e.matmul(
                            pst[:], WSM[:, slot * C + cch * 128: slot * C + cch * 128 + 128],
                            SCA[:, cch * T + tc * 512: cch * T + tc * 512 + 512],
                            start=(cch == 0), stop=(cch == NCH - 1), skip_group_check=True),
                            waits=w, inc=(s.pe if cch == NCH - 1 else None))
                rope_toks = []
                for tc in range(NTC):
                    tsl = slice(tc * 512, tc * 512 + 512)
                    pst = banks[tc]
                    ev = p.op("scalar", lambda e, tsl=tsl, pst=pst: e.copy(prerope[:, tsl], pst[:]),
                              waits=[mm_last[tc]] + ([s.prerope_free] if getattr(s, "prerope_free", None) else []),
                              inc=s.act)
                    s.psfree[id(pst)] = ev
                    rot = p.op("tensor", lambda e, tsl=tsl, pst=pst: e.matmul(
                        pst[:], PERM[:], prerope[:, tsl], start=True, stop=True,
                        skip_group_check=True),
                        waits=[ev] + s.cgate, inc=s.pe)
                    t1 = p.op("vector", lambda e, tsl=tsl: e.tensor_mul(
                        TMPF[:], prerope[:, tsl], COS[:, tsl]),
                        waits=[ev] + s.cgate + ([s.tmpf_free] if s.tmpf_free else []), inc=s.dve)
                    t2 = p.op("vector", lambda e, tsl=tsl, pst=pst: e.tensor_mul(
                        TMPG[:], pst[:], SIN[:, tsl]),
                        waits=[rot] + ([s.tmpg_free] if s.tmpg_free else []),
                        inc=s.dve)
                    s.psfree[id(pst)] = t2
                    dstb = QR2 if which == "q" else KR2
                    t3 = p.op("vector", lambda e, tsl=tsl, dstb=dstb: e.tensor_add(
                        dstb[:, tsl], TMPF[:], TMPG[:]), waits=[t1, t2], inc=s.dve)
                    s.tmpf_free = t3
                    s.tmpg_free = t3
                    rope_toks.append(t3)
                s.psfree[("wsm", slot)] = rope_toks[-1]
                s.prerope_free = rope_toks[-1]
                pr_toks[which] = rope_toks

            for half in range(2):
                h = hp * 2 + half
                sqh = prerope[0:64, :]
                ksq = p.op("scalar", lambda e, half=half, sqh=sqh: e.activation(
                    sqh, KR2[64 * half:64 * half + 64, :], AF.Square),
                    waits=pr_toks["k"] + pr_toks["q"] + ([prev_qn] if prev_qn else []),
                    inc=s.act)
                kn_toks = []
                for tc in range(NTC):
                    mm = p.op("tensor", lambda e, tc=tc, sqh=sqh: e.matmul(
                        ps_s1[:], ONES64[:], sqh[:, tc * 512:(tc + 1) * 512],
                        start=True, stop=True, skip_group_check=True),
                        waits=[ksq] + s.cgate + ([s.psfree[id(s.ps_s1)]] if s.psfree.get(id(s.ps_s1)) else []), inc=s.pe)
                    kn = p.op("scalar", lambda e, tc=tc: e.activation(
                        STAT[0:1, SOFF_SUM + tc * 512:SOFF_SUM + tc * 512 + 512],
                        ps_s1[:], AF.Sqrt), waits=[mm], inc=s.act)
                    s.psfree[id(s.ps_s1)] = kn
                    kn_toks.append(kn)
                maxk = p.op("vector", lambda e: e.reduce_max(
                    STAT[0:1, SOFF_MAXK:SOFF_MAXK + 1],
                    STAT[0:1, SOFF_SUM:SOFF_SUM + T], axis=AX.X),
                    waits=kn_toks, inc=s.dve)
                qsq = p.op("scalar", lambda e, half=half, sqh=sqh: e.activation(
                    sqh, QR2[64 * half:64 * half + 64, :], AF.Square),
                    waits=kn_toks + [ksq], inc=s.act)
                qn_toks = []
                for tc in range(NTC):
                    mm = p.op("tensor", lambda e, tc=tc, sqh=sqh: e.matmul(
                        ps_s1[:], ONES64[:], sqh[:, tc * 512:(tc + 1) * 512],
                        start=True, stop=True, skip_group_check=True),
                        waits=[qsq, maxk] + s.cgate + ([s.psfree[id(s.ps_s1)]] if s.psfree.get(id(s.ps_s1)) else []),
                        inc=s.pe)
                    qn = p.op("scalar", lambda e, tc=tc: e.activation(
                        STAT[0:1, SOFF_SUM + tc * 512:SOFF_SUM + tc * 512 + 512],
                        ps_s1[:], AF.Sqrt), waits=[mm], inc=s.act)
                    s.psfree[id(s.ps_s1)] = qn
                    qn_toks.append(qn)
                prev_qn = qn_toks[-1]
                s.prerope_free = qn_toks[-1]
                bq = p.op("vector", lambda e: e.tensor_scalar(
                    STAT[0:1, SOFF_SQ:SOFF_SQ + T],
                    STAT[0:1, SOFF_SUM:SOFF_SUM + T],
                    STAT[0:1, SOFF_MAXK:SOFF_MAXK + 1], BPAD,
                    ALU.mult, ALU.mult),
                    waits=qn_toks + [maxk], inc=s.dve)
                prev_o = tokmax(o_toks)
                qmove = p.op("scalar", lambda e, half=half: e.copy(
                    QH[0:64, :], QR2[64 * half:64 * half + 64, :]),
                    waits=pr_toks["q"] + prev_o, inc=s.act)
                kmove = p.op("scalar", lambda e, half=half: e.copy(
                    KH[0:64, :], KR2[64 * half:64 * half + 64, :]),
                    waits=pr_toks["k"] + [khm1] + prev_o, inc=s.act)
                brow = p.op("scalar", lambda e: e.copy(
                    QH[64:65, :], STAT[0:1, SOFF_SQ:SOFF_SQ + T]),
                    waits=[bq, qmove], inc=s.act)
                s.stat_users += [bq, brow, maxk]

                for qch in range(4):
                    nkt = 4 * qch + 4
                    pv_tok = None
                    for kt in range(nkt):
                        pst = ps_a if kt % 2 == 0 else ps_b
                        ptslot = kt % 4
                        stw = [brow, kmove]
                        ft = s.psfree.get(id(pst))
                        if ft is not None:
                            stw.append(ft)
                        st = p.op("tensor", lambda e, kt=kt, qch=qch, pst=pst: e.matmul(
                            pst[:], KH[:, kt * 128:(kt + 1) * 128],
                            QH[:, qch * 512:(qch + 1) * 512],
                            start=True, stop=True, skip_group_check=True),
                            waits=stw, inc=s.pe)
                        expw = [st]
                        ptfree = s.psfree.get(("pt", ptslot))
                        if ptfree is not None:
                            expw.append(ptfree)
                        ex = p.op("scalar", lambda e, ptslot=ptslot, pst=pst: e.activation(
                            PT[:, ptslot * 512:(ptslot + 1) * 512], pst[:], AF.Exp,
                            scale=SM_SCALE),
                            waits=expw, inc=s.act)
                        s.psfree[id(pst)] = ex
                        if kt >= 4 * qch:
                            j = kt - 4 * qch
                            ex = p.op("vector", lambda e, ptslot=ptslot, j=j: e.tensor_mul(
                                PT[:, ptslot * 512:(ptslot + 1) * 512],
                                PT[:, ptslot * 512:(ptslot + 1) * 512],
                                MASKS[:, (3 - j) * 128:(3 - j) * 128 + 512]),
                                waits=[ex] + s.cgate, inc=s.dve)
                        pvw = [ex]
                        if kt == 0:
                            ft = s.psfree.get(id(ps_o))
                            if ft is not None:
                                pvw.append(ft)
                            pvw.append(vlast)
                            pvw.append(ones_v)
                        pv_tok = p.op("tensor", lambda e, kt=kt, h=h, ptslot=ptslot: e.matmul(
                            ps_o[:], V65[:, kt * 1040 + h * 65: kt * 1040 + h * 65 + 65],
                            PT[:, ptslot * 512:(ptslot + 1) * 512],
                            start=(kt == 0), stop=(kt == nkt - 1), skip_group_check=True),
                            waits=pvw, inc=(s.pe if kt == nkt - 1 else None))
                        s.psfree[("pt", ptslot)] = pv_tok
                    rec = p.op("vector", lambda e: e.reciprocal(
                        STAT[0:1, SOFF_REC:SOFF_REC + 512], ps_o[64:65, :]),
                        waits=[pv_tok] + ([s.rec_free] if getattr(s, "rec_free", None) else []),
                        inc=s.dve)
                    otmp = p.op("scalar", lambda e: e.copy(OT2[:], ps_o[0:64, :]),
                                waits=[pv_tok] + ([s.ot2_free] if getattr(s, "ot2_free", None) else []),
                                inc=s.act)
                    s.psfree[id(ps_o)] = otmp
                    rb = p.op("tensor", lambda e: e.matmul(
                        ps_r[:], ONESR64[:], STAT[0:1, SOFF_REC:SOFF_REC + 512],
                        start=True, stop=True, skip_group_check=True),
                        waits=[rec] + s.cgate + ([s.ps_r_free] if getattr(s, "ps_r_free", None) else []),
                        inc=s.pe)
                    s.rec_free = rb
                    s.mu2_free = rb
                    odn = p.op("vector", lambda e: e.tensor_mul(OTN[:], OT2[:], ps_r[:]),
                               waits=[rb, otmp] + ([s.otn_free] if getattr(s, "otn_free", None) else []),
                               inc=s.dve)
                    s.ps_r_free = odn
                    s.ot2_free = odn
                    oc = p.op("scalar", lambda e, h=h, qch=qch: e.copy(
                        SCB[:, (h // 2) * T + qch * 512:(h // 2) * T + qch * 512 + 512][
                            64 * (h % 2):64 * (h % 2) + 64, :], OTN[:]),
                        waits=[odn], inc=s.act)
                    s.otn_free = oc
                    s.stat_users += [rb]
                    o_toks.append(oc)
        o_last = tokmax(o_toks)

        yb_toks = {}
        for outc in range(NCH):
            slot = outc % 2
            wtok = p.dma("gpsimd", WSM[:, slot * C:(slot + 1) * C], wot[outc],
                         waits=[s.psfree.get(("wsm", slot))] if s.psfree.get(("wsm", slot)) else [])
            banks = [ps_a, ps_b, ps_c, ps_d]
            mm_last = {}
            for tc in range(NTC):
                for cch in range(NCH):
                    pst = banks[tc]
                    w = []
                    if cch == 0:
                        w = [wtok] + o_last
                        ft = s.psfree.get(id(pst))
                        if ft is not None:
                            w.append(ft)
                    mm_last[tc] = p.op("tensor", lambda e, cch=cch, tc=tc, pst=pst, slot=slot: e.matmul(
                        pst[:], WSM[:, slot * C + cch * 128: slot * C + cch * 128 + 128],
                        SCB[:, cch * T + tc * 512: cch * T + tc * 512 + 512],
                        start=(cch == 0), stop=(cch == NCH - 1), skip_group_check=True),
                        waits=w, inc=(s.pe if cch == NCH - 1 else None))
            evs = []
            for tc in range(NTC):
                tsl = slice(tc * 512, tc * 512 + 512)
                pst = banks[tc]
                ev = p.op("scalar", lambda e, outc=outc, tsl=tsl, pst=pst: e.copy(
                    SCA[:, outc * T:(outc + 1) * T][:, tsl], pst[:]),
                    waits=[mm_last[tc]], inc=s.act)
                s.psfree[id(pst)] = ev
                evs.append(ev)
            s.psfree[("wsm", slot)] = evs[-1]
            yb_toks[outc] = evs

        st_toks = emit_lnout_add_store(p, s, SCA, yb_toks, XC[:, 0:T],
                                       xt, xout, bsl, XS, TMPF)
        store_last = st_toks[-1]

    return store_last


def build_stage_a():
    nc = bass.Bass("TRN2", target_bir_lowering=False)
    xt = nc.dram_tensor("xt", [NCH, 128, B_CORE * T], F32, kind="ExternalInput").ap()
    wqt = nc.dram_tensor("wqt", [8, 128, NCH * 128], F32, kind="ExternalInput").ap()
    wkt = nc.dram_tensor("wkt", [8, 128, NCH * 128], F32, kind="ExternalInput").ap()
    wvt = nc.dram_tensor("wvt", [128, NCH * C], F32, kind="ExternalInput").ap()
    wot = nc.dram_tensor("wot", [8, 128, NCH * 128], F32, kind="ExternalInput").ap()
    cos0 = nc.dram_tensor("cos0", [128, T], BF16, kind="ExternalInput").ap()
    sin0 = nc.dram_tensor("sin0", [128, T], BF16, kind="ExternalInput").ap()
    perm_in = nc.dram_tensor("perm", [128, 128], BF16, kind="ExternalInput").ap()
    masks_in = nc.dram_tensor("masks", [128, 896], BF16, kind="ExternalInput").ap()
    xout = nc.dram_tensor("xout", [NCH, 128, B_CORE * T], F32, kind="ExternalOutput").ap()

    with contextlib.ExitStack() as ctx:
        p = Prog(nc, ctx)
        p.dma_sem = {"sync": p.sem("s_dsync"), "gpsimd": p.sem("s_dgp")}
        p.eng_sems = (p.sem("s_pe"), p.sem("s_act"), p.sem("s_dve"))
        with contextlib.ExitStack() as c1:
            store_last = _emit_stage_a(nc, c1, p, xt, wqt, wkt, wvt, wot, cos0, sin0, perm_in, masks_in, xout)
        p.op("sync", lambda e: e.nop(), waits=[store_last])
        with nc.Block() as block:
            p.emit(block)
    return nc

def _emit_stage_d(nc, ctx, p, xt, u1, v1, b1in, u2, v2, b2in, xout):
    sb = lambda name, shape, dt: ctx.enter_context(nc.sbuf_tensor('d_' + name, shape, dt))
    psumt = lambda name, shape: ctx.enter_context(nc.psum_tensor('d_' + name, shape, F32))

    SCA = sb("SCA", [128, NCH * T], BF16)
    PB = sb("PB", [128, 2 * T], BF16)
    QB = sb("QB", [128, 2 * T], BF16)
    HB = sb("HB", [128, 32 * 512], BF16)
    U1SB = sb("U1SB", [128, NCH * 256], BF16)
    V1SB = sb("V1SB", [128, 2 * 4096], BF16)
    U2SB = sb("U2SB", [128, 32 * 256], BF16)
    V2SB = sb("V2SB", [128, 2 * C], BF16)
    B1SB = sb("B1SB", [128, 32], F32)
    B2SB = sb("B2SB", [128, 8], F32)
    XC = sb("XC", [128, 2 * T], BF16)
    XS = sb("XS", [128, T], F32)
    ONES128 = sb("ONES128", [128, 1], BF16)
    ONESR1 = sb("ONESR1", [1, 128], F32)
    NEGR1 = sb("NEGR1", [1, 128], F32)
    STAT = sb("STAT", [1, STAT_W], F32)
    EPS = sb("EPS", [1, 8], F32)
    RSTDB = sb("RSTDB", [128, T], F32)
    NMRB = sb("NMRB", [128, T], BF16)
    TMPF = sb("TMPF", [128, 512], F32)
    VB = sb("VB", [128, 512], F32)
    SQ = sb("SQ", [128, 512], F32)
    SQCH = sb("SQCH", [128, T], BF16)

    ps_a = psumt("ps_a", [128, 512])
    ps_b = psumt("ps_b", [128, 512])
    ps_c = psumt("ps_c", [128, 512])
    ps_d = psumt("ps_d", [128, 512])
    ps_s1 = psumt("ps_s1", [1, 512])
    ps_s2 = psumt("ps_s2", [1, 512])

    s = NS()
    s.pe, s.act, s.dve = p.eng_sems
    s.ones128, s.onesr1, s.negr1 = ONES128, ONESR1, NEGR1
    s.stat = STAT
    s.rstdB, s.nmrB = RSTDB, NMRB
    s.ps_s1, s.ps_s2, s.ps_c, s.ps_d = ps_s1, ps_s2, ps_c, ps_d
    s.psfree = {}
    s.sqch_free = None
    s.stat_users = []
    s.tmpf_free = None
    s.eps = EPS

    const_toks = []
    for dst, srcin in [(B1SB, b1in), (B2SB, b2in)]:
        const_toks.append(p.dma("sync", dst[:], srcin))
    for tns, val in [(ONES128, 1.0), (ONESR1, 1.0), (NEGR1, -1.0)]:
        const_toks.append(p.op("vector", lambda e, tns=tns, val=val: e.memset(tns[:], val),
                               inc=s.dve))
    for dst, srcin in [(U1SB, u1), (U2SB, u2), (V1SB, v1), (V2SB, v2)]:
        const_toks.append(p.dma("gpsimd", dst[:], srcin))
    s.cgate = tokmax(const_toks)
    s.eps_tok = p.op("vector", lambda e: e.memset(EPS[:], 1e-5), inc=s.dve)

    store_last = None
    for b in range(B_CORE):
        bsl = slice(b * T, (b + 1) * T)
        bgate = [store_last] if store_last else []

        xln_toks = emit_lnin_streamed(p, s, xt, bsl, SCA, XC, SQCH[:], TMPF, bgate)

        pb_toks = []
        for e2 in range(2):
            for tc in range(NTC):
                tsl = slice(tc * 512, tc * 512 + 512)
                pst = ps_a if tc % 2 == 0 else ps_b
                mm_tok = None
                for cch in range(NCH):
                    w = []
                    if cch == 0:
                        w = s.cgate + [xln_toks[-1]]
                        ft = s.psfree.get(id(pst))
                        if ft is not None:
                            w.append(ft)
                    mm_tok = p.op("tensor", lambda e, cch=cch, e2=e2, tsl=tsl, pst=pst: e.matmul(
                        pst[:], U1SB[:, cch * 256 + e2 * 128: cch * 256 + e2 * 128 + 128],
                        SCA[:, cch * T:(cch + 1) * T][:, tsl],
                        start=(cch == 0), stop=(cch == NCH - 1), skip_group_check=True),
                        waits=w, inc=(s.pe if cch == NCH - 1 else None))
                ev = p.op("scalar", lambda e, e2=e2, tsl=tsl, pst=pst: e.copy(
                    PB[:, e2 * T:(e2 + 1) * T][:, tsl], pst[:]),
                    waits=[mm_tok] + bgate, inc=s.act)
                s.psfree[id(pst)] = ev
                pb_toks.append(ev)
        pb_last = pb_toks[-1]

        qb_toks = []
        hb_dumped = False
        for tc in range(NTC):
            tsl = slice(tc * 512, tc * 512 + 512)
            hb_toks = []
            for f in range(32):
                pst = ps_a if f % 2 == 0 else ps_b
                mm_tok = None
                for k2 in range(2):
                    w = []
                    if k2 == 0:
                        w = [pb_last]
                        ft = s.psfree.get(id(pst))
                        if ft is not None:
                            w.append(ft)
                    mm_tok = p.op("tensor", lambda e, k2=k2, f=f, tsl=tsl, pst=pst: e.matmul(
                        pst[:], V1SB[:, k2 * 4096 + f * 128: k2 * 4096 + f * 128 + 128],
                        PB[:, k2 * T:(k2 + 1) * T][:, tsl],
                        start=(k2 == 0), stop=(k2 == 1), skip_group_check=True),
                        waits=w, inc=(s.pe if k2 == 1 else None))
                vfree = getattr(s, "vb_free", None)
                t_v = p.op("scalar", lambda e, f=f, pst=pst: e.activation(
                    VB[:], pst[:], AF.Identity, bias=B1SB[:, f:f + 1]),
                    waits=[mm_tok] + ([vfree] if vfree else []), inc=s.act)
                t_sq = p.op("scalar", lambda e, f=f, pst=pst: e.activation(
                    SQ[:], pst[:], AF.Square, bias=B1SB[:, f:f + 1]),
                    waits=[mm_tok] + ([getattr(s, "sq_free", None)] if getattr(s, "sq_free", None) else []),
                    inc=s.act)
                s.psfree[id(pst)] = t_sq
                t_u2 = p.op("vector", lambda e: e.tensor_scalar(
                    SQ[:], SQ[:], GA, 1.0, ALU.mult, ALU.add),
                    waits=[t_sq], inc=s.dve)
                t_w = p.op("vector", lambda e: e.tensor_mul(SQ[:], SQ[:], VB[:]),
                           waits=[t_u2, t_v], inc=s.dve)
                t_t = p.op("scalar", lambda e: e.activation(SQ[:], SQ[:], AF.Tanh, scale=GS),
                           waits=[t_w], inc=s.act)
                t_u3 = p.op("vector", lambda e: e.tensor_scalar(
                    SQ[:], SQ[:], 0.5, 0.5, ALU.mult, ALU.add),
                    waits=[t_t], inc=s.dve)
                t_h = p.op("vector", lambda e, f=f: e.tensor_mul(
                    HB[:, f * 512:(f + 1) * 512], SQ[:], VB[:]),
                    waits=[t_u3] + ([s.psfree.get(("hb", f))] if s.psfree.get(("hb", f)) else []),
                    inc=s.dve)
                s.vb_free = t_h
                s.sq_free = t_h
                hb_toks.append(t_h)
            for outr in range(2):
                pst = ps_c
                mm_tok = None
                for f in range(32):
                    w = [hb_toks[f]]
                    if f == 0:
                        ft = s.psfree.get(id(s.ps_c))
                        if ft is not None:
                            w.append(ft)
                    mm_tok = p.op("tensor", lambda e, f=f, outr=outr, pst=pst: e.matmul(
                        pst[:], U2SB[:, f * 256 + outr * 128: f * 256 + outr * 128 + 128],
                        HB[:, f * 512:(f + 1) * 512],
                        start=(f == 0), stop=(f == 31), skip_group_check=True),
                        waits=w, inc=(s.pe if f == 31 else None))
                ev = p.op("scalar", lambda e, outr=outr, tsl=tsl, pst=pst: e.copy(
                    QB[:, outr * T:(outr + 1) * T][:, tsl], pst[:]),
                    waits=[mm_tok], inc=s.act)
                s.psfree[id(s.ps_c)] = ev
                qb_toks.append(ev)
                for f in range(32):
                    s.psfree[("hb", f)] = mm_tok
        qb_last = qb_toks[-1]

        yb_toks = {}
        for outc in range(NCH):
            evs = []
            for tc in range(NTC):
                tsl = slice(tc * 512, tc * 512 + 512)
                pst = ps_a if tc % 2 == 0 else ps_b
                mm_tok = None
                for r2 in range(2):
                    w = []
                    if r2 == 0:
                        w = [qb_last]
                        ft = s.psfree.get(id(pst))
                        if ft is not None:
                            w.append(ft)
                    mm_tok = p.op("tensor", lambda e, r2=r2, outc=outc, tsl=tsl, pst=pst: e.matmul(
                        pst[:], V2SB[:, r2 * C + outc * 128: r2 * C + outc * 128 + 128],
                        QB[:, r2 * T:(r2 + 1) * T][:, tsl],
                        start=(r2 == 0), stop=(r2 == 1), skip_group_check=True),
                        waits=w, inc=(s.pe if r2 == 1 else None))
                ev = p.op("scalar", lambda e, outc=outc, tsl=tsl, pst=pst: e.activation(
                    SCA[:, outc * T:(outc + 1) * T][:, tsl], pst[:], AF.Identity,
                    bias=B2SB[:, outc:outc + 1]),
                    waits=[mm_tok], inc=s.act)
                s.psfree[id(pst)] = ev
                evs.append(ev)
            yb_toks[outc] = evs

        st_toks = emit_lnout_add_store(p, s, SCA, yb_toks, XC[:, 0:T],
                                       xt, xout, bsl, XS, TMPF)
        store_last = st_toks[-1]

    return store_last


def build_stage_d():
    nc = bass.Bass("TRN2", target_bir_lowering=False)
    xt = nc.dram_tensor("xt", [NCH, 128, B_CORE * T], F32, kind="ExternalInput").ap()
    u1 = nc.dram_tensor("u1", [128, NCH * 256], F32, kind="ExternalInput").ap()
    v1 = nc.dram_tensor("v1", [128, 2 * 4096], F32, kind="ExternalInput").ap()
    b1in = nc.dram_tensor("b1in", [128, 32], F32, kind="ExternalInput").ap()
    u2 = nc.dram_tensor("u2", [128, 32 * 256], F32, kind="ExternalInput").ap()
    v2 = nc.dram_tensor("v2", [128, 2 * C], F32, kind="ExternalInput").ap()
    b2in = nc.dram_tensor("b2in", [128, 8], F32, kind="ExternalInput").ap()
    xout = nc.dram_tensor("xout", [NCH, 128, B_CORE * T], F32, kind="ExternalOutput").ap()


    with contextlib.ExitStack() as ctx:
        p = Prog(nc, ctx)
        p.dma_sem = {"sync": p.sem("s_dsync"), "gpsimd": p.sem("s_dgp")}
        p.eng_sems = (p.sem("s_pe"), p.sem("s_act"), p.sem("s_dve"))
        with contextlib.ExitStack() as c1:
            store_last = _emit_stage_d(nc, c1, p, xt, u1, v1, b1in, u2, v2, b2in, xout)
        p.op("sync", lambda e: e.nop(), waits=[store_last])
        with nc.Block() as block:
            p.emit(block)
    return nc

def run_stage_d(x_full, U1, V1, b1, U2, V2, b2):
    nc = build_stage_d()
    consts = _consts_common()
    b1h = np.ascontiguousarray(np.asarray(b1, dtype=np.float32).reshape(32, 128).T)
    b2h = np.ascontiguousarray(np.asarray(b2, dtype=np.float32).reshape(8, 128).T)
    in_maps = []
    for core in range(N_CORES):
        xc = np.asarray(x_full[core * B_CORE:(core + 1) * B_CORE], dtype=np.float32)
        im = {"xt": _to_xt(xc), "u1": _chunked(U1), "v1": _chunked(V1),
              "b1in": b1h, "u2": _chunked(U2), "v2": _chunked(V2), "b2in": b2h}
        in_maps.append(im)
    res = run_bass_kernel_spmd(nc, in_maps, core_ids=list(range(N_CORES)))
    outs = [_from_xt(r["xout"]) for r in res.results]
    return np.concatenate(outs, axis=0)




def _emit_stage_c(nc, ctx, p, xt, waq, wak, wav, wproj, lvin, cos1, sin1, perm_in, masks_in, xout):
    SM = 1.0 / math.sqrt(32)
    LAMBDA_INIT = 0.2
    sb = lambda name, shape, dt: ctx.enter_context(nc.sbuf_tensor('c_' + name, shape, dt))
    psumt = lambda name, shape, dt=F32: ctx.enter_context(nc.psum_tensor('c_' + name, shape, dt))

    SCA = sb("SCA", [128, NCH * T], BF16)
    SCB = sb("SCB", [128, NCH * T], BF16)
    V65 = sb("V65", [128, 16 * 1040], BF16)
    QR2a = sb("QR2a", [128, T], BF16)
    QR2b = sb("QR2b", [128, T], BF16)
    KR2a = sb("KR2a", [128, T], BF16)
    KR2b = sb("KR2b", [128, T], BF16)
    QH = sb("QH", [33, T], BF16)
    KH = sb("KH", [33, T], BF16)
    PT = sb("PT", [128, 2 * 512], BF16)
    OT2 = sb("OT2", [64, 512], BF16)
    A1N = sb("A1N", [64, T], BF16)
    A2T = sb("A2T", [64, 512], BF16)
    DIFU = sb("DIFU", [64, T], BF16)
    SQD = sb("SQD", [64, T], BF16)
    OTN = A2T
    XC = sb("XC", [128, 2 * T], BF16)
    XS = sb("XS", [128, T], F32)
    WVH = sb("WVH", [128, NCH * 512], BF16)
    WSM = sb("WSM", [128, 1 * C], BF16)
    COS = sb("COS", [128, T], BF16)
    SIN = sb("SIN", [128, T], BF16)
    PERM = sb("PERM", [128, 128], BF16)
    ONES128 = sb("ONES128", [128, 1], BF16)
    ONES32 = sb("ONES32", [32, 1], BF16)
    ONES32F = sb("ONES32F", [32, 1], F32)
    ONES64B = sb("ONES64B", [64, 1], BF16)
    ONESR1 = sb("ONESR1", [1, 128], F32)
    NEGR1 = sb("NEGR1", [1, 128], F32)
    ONESR64 = sb("ONESR64", [1, 64], F32)
    NEG08R64 = sb("NEG08R64", [1, 64], F32)

    LV = sb("LV", [128, 4], F32)
    LAMC = sb("LAMC", [64, 1], F32)
    MASKS = sb("MASKS", [128, 896], BF16)
    STAT = sb("STAT", [1, STAT_W], F32)
    EPS = sb("EPS", [1, 8], F32)
    RSTDB = sb("RSTDB", [128, T], F32)
    NMRB = sb("NMRB", [128, T], BF16)
    TMPF = sb("TMPF", [128, 512], F32)
    TMPG = XS

    ps_a = psumt("ps_a", [128, 512])
    ps_b = psumt("ps_b", [128, 512])
    ps_c = psumt("ps_c", [128, 512])
    ps_d = psumt("ps_d", [128, 512])
    ps_o = psumt("ps_o", [65, 512])
    ps_r = psumt("ps_r", [64, 512])
    ps_s1 = psumt("ps_s1", [1, 512])
    ps_s2 = psumt("ps_s2", [1, 512])

    s = NS()
    s.pe, s.act, s.dve = p.eng_sems

    s.ones128, s.onesr1, s.negr1 = ONES128, ONESR1, NEGR1
    s.stat = STAT
    s.rstdB, s.nmrB = RSTDB, NMRB
    s.ps_s1, s.ps_s2, s.ps_c, s.ps_d = ps_s1, ps_s2, ps_c, ps_d
    s.psfree = {}
    s.sqch_free = None
    s.stat_users = []
    s.tmpf_free = s.tmpg_free = None
    s.eps = EPS

    const_toks = []
    for dst, srcap in [(COS, cos1), (SIN, sin1), (PERM, perm_in),
                       (MASKS, masks_in), (LV, lvin)]:
        const_toks.append(p.dma("sync", dst[:], srcap))
    for tns, val in [(ONES128, 1.0), (ONES32, 1.0), (ONESR1, 1.0),
                     (NEGR1, -1.0), (ONESR64, 1.0), (NEG08R64, -0.8),
                     (ONES64B, 1.0), (ONES32F, 1.0)]:
        const_toks.append(p.op("vector", lambda e, tns=tns, val=val: e.memset(tns[:], val),
                               inc=s.dve))
    s.cgate = tokmax(const_toks)
    s.eps_tok = p.op("vector", lambda e: e.memset(EPS[:], 1e-5), inc=s.dve)
    khm1 = p.op("vector", lambda e: e.memset(KH[32:33, :], -1.0), inc=s.dve)
    ones_v = p.op("vector", lambda e: e.memset(
        V65.ap().rearrange("p (tt h o) -> p tt h o", tt=16, h=16)[:, :, :, 64:65],
        1.0), inc=s.dve)

    lmul = p.op("vector", lambda e: e.tensor_mul(LV[:, 0:1], LV[:, 0:1], LV[:, 1:2]),
                waits=s.cgate, inc=s.dve)
    lmul2 = p.op("vector", lambda e: e.tensor_mul(LV[:, 2:3], LV[:, 2:3], LV[:, 3:4]),
                 waits=s.cgate, inc=s.dve)
    d1 = p.op("tensor", lambda e: e.matmul(ps_s1[:, 0:1], ONES32F[:], LV[0:32, 0:1],
                                           start=True, stop=True, skip_group_check=True),
              waits=[lmul] + s.cgate, inc=s.pe)
    d1c = p.op("scalar", lambda e: e.activation(STAT[0:1, 0:1], ps_s1[:, 0:1], AF.Exp),
               waits=[d1], inc=s.act)
    d2 = p.op("tensor", lambda e: e.matmul(ps_s2[:, 0:1], ONES32F[:], LV[0:32, 2:3],
                                           start=True, stop=True, skip_group_check=True),
              waits=[lmul2] + s.cgate, inc=s.pe)
    d2c = p.op("scalar", lambda e: e.activation(STAT[0:1, 1:2], ps_s2[:, 0:1], AF.Exp),
               waits=[d2], inc=s.act)
    lsub = p.op("vector", lambda e: e.tensor_sub(STAT[0:1, 0:1], STAT[0:1, 0:1], STAT[0:1, 1:2]),
                waits=[d1c, d2c], inc=s.dve)
    ladd = p.op("vector", lambda e: e.tensor_scalar_add(STAT[0:1, 0:1], STAT[0:1, 0:1],
                                                        LAMBDA_INIT),
                waits=[lsub], inc=s.dve)
    lbc = p.op("tensor", lambda e: e.matmul(ps_r[:, 0:1], ONESR64[:], STAT[0:1, 0:1],
                                            start=True, stop=True, skip_group_check=True),
              waits=[ladd], inc=s.pe)
    lamc = p.op("scalar", lambda e: e.copy(LAMC[:], ps_r[:, 0:1]), waits=[lbc], inc=s.act)
    s.psfree[id(s.ps_s1)] = d1c
    s.psfree[id(s.ps_s2)] = d2c
    s.ps_r_free = lamc
    s.stat_users += [lbc, ladd]

    BPAD = 1.02
    store_last = None
    for b in range(B_CORE):
        bsl = slice(b * T, (b + 1) * T)
        bgate = [store_last] if store_last else []

        xln_toks = emit_lnin_streamed(p, s, xt, bsl, SCA, XC, KR2a[:], TMPF, bgate)
        xln_last = xln_toks[-1]

        for half in range(2):
            wv_tok = p.dma("gpsimd", WVH[:], wav[half],
                           waits=bgate + ([s.psfree.get("wvh")] if s.psfree.get("wvh") else []))
            for tt in range(16):
                pst = ps_a if tt % 2 == 0 else ps_b
                mm_tok = None
                for cch in range(NCH):
                    w = []
                    if cch == 0:
                        w = [wv_tok, xln_last, ones_v]
                        ft = s.psfree.get(id(pst))
                        if ft is not None:
                            w.append(ft)
                    mm_tok = p.op("tensor", lambda e, cch=cch, tt=tt, half=half, pst=pst: e.matmul(
                        pst[:], SCA[:, cch * T + tt * 128: cch * T + tt * 128 + 128],
                        WVH[:, cch * 512:(cch + 1) * 512],
                        start=(cch == 0), stop=(cch == NCH - 1), skip_group_check=True),
                        waits=w, inc=(s.pe if cch == NCH - 1 else None))

                def evac_v(e, tt=tt, half=half, pst=pst):
                    src_ap = pst.ap().rearrange("p (h d) -> p h d", h=8)
                    dst = V65.ap().rearrange("p (t2 h o) -> p t2 h o", t2=16, h=16)[
                        :, tt, half * 8:(half + 1) * 8, 0:64]
                    return e.activation(dst, src_ap, AF.Copy)
                ev = p.op("scalar", evac_v, waits=[mm_tok], inc=s.act)
                s.psfree[id(pst)] = ev
                vlast = ev
            s.psfree["wvh"] = vlast

        o_toks = []
        prerope = XC[:, T:2 * T]
        for sc in range(4):
            pr_done = {}
            for (buf, which, cc) in ((QR2a, "q", sc), (KR2a, "k", sc),
                                     (QR2b, "q", sc + 4), (KR2b, "k", sc + 4)):
                wdram = waq if which == "q" else wak
                slot = 0
                wtok = p.dma("gpsimd", WSM[:, slot * C:(slot + 1) * C], wdram[cc],
                             waits=[s.psfree.get(("wsm", slot))] if s.psfree.get(("wsm", slot)) else [])
                banks = [ps_a, ps_b, ps_c, ps_d]
                mm_last = {}
                for tc in range(NTC):
                    for cch in range(NCH):
                        pst = banks[tc]
                        w = []
                        if cch == 0:
                            w = [wtok, xln_last]
                            ft = s.psfree.get(id(pst))
                            if ft is not None:
                                w.append(ft)
                        mm_last[tc] = p.op("tensor", lambda e, cch=cch, tc=tc, pst=pst, slot=slot: e.matmul(
                            pst[:], WSM[:, slot * C + cch * 128: slot * C + cch * 128 + 128],
                            SCA[:, cch * T + tc * 512: cch * T + tc * 512 + 512],
                            start=(cch == 0), stop=(cch == NCH - 1), skip_group_check=True),
                            waits=w, inc=(s.pe if cch == NCH - 1 else None))
                rope_toks = []
                for tc in range(NTC):
                    tsl = slice(tc * 512, tc * 512 + 512)
                    pst = banks[tc]
                    ev = p.op("scalar", lambda e, tsl=tsl, pst=pst: e.copy(prerope[:, tsl], pst[:]),
                              waits=[mm_last[tc]] + ([s.prerope_free] if getattr(s, "prerope_free", None) else []),
                              inc=s.act)
                    s.psfree[id(pst)] = ev
                    rot = p.op("tensor", lambda e, tsl=tsl, pst=pst: e.matmul(
                        pst[:], PERM[:], prerope[:, tsl], start=True, stop=True,
                        skip_group_check=True),
                        waits=[ev] + s.cgate, inc=s.pe)
                    t1 = p.op("vector", lambda e, tsl=tsl: e.tensor_mul(
                        TMPF[:], prerope[:, tsl], COS[:, tsl]),
                        waits=[ev] + s.cgate + ([s.tmpf_free] if s.tmpf_free else []), inc=s.dve)
                    t2 = p.op("vector", lambda e, tsl=tsl, pst=pst: e.tensor_mul(
                        TMPG[:, 0:512], pst[:], SIN[:, tsl]),
                        waits=[rot] + ([s.tmpg_free] if s.tmpg_free else []),
                        inc=s.dve)
                    s.psfree[id(pst)] = t2
                    t3 = p.op("vector", lambda e, tsl=tsl, buf=buf: e.tensor_add(
                        buf[:, tsl], TMPF[:], TMPG[:, 0:512]), waits=[t1, t2], inc=s.dve)
                    s.tmpf_free = t3
                    s.tmpg_free = t3
                    rope_toks.append(t3)
                s.psfree[("wsm", slot)] = rope_toks[-1]
                s.prerope_free = rope_toks[-1]
                pr_done[id(buf)] = rope_toks

            for m in range(4):
                pair_res = {}
                for (pi, qbuf, kbuf) in ((0, QR2a, KR2a), (1, QR2b, KR2b)):
                    j = 4 * sc + m + 16 * pi
                    def mkmove(dstH, srcbuf, m=m, stg_extra=None):
                        base = 32 * m
                        prev_o = tokmax(o_toks)
                        if m < 3:
                            return p.op("scalar", lambda e, dstH=dstH, srcbuf=srcbuf, base=base: e.copy(
                                dstH[0:32, :], srcbuf[base:base + 32, :]),
                                waits=pr_done[id(srcbuf)] + prev_o + [khm1], inc=s.act)
                        hop = p.op("scalar", lambda e, srcbuf=srcbuf: e.copy(
                            SQD[0:64, :], srcbuf[64:128, :]),
                            waits=pr_done[id(srcbuf)] + prev_o +
                            ([s.sqd_free] if getattr(s, "sqd_free", None) else []),
                            inc=s.act)
                        mv = p.op("scalar", lambda e, dstH=dstH: e.copy(
                            dstH[0:32, :], SQD[32:64, :]), waits=[hop, khm1], inc=s.act)
                        s.sqd_free = mv
                        return mv
                    qmove = mkmove(QH, qbuf)
                    kmove = mkmove(KH, kbuf)
                    sqh = SQD[0:32, :]
                    ksq = p.op("scalar", lambda e, sqh=sqh: e.activation(
                        sqh, KH[0:32, :], AF.Square),
                        waits=[kmove, qmove] + ([s.sqd_free] if getattr(s, "sqd_free", None) else []) +
                        ([getattr(s, "prev_qn", None)] if getattr(s, "prev_qn", None) else []),
                        inc=s.act)
                    kn_toks = []
                    for tc in range(NTC):
                        mm = p.op("tensor", lambda e, tc=tc, sqh=sqh: e.matmul(
                            ps_s1[:], ONES32[:], sqh[:, tc * 512:(tc + 1) * 512],
                            start=True, stop=True, skip_group_check=True),
                            waits=[ksq] + s.cgate + ([s.psfree[id(s.ps_s1)]] if s.psfree.get(id(s.ps_s1)) else []),
                            inc=s.pe)
                        kn = p.op("scalar", lambda e, tc=tc: e.activation(
                            STAT[0:1, SOFF_SUM + tc * 512:SOFF_SUM + tc * 512 + 512],
                            ps_s1[:], AF.Sqrt), waits=[mm], inc=s.act)
                        s.psfree[id(s.ps_s1)] = kn
                        kn_toks.append(kn)
                    maxk = p.op("vector", lambda e: e.reduce_max(
                        STAT[0:1, SOFF_MAXK:SOFF_MAXK + 1],
                        STAT[0:1, SOFF_SUM:SOFF_SUM + T], axis=AX.X),
                        waits=kn_toks, inc=s.dve)
                    qsq = p.op("scalar", lambda e, sqh=sqh: e.activation(
                        sqh, QH[0:32, :], AF.Square),
                        waits=kn_toks + [ksq, qmove], inc=s.act)
                    qn_toks = []
                    for tc in range(NTC):
                        mm = p.op("tensor", lambda e, tc=tc, sqh=sqh: e.matmul(
                            ps_s1[:], ONES32[:], sqh[:, tc * 512:(tc + 1) * 512],
                            start=True, stop=True, skip_group_check=True),
                            waits=[qsq, maxk] + s.cgate + ([s.psfree[id(s.ps_s1)]] if s.psfree.get(id(s.ps_s1)) else []),
                            inc=s.pe)
                        qn = p.op("scalar", lambda e, tc=tc: e.activation(
                            STAT[0:1, SOFF_SUM + tc * 512:SOFF_SUM + tc * 512 + 512],
                            ps_s1[:], AF.Sqrt), waits=[mm], inc=s.act)
                        s.psfree[id(s.ps_s1)] = qn
                        qn_toks.append(qn)
                    s.prev_qn = qn_toks[-1]
                    s.sqd_free = qn_toks[-1]
                    bq = p.op("vector", lambda e: e.tensor_scalar(
                        STAT[0:1, SOFF_SQ:SOFF_SQ + T],
                        STAT[0:1, SOFF_SUM:SOFF_SUM + T],
                        STAT[0:1, SOFF_MAXK:SOFF_MAXK + 1], BPAD,
                        ALU.mult, ALU.mult),
                        waits=qn_toks + [maxk], inc=s.dve)
                    brow = p.op("scalar", lambda e: e.copy(
                        QH[32:33, :], STAT[0:1, SOFF_SQ:SOFF_SQ + T]),
                        waits=[bq, qmove], inc=s.act)
                    s.stat_users += [bq, brow, maxk]

                    pair_out = []
                    for qch in range(4):
                        nkt = 4 * qch + 4
                        pv_tok = None
                        for kt in range(nkt):
                            pst = ps_a if kt % 2 == 0 else ps_b
                            ptslot = kt % 2
                            stw = [brow, kmove]
                            ft = s.psfree.get(id(pst))
                            if ft is not None:
                                stw.append(ft)
                            st = p.op("tensor", lambda e, kt=kt, qch=qch, pst=pst: e.matmul(
                                pst[:], KH[:, kt * 128:(kt + 1) * 128],
                                QH[:, qch * 512:(qch + 1) * 512],
                                start=True, stop=True, skip_group_check=True),
                                waits=stw, inc=s.pe)
                            expw = [st]
                            ptfree = s.psfree.get(("pt", ptslot))
                            if ptfree is not None:
                                expw.append(ptfree)
                            ex = p.op("scalar", lambda e, ptslot=ptslot, pst=pst: e.activation(
                                PT[:, ptslot * 512:(ptslot + 1) * 512], pst[:], AF.Exp,
                                scale=SM),
                                waits=expw, inc=s.act)
                            s.psfree[id(pst)] = ex
                            if kt >= 4 * qch:
                                jj = kt - 4 * qch
                                ex = p.op("vector", lambda e, ptslot=ptslot, jj=jj: e.tensor_mul(
                                    PT[:, ptslot * 512:(ptslot + 1) * 512],
                                    PT[:, ptslot * 512:(ptslot + 1) * 512],
                                    MASKS[:, (3 - jj) * 128:(3 - jj) * 128 + 512]),
                                    waits=[ex] + s.cgate, inc=s.dve)
                            pvw = [ex]
                            if kt == 0:
                                ft = s.psfree.get(id(ps_o))
                                if ft is not None:
                                    pvw.append(ft)
                                pvw.append(vlast)
                                pvw.append(ones_v)
                            vh = j % 16
                            pv_tok = p.op("tensor", lambda e, kt=kt, vh=vh, ptslot=ptslot: e.matmul(
                                ps_o[:], V65[:, kt * 1040 + vh * 65: kt * 1040 + vh * 65 + 65],
                                PT[:, ptslot * 512:(ptslot + 1) * 512],
                                start=(kt == 0), stop=(kt == nkt - 1), skip_group_check=True),
                                waits=pvw, inc=(s.pe if kt == nkt - 1 else None))
                            s.psfree[("pt", ptslot)] = pv_tok
                        rec = p.op("vector", lambda e: e.reciprocal(
                            STAT[0:1, SOFF_REC:SOFF_REC + 512], ps_o[64:65, :]),
                            waits=[pv_tok] + ([s.rec_free] if getattr(s, "rec_free", None) else []),
                            inc=s.dve)
                        otmp = p.op("scalar", lambda e: e.copy(OT2[:], ps_o[0:64, :]),
                                    waits=[pv_tok] + ([s.ot2_free] if getattr(s, "ot2_free", None) else []),
                                    inc=s.act)
                        s.psfree[id(ps_o)] = otmp
                        rb = p.op("tensor", lambda e: e.matmul(
                            ps_r[:], ONESR64[:], STAT[0:1, SOFF_REC:SOFF_REC + 512],
                            start=True, stop=True, skip_group_check=True),
                            waits=[rec] + s.cgate + ([s.ps_r_free] if getattr(s, "ps_r_free", None) else []),
                            inc=s.pe)
                        s.rec_free = rb
                        s.mu2_free = rb
                        if pi == 0:
                            odn = p.op("vector", lambda e, qch=qch: e.tensor_mul(
                                A1N[:, qch * 512:(qch + 1) * 512], OT2[:], ps_r[:]),
                                waits=[rb, otmp] + ([s.psfree.get("a1n")] if qch == 0 and s.psfree.get("a1n") else []),
                                inc=s.dve)
                        else:
                            odn = p.op("vector", lambda e: e.tensor_mul(
                                A2T[:], OT2[:], ps_r[:]),
                                waits=[rb, otmp] + ([s.psfree.get("a2t")] if s.psfree.get("a2t") else []),
                                inc=s.dve)
                            odn = p.op("vector", lambda e, qch=qch: e.scalar_tensor_tensor(
                                DIFU[:, qch * 512:(qch + 1) * 512], A2T[:], LAMC[:],
                                A1N[:, qch * 512:(qch + 1) * 512],
                                ALU.mult, ALU.subtract),
                                waits=[odn, lamc] + ([s.psfree.get("difu")] if qch == 0 and s.psfree.get("difu") else []),
                                inc=s.dve)
                            s.psfree["a2t"] = odn
                        s.ps_r_free = odn
                        s.ot2_free = odn
                        s.stat_users += [rb]
                        pair_out.append(odn)
                    pair_res[pi] = pair_out
                s.psfree["a1n"] = pair_res[1][-1]

                h = 4 * sc + m
                dq = p.op("scalar", lambda e: e.activation(SQD[:], DIFU[:], AF.Square),
                          waits=pair_res[1] + ([s.sqd_free] if getattr(s, "sqd_free", None) else []),
                          inc=s.act)
                ms_toks = []
                for tc in range(NTC):
                    mm = p.op("tensor", lambda e, tc=tc: e.matmul(
                        ps_s2[:], ONES64B[:], SQD[:, tc * 512:(tc + 1) * 512],
                        start=True, stop=True, skip_group_check=True),
                        waits=[dq] + s.cgate + ([s.psfree[id(s.ps_s2)]] if s.psfree.get(id(s.ps_s2)) else []),
                        inc=s.pe)
                    ln1 = p.op("scalar", lambda e, tc=tc: e.activation(
                        STAT[0:1, SOFF_SUM + tc * 512:SOFF_SUM + tc * 512 + 512],
                        ps_s2[:], AF.Ln, scale=1.0 / 64, bias=s.eps[0:1, 0:1]),
                        waits=[mm, s.eps_tok], inc=s.act)
                    s.psfree[id(s.ps_s2)] = ln1
                    rs1 = p.op("scalar", lambda e, tc=tc: e.activation(
                        STAT[0:1, SOFF_SUM + tc * 512:SOFF_SUM + tc * 512 + 512],
                        STAT[0:1, SOFF_SUM + tc * 512:SOFF_SUM + tc * 512 + 512],
                        AF.Exp, scale=-0.5), waits=[ln1], inc=s.act)
                    rbb = p.op("tensor", lambda e, tc=tc: e.matmul(
                        ps_r[:], NEG08R64[:],
                        STAT[0:1, SOFF_SUM + tc * 512:SOFF_SUM + tc * 512 + 512],
                        start=True, stop=True, skip_group_check=True),
                        waits=[rs1] + s.cgate + ([s.ps_r_free] if s.ps_r_free else []),
                        inc=s.pe)
                    dn = p.op("vector", lambda e, tc=tc: e.tensor_mul(
                        OTN[:], DIFU[:, tc * 512:(tc + 1) * 512], ps_r[:]),
                        waits=[rbb] + ([s.psfree.get("a2t")] if s.psfree.get("a2t") else []),
                        inc=s.dve)
                    s.ps_r_free = dn
                    oc = p.op("scalar", lambda e, h=h, tc=tc: e.copy(
                        SCB[:, (h // 2) * T + tc * 512:(h // 2) * T + tc * 512 + 512][
                            64 * (h % 2):64 * (h % 2) + 64, :], OTN[:]),
                        waits=[dn], inc=s.act)
                    s.psfree["a2t"] = oc
                    s.stat_users += [rbb]
                    ms_toks.append(oc)
                    o_toks.append(oc)
                s.sqd_free = ms_toks[-1]
                s.psfree["difu"] = ms_toks[-1]
        o_last = tokmax(o_toks)

        yb_toks = {}
        for outc in range(NCH):
            slot = 0
            wtok = p.dma("gpsimd", WSM[:, slot * C:(slot + 1) * C], wproj[outc],
                         waits=[s.psfree.get(("wsm", slot))] if s.psfree.get(("wsm", slot)) else [])
            banks = [ps_a, ps_b, ps_c, ps_d]
            mm_last = {}
            for tc in range(NTC):
                for cch in range(NCH):
                    pst = banks[tc]
                    w = []
                    if cch == 0:
                        w = [wtok] + o_last
                        ft = s.psfree.get(id(pst))
                        if ft is not None:
                            w.append(ft)
                    mm_last[tc] = p.op("tensor", lambda e, cch=cch, tc=tc, pst=pst, slot=slot: e.matmul(
                        pst[:], WSM[:, slot * C + cch * 128: slot * C + cch * 128 + 128],
                        SCB[:, cch * T + tc * 512: cch * T + tc * 512 + 512],
                        start=(cch == 0), stop=(cch == NCH - 1), skip_group_check=True),
                        waits=w, inc=(s.pe if cch == NCH - 1 else None))
            evs = []
            for tc in range(NTC):
                tsl = slice(tc * 512, tc * 512 + 512)
                pst = banks[tc]
                ev = p.op("scalar", lambda e, outc=outc, tsl=tsl, pst=pst: e.copy(
                    SCA[:, outc * T:(outc + 1) * T][:, tsl], pst[:]),
                    waits=[mm_last[tc]], inc=s.act)
                s.psfree[id(pst)] = ev
                evs.append(ev)
            s.psfree[("wsm", slot)] = evs[-1]
            yb_toks[outc] = evs

        s.xs_gate = s.tmpg_free
        st_toks = emit_lnout_add_store(p, s, SCA, yb_toks, XC[:, 0:T],
                                       xt, xout, bsl, XS, TMPF)
        store_last = st_toks[-1]

    return store_last


def build_stage_c():
    nc = bass.Bass("TRN2", target_bir_lowering=False)
    xt = nc.dram_tensor("xt", [NCH, 128, B_CORE * T], F32, kind="ExternalInput").ap()
    waq = nc.dram_tensor("waq", [8, 128, NCH * 128], F32, kind="ExternalInput").ap()
    wak = nc.dram_tensor("wak", [8, 128, NCH * 128], F32, kind="ExternalInput").ap()
    wav = nc.dram_tensor("wav", [2, 128, NCH * 512], F32, kind="ExternalInput").ap()
    wproj = nc.dram_tensor("wproj", [8, 128, NCH * 128], F32, kind="ExternalInput").ap()
    lvin = nc.dram_tensor("lvin", [128, 4], F32, kind="ExternalInput").ap()
    cos1 = nc.dram_tensor("cos1", [128, T], BF16, kind="ExternalInput").ap()
    sin1 = nc.dram_tensor("sin1", [128, T], BF16, kind="ExternalInput").ap()
    perm_in = nc.dram_tensor("perm32", [128, 128], BF16, kind="ExternalInput").ap()
    masks_in = nc.dram_tensor("masks", [128, 896], BF16, kind="ExternalInput").ap()
    xout = nc.dram_tensor("xout", [NCH, 128, B_CORE * T], F32, kind="ExternalOutput").ap()

    SM = 1.0 / math.sqrt(32)
    LAMBDA_INIT = 0.2

    with contextlib.ExitStack() as ctx:
        p = Prog(nc, ctx)
        p.dma_sem = {"sync": p.sem("s_dsync"), "gpsimd": p.sem("s_dgp")}
        p.eng_sems = (p.sem("s_pe"), p.sem("s_act"), p.sem("s_dve"))
        with contextlib.ExitStack() as c1:
            store_last = _emit_stage_c(nc, c1, p, xt, waq, wak, wav, wproj, lvin, cos1, sin1, perm_in, masks_in, xout)
        p.op("sync", lambda e: e.nop(), waits=[store_last])
        with nc.Block() as block:
            p.emit(block)
    return nc

def _rope_tables_l1():
    theta = 10000.0 ** (-(np.arange(16, dtype=np.float64) * 2.0 / 32))
    t = np.arange(T, dtype=np.float64)
    ang = np.outer(t, theta)
    blk_c = np.concatenate([np.cos(ang), np.cos(ang)], 1).T
    blk_s = np.concatenate([np.sin(ang), np.sin(ang)], 1).T
    cos = np.concatenate([blk_c] * 4, 0).astype(np.float32)
    sin = np.concatenate([blk_s] * 4, 0).astype(np.float32)
    return cos.astype(BF), sin.astype(BF)


def _deinterleave_wa(W):
    W = np.asarray(W, dtype=np.float32)
    idx = np.concatenate([np.arange(0, 32, 2), np.arange(1, 32, 2)])
    out = W.copy()
    for part in range(2):
        base = part * C
        for j in range(32):
            cols = base + 32 * j + np.arange(32)
            out[:, cols] = W[:, base + 32 * j + idx]
    return np.ascontiguousarray(out)


def run_stage_c(x_full, W_attn, W_proj, lq1, lk1, lq2, lk2):
    nc = build_stage_c()
    consts = _consts_common()
    cos, sin = _rope_tables_l1()
    wa = _deinterleave_wa(W_attn)
    waq = _slot8(wa[:, 0:C])
    wak = _slot8(wa[:, C:2 * C])
    wav = np.ascontiguousarray(np.stack(
        [_chunked(wa[:, 2 * C + h * 512:2 * C + (h + 1) * 512]) for h in range(2)]))
    wp = _slot8(np.asarray(W_proj, dtype=np.float32))
    lv = np.zeros((128, 4), dtype=np.float32)
    lv[0:32, 0] = np.asarray(lq1); lv[0:32, 1] = np.asarray(lk1)
    lv[0:32, 2] = np.asarray(lq2); lv[0:32, 3] = np.asarray(lk2)
    perm = _perm_rotate_half(32)
    in_maps = []
    for core in range(N_CORES):
        xc = np.asarray(x_full[core * B_CORE:(core + 1) * B_CORE], dtype=np.float32)
        im = {"xt": _to_xt(xc), "waq": waq, "wak": wak, "wav": wav,
              "wproj": wp, "lvin": lv,
              "cos1": cos, "sin1": sin, "perm32": perm, "masks": consts["masks"]}
        in_maps.append(im)
    res = run_bass_kernel_spmd(nc, in_maps, core_ids=list(range(N_CORES)))
    outs = [_from_xt(r["xout"]) for r in res.results]
    return np.concatenate(outs, axis=0)




TG = T + 1


def _emit_stage_b(nc, ctx, p, xt, wqk, wob, w1p, w2p, b1p, b2p, qsp, ksp, lngp, lnbp, cosS_in, sinS_in, cosI_in, sinI_in, perm_in, idn_in, maska_in, mask0_in, xout, n_sub=6):
    sb = lambda name, shape, dt: ctx.enter_context(nc.sbuf_tensor('b_' + name, shape, dt))

    XP = sb("XP", [128, NCH * TG], BF16)
    SCB = sb("SCB", [128, NCH * T], BF16)
    VT2 = sb("VT2", [128, 16 * 128], BF16)
    VSTG = sb("VSTG", [128, 2 * 512], BF16)
    QR2 = sb("QR2", [128, T], BF16)
    KR2 = sb("KR2", [128, T], BF16)
    XC = sb("XC", [128, 2 * TG], BF16)
    PRAW = None
    SQP = sb("SQP", [128, T], BF16)
    RNB = sb("RNB", [128, T], BF16)
    PQ = sb("PQ", [128, 2 * 384], BF16)
    PTS = sb("PTS", [128, 2 * 384], BF16)
    HB = sb("HB", [128, 22 * 512], BF16)
    W12S = sb("W12S", [128, 1 * NCH * 256], BF16)
    W2S = sb("W2S", [128, 1 * 22 * 128], BF16)
    WQKS = sb("WQKS", [128, 1 * NCH * 384], BF16)
    WOS = sb("WOS", [128, 1 * NCH * 128], BF16)
    COSS = sb("COSS", [128, T], BF16)
    SINS = sb("SINS", [128, T], BF16)
    COSI = sb("COSI", [128, T], BF16)
    SINI = sb("SINI", [128, T], BF16)
    PERM = sb("PERM", [128, 128], BF16)
    IDN = sb("IDN", [128, 128], BF16)
    MASKA = sb("MASKA", [128, 384], BF16)
    MASK0 = sb("MASK0", [128, 2 * 256], BF16)
    B1SB = sb("B1SB", [128, n_sub * 44], F32)
    B2SB = sb("B2SB", [128, n_sub * 8], F32)
    QSC = sb("QSC", [128, n_sub], F32)
    KSC = sb("KSC", [128, n_sub], F32)
    LNG = sb("LNG", [128, n_sub * 8], F32)
    LNB = sb("LNB", [128, n_sub * 8], F32)
    ONES128 = sb("ONES128", [128, 1], BF16)
    ONES64 = sb("ONES64", [64, 1], BF16)
    ONESR1 = sb("ONESR1", [1, 128], F32)
    NEGR1 = sb("NEGR1", [1, 128], F32)
    ONESR64 = sb("ONESR64", [1, 64], F32)
    STAT = sb("STAT", [1, STAT_W], F32)
    EPS = sb("EPS", [1, 8], F32)
    RSTDB = sb("RSTDB", [128, T], F32)
    NMRB = sb("NMRB", [128, T], BF16)
    TMPF = sb("TMPF", [128, 512], F32)
    TMPG = sb("TMPG", [128, 512], F32)
    VB = sb("VB", [128, 512], F32)
    GB = sb("GB", [128, 512], F32)
    NM8 = sb("NM8", [128, 4], F32)
    SUMC = sb("SUMC", [128, 4], F32)
    XS = sb("XS", [128, T], F32)
    PRAW = XC[:, 0:T]
    SQX = XC[:, TG:2 * TG]

    psumt = lambda name, shape, dt=F32: ctx.enter_context(nc.psum_tensor('b_' + name, shape, dt))
    ps_a = psumt("ps_a", [128, 512])
    ps_b = psumt("ps_b", [128, 512])
    ps_c = psumt("ps_c", [128, 512])
    ps_d = ps_a
    ps_t = psumt("ps_t", [128, 512], BF16)
    ps_t2 = psumt("ps_t2", [128, 512], BF16)
    ps_o2 = psumt("ps_o2", [64, 128])
    ps_o2b = psumt("ps_o2b", [64, 128])
    ps_s1 = psumt("ps_s1", [1, 512])
    ps_s2 = ps_s1

    s = NS()
    s.pe, s.act, s.dve = p.eng_sems
    s.ones128, s.onesr1, s.negr1 = ONES128, ONESR1, NEGR1
    s.stat = STAT
    s.rstdB, s.nmrB = RSTDB, NMRB
    s.ps_s1, s.ps_s2, s.ps_c, s.ps_d = ps_s1, ps_s2, ps_c, ps_d
    s.psfree = {}
    s.sqch_free = None
    s.stat_users = []
    s.nm8_free = [None, None]
    s.vstg_free = [None, None]
    s.pq_free = [None, None]
    s.pts_free = [None, None]
    s.tmpf_free = s.tmpg_free = None
    s.eps = EPS

    const_toks = []
    for dst, srcap in [(COSS, cosS_in), (SINS, sinS_in), (COSI, cosI_in),
                       (SINI, sinI_in), (PERM, perm_in), (IDN, idn_in),
                       (MASKA, maska_in), (MASK0, mask0_in),
                       (B1SB, b1p), (B2SB, b2p), (QSC, qsp), (KSC, ksp),
                       (LNG, lngp), (LNB, lnbp)]:
        const_toks.append(p.dma("sync", dst[:], srcap))
    for tns, val in [(ONES128, 1.0), (ONES64, 1.0), (ONESR1, 1.0),
                     (NEGR1, -1.0), (ONESR64, 1.0)]:
        const_toks.append(p.op("vector", lambda e, tns=tns, val=val: e.memset(tns[:], val),
                               inc=s.dve))
    guard0 = p.op("vector", lambda e: e.memset(
        XP.ap().rearrange("p (c g) -> p c g", c=NCH)[:, :, 0:1], 0.0), inc=s.dve)
    const_toks.append(guard0)
    s.cgate = tokmax(const_toks)
    s.eps_tok = p.op("vector", lambda e: e.memset(EPS[:], 1e-5), inc=s.dve)

    def xpu(c, st, w):
        return XP[:, c * TG + 1 + st: c * TG + 1 + st + w]

    def xps(c, st, w):
        off = 1 if c < 4 else 0
        return XP[:, c * TG + off + st: c * TG + off + st + w]

    xp_w = {}
    xp_r = {c: [] for c in range(NCH)}

    def rd_gate(c):
        return [xp_w[c]] if xp_w.get(c) else []

    def wr_gate(c):
        g = rd_gate(c) + xp_r[c]
        return tokmax(g)

    store_last = None
    for b in range(B_CORE):
        bsl = slice(b * T, (b + 1) * T)
        bgate = [store_last] if store_last else []

        xln_toks = emit_lnin_streamed(p, s, xt, bsl, None, XC, SQP[:], TMPF,
                                      bgate, dst_of=lambda c: xpu(c, 0, T))
        for c in range(NCH):
            xp_w[c] = xln_toks[c]
            xp_r[c] = []

        for i in range(n_sub):
            o_toks = []
            for hp in range(8):
                slot = 0
                wtok = p.dma("gpsimd",
                             WQKS[:, slot * NCH * 384:(slot + 1) * NCH * 384],
                             wqk[i, hp],
                             waits=[s.psfree.get(("wqk", slot))] if s.psfree.get(("wqk", slot)) else [])
                banks = [ps_a, ps_b, ps_c, ps_a]
                mm_last = {}
                for tc in range(NTC):
                    for cch in range(NCH):
                        pst = banks[tc]
                        w = []
                        if cch == 0:
                            w = [wtok]
                            ft = s.psfree.get(id(pst))
                            if ft is not None:
                                w.append(ft)
                        w = w + rd_gate(cch)
                        mm_last[tc] = p.op("tensor", lambda e, cch=cch, tc=tc, pst=pst, slot=slot: e.matmul(
                            pst[:], WQKS[:, slot * NCH * 384 + cch * 384 + 256:
                                         slot * NCH * 384 + cch * 384 + 384],
                            xps(cch, tc * 512, 512),
                            start=(cch == 0), stop=(cch == NCH - 1), skip_group_check=True),
                            waits=w, inc=(s.pe if cch == NCH - 1 else None))
                for cch in range(NCH):
                    xp_r[cch].append(mm_last[NTC - 1])
                for tc in range(NTC):
                    pst = banks[tc]
                    pstt = ps_t if tc % 2 == 0 else ps_t2
                    vse = p.op("scalar", lambda e, pst=pst, tc=tc: e.copy(
                        VSTG[:, (tc % 2) * 512:(tc % 2) * 512 + 512], pst[:]),
                        waits=[mm_last[tc]] + ([s.vstg_free[tc % 2]] if s.vstg_free[tc % 2] else []),
                        inc=s.act)
                    s.psfree[id(pst)] = vse
                    trs = []
                    for kt in range(4):
                        trw = [vse]
                        if kt == 0:
                            ftt = s.psfree.get(id(pstt))
                            if ftt is not None:
                                trw.append(ftt)
                        tr = p.op("tensor", lambda e, kt=kt, pstt=pstt, tc=tc: e.transpose(
                            pstt[:, kt * 128:(kt + 1) * 128],
                            VSTG[:, (tc % 2) * 512 + kt * 128:(tc % 2) * 512 + (kt + 1) * 128],
                            IDN[:]),
                            waits=trw + s.cgate, inc=s.pe)
                        trs.append(tr)
                    vte = p.op("scalar", lambda e, tc=tc, pstt=pstt: e.copy(
                        VT2[:, tc * 512:(tc + 1) * 512], pstt[:]),
                        waits=trs + ([s.psfree.get("vt2")] if tc == 0 and s.psfree.get("vt2") else []),
                        inc=s.act)
                    s.psfree[id(pstt)] = vte
                    s.vstg_free[tc % 2] = vte
                    vlast = vte
                pr_toks = {}
                for which in ("q", "k"):
                    woff = 0 if which == "q" else 128
                    praw_toks = []
                    banks = [ps_a, ps_b, ps_c, ps_a]
                    mm_last = {}
                    for tc in range(NTC):
                        for cch in range(NCH):
                            pst = banks[tc]
                            w = []
                            if cch == 0:
                                w = [wtok]
                                ft = s.psfree.get(id(pst))
                                if ft is not None:
                                    w.append(ft)
                            w = w + rd_gate(cch)
                            mm_last[tc] = p.op("tensor", lambda e, cch=cch, tc=tc, pst=pst, slot=slot, woff=woff: e.matmul(
                                pst[:], WQKS[:, slot * NCH * 384 + cch * 384 + woff:
                                             slot * NCH * 384 + cch * 384 + woff + 128],
                                xps(cch, tc * 512, 512),
                                start=(cch == 0), stop=(cch == NCH - 1), skip_group_check=True),
                                waits=w, inc=(s.pe if cch == NCH - 1 else None))
                    for cch in range(NCH):
                        xp_r[cch].append(mm_last[NTC - 1])
                    for tc in range(NTC):
                        tsl = slice(tc * 512, tc * 512 + 512)
                        pst = banks[tc]
                        ev = p.op("scalar", lambda e, tsl=tsl, pst=pst: e.copy(PRAW[:, tsl], pst[:]),
                                  waits=[mm_last[tc]] + ([s.praw_free] if getattr(s, "praw_free", None) else []),
                                  inc=s.act)
                        s.psfree[id(pst)] = ev
                        praw_toks.append(ev)
                    stat_gate = tokmax(s.stat_users)
                    s.stat_users = []
                    sqt = p.op("scalar", lambda e: e.activation(SQP[:], PRAW, AF.Square),
                               waits=praw_toks + ([s.sqp_free] if getattr(s, "sqp_free", None) else []),
                               inc=s.act)
                    ln_toks = []
                    for half in range(2):
                        soff = SOFF_SUM if half == 0 else SOFF_SQ
                        for tc in range(NTC):
                            mm = p.op("tensor", lambda e, half=half, tc=tc: e.matmul(
                                ps_s1[:], ONES128[64 * half:64 * half + 64, 0:1],
                                SQP[64 * half:64 * half + 64, tc * 512:(tc + 1) * 512],
                                start=True, stop=True, skip_group_check=True),
                                waits=[sqt] + s.cgate + ([s.psfree[id(s.ps_s1)]] if s.psfree.get(id(s.ps_s1)) else []),
                                inc=s.pe)
                            l1 = p.op("scalar", lambda e, soff=soff, tc=tc: e.activation(
                                STAT[0:1, soff + tc * 512: soff + tc * 512 + 512],
                                ps_s1[:], AF.Ln),
                                waits=[mm, stat_gate[0] if stat_gate else None]
                                + (stat_gate[1:] if len(stat_gate) > 1 else []),
                                inc=s.act)
                            s.psfree[id(s.ps_s1)] = l1
                            ln_toks.append(l1)
                    e1 = p.op("scalar", lambda e: e.activation(
                        STAT[0:1, 0:2 * T], STAT[0:1, 0:2 * T], AF.Exp, scale=-0.5),
                        waits=ln_toks, inc=s.act)
                    e2 = p.op("vector", lambda e: e.tensor_scalar_min(
                        STAT[0:1, 0:2 * T], STAT[0:1, 0:2 * T], 1e12),
                        waits=[e1], inc=s.dve)
                    s.stat_users += [e2]
                    rn_toks = []
                    for half in range(2):
                        soff = SOFF_SUM if half == 0 else SOFF_SQ
                        for tc in range(NTC):
                            bmm = p.op("tensor", lambda e, soff=soff, tc=tc: e.matmul(
                                ps_c[:, 0:512][0:64, :], ONESR64[:],
                                STAT[0:1, soff + tc * 512: soff + tc * 512 + 512],
                                start=True, stop=True, skip_group_check=True),
                                waits=[e2] + s.cgate + ([s.psfree[id(s.ps_c)]] if s.psfree.get(id(s.ps_c)) else []),
                                inc=s.pe)
                            cp = p.op("scalar", lambda e, half=half, tc=tc: e.copy(
                                RNB[64 * half:64 * half + 64, tc * 512:(tc + 1) * 512],
                                ps_c[0:64, :]),
                                waits=[bmm], inc=s.act)
                            s.psfree[id(s.ps_c)] = cp
                            s.psfree[id(ps_c)] = cp
                            rn_toks.append(cp)
                    s.stat_users += rn_toks
                    sccol = QSC if which == "q" else KSC
                    ap1 = p.op("vector", lambda e, sccol=sccol, i=i: e.scalar_tensor_tensor(
                        SQP[:], PRAW, sccol[:, i:i + 1], RNB[:],
                        ALU.mult, ALU.mult),
                        waits=rn_toks + [sqt] + s.cgate, inc=s.dve)
                    s.praw_free = ap1
                    tcos = COSS if which == "q" else COSI
                    tsin = SINS if which == "q" else SINI
                    dstb = QR2 if which == "q" else KR2
                    rope_toks = []
                    for tc in range(NTC):
                        tsl = slice(tc * 512, tc * 512 + 512)
                        pst = banks[tc]
                        rotw = [ap1] + s.cgate
                        ft = s.psfree.get(id(pst))
                        if ft is not None:
                            rotw.append(ft)
                        rot = p.op("tensor", lambda e, tsl=tsl, pst=pst: e.matmul(
                            pst[:], PERM[:], SQP[:, tsl], start=True, stop=True,
                            skip_group_check=True),
                            waits=rotw, inc=s.pe)
                        t1 = p.op("vector", lambda e, tsl=tsl, tcos=tcos: e.tensor_mul(
                            TMPF[:], SQP[:, tsl], tcos[:, tsl]),
                            waits=[ap1] + s.cgate + ([s.tmpf_free] if s.tmpf_free else []),
                            inc=s.dve)
                        t2 = p.op("vector", lambda e, tsl=tsl, tsin=tsin, pst=pst: e.tensor_mul(
                            TMPG[:], pst[:], tsin[:, tsl]),
                            waits=[rot] + ([s.tmpg_free] if s.tmpg_free else []),
                            inc=s.dve)
                        s.psfree[id(pst)] = t2
                        t3 = p.op("vector", lambda e, tsl=tsl, dstb=dstb: e.tensor_add(
                            dstb[:, tsl], TMPF[:], TMPG[:]),
                            waits=[t1, t2] + ([s.qr_free[id(dstb)]] if getattr(s, "qr_free", {}).get(id(dstb)) else []),
                            inc=s.dve)
                        s.tmpf_free = t3
                        s.tmpg_free = t3
                        rope_toks.append(t3)
                    s.sqp_free = rope_toks[-1]
                    pr_toks[which] = rope_toks
                s.psfree[("wqk", slot)] = pr_toks["k"][-1]

                if not hasattr(s, "qr_free"):
                    s.qr_free = {}
                head_toks = []
                for half in range(2):
                    h = hp * 2 + half
                    for w_i in range(8):
                        for j in range(2):
                            if w_i == 0:
                                kstart, kw = 0, 256
                            else:
                                kstart = (w_i - 1) * 256 + j * 128
                                kw = 384
                            nkt = kw // 128
                            qstart = w_i * 256 + j * 128
                            par = (w_i * 2 + j) % 2
                            pst = ps_a if par == 0 else ps_b
                            pq = PQ[:, par * 384: par * 384 + kw]
                            pts = PTS[:, par * 384: par * 384 + kw]
                            pstt = ps_t if par == 0 else ps_t2
                            psto = ps_o2 if par == 0 else ps_o2b
                            nm_r = NM8[:, 2 * par: 2 * par + 1]
                            nm_8 = NM8[:, 2 * par + 1: 2 * par + 2]
                            sm_r = SUMC[:, 2 * par: 2 * par + 1]
                            sm_i = SUMC[:, 2 * par + 1: 2 * par + 2]
                            sw = pr_toks["q"] + pr_toks["k"]
                            ft = s.psfree.get(id(pst))
                            if ft is not None:
                                sw.append(ft)
                            smm = p.op("tensor", lambda e, half=half, qstart=qstart, kstart=kstart, kw=kw, pst=pst: e.matmul(
                                pst[:, 0:kw], QR2[64 * half:64 * half + 64, qstart:qstart + 128],
                                KR2[64 * half:64 * half + 64, kstart:kstart + kw],
                                start=True, stop=True, skip_group_check=True),
                                waits=sw, inc=s.pe)
                            mask_ap = (MASK0[:, j * 256:(j + 1) * 256] if w_i == 0
                                       else MASKA[:, 0:384])
                            madd = p.op("vector", lambda e, pst=pst, kw=kw, mask_ap=mask_ap: e.tensor_add(
                                pst[:, 0:kw], pst[:, 0:kw], mask_ap),
                                waits=[smm] + s.cgate, inc=s.dve)
                            nmx = p.op("vector", lambda e, pst=pst, kw=kw, nm_r=nm_r: e.reduce_max(
                                nm_r, pst[:, 0:kw], axis=AX.X, negate=True),
                                waits=[madd] + ([s.nm8_free[par]] if s.nm8_free[par] else []),
                                inc=s.dve)
                            nm8 = p.op("vector", lambda e, nm_r=nm_r, nm_8=nm_8: e.tensor_scalar_mul(
                                nm_8, nm_r, 8.0),
                                waits=[nmx], inc=s.dve)
                            exq = p.op("scalar", lambda e, pst=pst, kw=kw, pq=pq, nm_8=nm_8, sm_r=sm_r: e.activation(
                                pq, pst[:, 0:kw], AF.Exp, scale=8.0,
                                bias=nm_8, accum_out=sm_r),
                                waits=[nm8] + ([s.pq_free[par]] if s.pq_free[par] else []),
                                inc=s.act)
                            s.psfree[id(pst)] = exq
                            rcp = p.op("vector", lambda e, sm_r=sm_r, sm_i=sm_i: e.reciprocal(
                                sm_i, sm_r),
                                waits=[exq], inc=s.dve)
                            s.nm8_free[par] = rcp
                            pnorm = p.op("vector", lambda e, kw=kw, pq=pq, sm_i=sm_i: e.tensor_scalar(
                                pq, pq, sm_i, None, ALU.mult),
                                waits=[rcp], inc=s.dve)
                            tr_toks = []
                            for kt in range(nkt):
                                trw = [pnorm]
                                ftt = s.psfree.get(id(pstt))
                                if ftt is not None and kt == 0:
                                    trw.append(ftt)
                                tr = p.op("tensor", lambda e, kt=kt, pstt=pstt, pq=pq: e.transpose(
                                    pstt[:, kt * 128:(kt + 1) * 128],
                                    pq[:, kt * 128:(kt + 1) * 128], IDN[:]),
                                    waits=trw + s.cgate, inc=s.pe)
                                tr_toks.append(tr)
                            ptc = p.op("scalar", lambda e, kw=kw, pts=pts, pstt=pstt: e.copy(
                                pts, pstt[:, 0:kw]),
                                waits=tr_toks + ([s.pts_free[par]] if s.pts_free[par] else []),
                                inc=s.act)
                            s.psfree[id(pstt)] = ptc
                            s.pq_free[par] = tr_toks[-1]
                            pv_tok = None
                            for kt in range(nkt):
                                ktg = (kstart + kt * 128) // 128
                                w = [ptc]
                                if kt == 0:
                                    ftv = s.psfree.get(id(psto))
                                    if ftv is not None:
                                        w.append(ftv)
                                    w.append(vlast)
                                pv_tok = p.op("tensor", lambda e, kt=kt, ktg=ktg, half=half, psto=psto, pts=pts: e.matmul(
                                    psto[:], VT2[:, ktg * 128 + 64 * half: ktg * 128 + 64 * half + 64],
                                    pts[:, kt * 128:(kt + 1) * 128],
                                    start=(kt == 0), stop=(kt == nkt - 1),
                                    skip_group_check=True),
                                    waits=w, inc=(s.pe if kt == nkt - 1 else None))
                            s.pts_free[par] = pv_tok
                            oc = p.op("scalar", lambda e, h=h, qstart=qstart, psto=psto: e.copy(
                                SCB[:, (h // 2) * T + qstart:(h // 2) * T + qstart + 128][
                                    64 * (h % 2):64 * (h % 2) + 64, :], psto[:]),
                                waits=[pv_tok], inc=s.act)
                            s.psfree[id(psto)] = oc
                            o_toks.append(oc)
                            head_toks.append(oc)
                s.qr_free[id(QR2)] = head_toks[-1]
                s.qr_free[id(KR2)] = head_toks[-1]
                s.psfree["vt2"] = head_toks[-1]
            o_last = tokmax(o_toks)

            for outc in range(NCH):
                slot = 0
                wtok = p.dma("gpsimd",
                             WOS[:, slot * NCH * 128:(slot + 1) * NCH * 128],
                             wob[i, outc],
                             waits=[s.psfree.get(("wos", slot))] if s.psfree.get(("wos", slot)) else [])
                last_add = None
                banks = [ps_a, ps_b, ps_c, ps_a]
                mm_last = {}
                for tc in range(NTC):
                    for cch in range(NCH):
                        pst = banks[tc]
                        w = []
                        if cch == 0:
                            w = [wtok] + o_last
                            ft = s.psfree.get(id(pst))
                            if ft is not None:
                                w.append(ft)
                        mm_last[tc] = p.op("tensor", lambda e, cch=cch, tc=tc, pst=pst, slot=slot: e.matmul(
                            pst[:], WOS[:, slot * NCH * 128 + cch * 128: slot * NCH * 128 + cch * 128 + 128],
                            SCB[:, cch * T + tc * 512: cch * T + tc * 512 + 512],
                            start=(cch == 0), stop=(cch == NCH - 1), skip_group_check=True),
                            waits=w, inc=(s.pe if cch == NCH - 1 else None))
                for tc in range(NTC):
                    pst = banks[tc]
                    radd = p.op("vector", lambda e, outc=outc, tc=tc, pst=pst: e.tensor_add(
                        xpu(outc, tc * 512, 512), xpu(outc, tc * 512, 512), pst[:]),
                        waits=[mm_last[tc]] + wr_gate(outc), inc=s.dve)
                    s.psfree[id(pst)] = radd
                    last_add = radd
                xp_w[outc] = last_add
                xp_r[outc] = []
                s.psfree[("wos", slot)] = last_add

            stat_gate = tokmax(s.stat_users)
            s.stat_users = []
            z1 = p.op("vector", lambda e: e.memset(STAT[0:1, 0:2 * T], 0.0),
                      waits=stat_gate, inc=s.dve)
            acc_toks = [z1]
            for cch in range(NCH):
                sqg = p.op("scalar", lambda e, cch=cch: e.activation(
                    SQX, XP[:, cch * TG:(cch + 1) * TG], AF.Square),
                    waits=rd_gate(cch) + ([s.sqx_free] if getattr(s, "sqx_free", None) else []),
                    inc=s.act)
                xp_r[cch].append(sqg)
                mm1 = mm2 = None
                for tc in range(NTC):
                    tsl = slice(tc * 512, tc * 512 + 512)
                    off = 1 if cch < 4 else 0
                    mm1 = p.op("tensor", lambda e, cch=cch, tc=tc: e.matmul(
                        ps_s1[:], ONES128[:], xps(cch, tc * 512, 512),
                        start=True, stop=True, skip_group_check=True),
                        waits=rd_gate(cch) + s.cgate + ([s.psfree[id(s.ps_s1)]] if s.psfree.get(id(s.ps_s1)) else []),
                        inc=s.pe)
                    a1 = p.op("vector", lambda e, tsl=tsl: e.tensor_add(
                        STAT[0:1, SOFF_SUM:SOFF_SUM + T][:, tsl],
                        STAT[0:1, SOFF_SUM:SOFF_SUM + T][:, tsl], ps_s1[:]),
                        waits=[mm1, z1], inc=s.dve)
                    s.psfree[id(s.ps_s1)] = a1
                    mm2 = p.op("tensor", lambda e, cch=cch, tc=tc, off=off: e.matmul(
                        ps_s2[:], ONES128[:],
                        SQX[:, off + tc * 512: off + tc * 512 + 512],
                        start=True, stop=True, skip_group_check=True),
                        waits=[sqg] + ([s.psfree[id(s.ps_s2)]] if s.psfree.get(id(s.ps_s2)) else []),
                        inc=s.pe)
                    a2 = p.op("vector", lambda e, tsl=tsl: e.tensor_add(
                        STAT[0:1, SOFF_SQ:SOFF_SQ + T][:, tsl],
                        STAT[0:1, SOFF_SQ:SOFF_SQ + T][:, tsl], ps_s2[:]),
                        waits=[mm2, z1], inc=s.dve)
                    s.psfree[id(s.ps_s2)] = a2
                    acc_toks += [a1, a2]
                xp_r[cch].append(mm1)
                s.sqx_free = acc_toks[-1]
            bc_toks = []
            for tc in range(NTC):
                mu = STAT[0:1, SOFF_SUM + tc * 512:SOFF_SUM + tc * 512 + 512]
                rs = STAT[0:1, SOFF_SQ + tc * 512:SOFF_SQ + tc * 512 + 512]
                mu2 = STAT[0:1, SOFF_MU2:SOFF_MU2 + 512]
                t_mu = p.op("vector", lambda e, mu=mu: e.tensor_scalar_mul(mu, mu, 1.0 / C),
                            waits=acc_toks, inc=s.dve)
                t_mu2 = p.op("scalar", lambda e, mu=mu, mu2=mu2: e.activation(mu2, mu, AF.Square),
                             waits=[t_mu] + ([s.mu2_free] if getattr(s, "mu2_free", None) else []),
                             inc=s.act)
                t_var = p.op("vector", lambda e, rs=rs, mu2=mu2: e.scalar_tensor_tensor(
                    rs, rs, 1.0 / C, mu2, ALU.mult, ALU.subtract),
                    waits=[t_mu2] + acc_toks, inc=s.dve)
                s.mu2_free = t_var
                t_ln = p.op("scalar", lambda e, rs=rs: e.activation(rs, rs, AF.Ln, bias=s.eps[0:1, 0:1]),
                            waits=[t_var, s.eps_tok], inc=s.act)
                t_rstd = p.op("scalar", lambda e, rs=rs: e.activation(rs, rs, AF.Exp, scale=-0.5),
                              waits=[t_ln], inc=s.act)
                t_mr = p.op("vector", lambda e, mu=mu, rs=rs: e.tensor_mul(mu, mu, rs),
                            waits=[t_rstd], inc=s.dve)
                b1 = p.op("tensor", lambda e, rs=rs: e.matmul(
                    ps_c[:], ONESR1[:], rs, start=True, stop=True, skip_group_check=True),
                    waits=[t_rstd] + s.cgate + ([s.psfree[id(s.ps_c)]] if s.psfree.get(id(s.ps_c)) else []), inc=s.pe)
                b2 = p.op("tensor", lambda e, mu=mu: e.matmul(
                    ps_d[:], NEGR1[:], mu, start=True, stop=True, skip_group_check=True),
                    waits=[t_mr] + s.cgate + ([s.psfree[id(s.ps_d)]] if s.psfree.get(id(s.ps_d)) else []), inc=s.pe)
                c1 = p.op("scalar", lambda e, tc=tc: e.copy(
                    RSTDB[:, tc * 512:tc * 512 + 512], ps_c[:]), waits=[b1], inc=s.act)
                c2 = p.op("scalar", lambda e, tc=tc: e.copy(
                    NMRB[:, tc * 512:tc * 512 + 512], ps_d[:]), waits=[b2], inc=s.act)
                s.psfree[id(s.ps_c)] = c1
                s.psfree[id(s.ps_d)] = c2
                bc_toks += [c1, c2]
            s.stat_users += bc_toks
            xsn_toks = {}
            for cch in range(NCH):
                evs = []
                for tc in range(NTC):
                    tsl = slice(tc * 512, tc * 512 + 512)
                    t1 = p.op("vector", lambda e, cch=cch, tc=tc, tsl=tsl: e.tensor_mul(
                        TMPF[:], xps(cch, tc * 512, 512), RSTDB[:, tsl]),
                        waits=bc_toks + rd_gate(cch) + ([s.tmpf_free] if s.tmpf_free else []),
                        inc=s.dve)
                    t2 = p.op("vector", lambda e, tsl=tsl: e.tensor_add(
                        TMPF[:], TMPF[:], NMRB[:, tsl]), waits=[t1], inc=s.dve)
                    s.tmpf_free = t2
                    t3 = p.op("scalar", lambda e, cch=cch, tsl=tsl, i=i: e.activation(
                        SCB[:, cch * T:(cch + 1) * T][:, tsl], TMPF[:], AF.Identity,
                        bias=LNB[:, i * 8 + cch:i * 8 + cch + 1],
                        scale=LNG[:, i * 8 + cch:i * 8 + cch + 1]),
                        waits=[t2] + s.cgate, inc=s.act)
                    s.tmpf_free = t3
                    evs.append(t3)
                    xp_r[cch].append(t1)
                xsn_toks[cch] = evs
            xsn_last = tokmax([t for l in xsn_toks.values() for t in l])

            for tc in range(NTC):
                tsl = slice(tc * 512, tc * 512 + 512)
                hb_toks = []
                for ep in range(22):
                    slot = 0
                    wtok = p.dma("gpsimd",
                                 W12S[:, slot * NCH * 256:(slot + 1) * NCH * 256],
                                 w1p[i, ep],
                                 waits=[s.psfree.get(("w12", slot))] if s.psfree.get(("w12", slot)) else [])
                    mmA = mmG = None
                    for cch in range(NCH):
                        w = []
                        if cch == 0:
                            w = [wtok] + xsn_last
                            ft = s.psfree.get(id(ps_a))
                            if ft is not None:
                                w.append(ft)
                        mmA = p.op("tensor", lambda e, cch=cch, tsl=tsl, slot=slot: e.matmul(
                            ps_a[:], W12S[:, slot * NCH * 256 + cch * 256: slot * NCH * 256 + cch * 256 + 128],
                            SCB[:, cch * T:(cch + 1) * T][:, tsl],
                            start=(cch == 0), stop=(cch == NCH - 1), skip_group_check=True),
                            waits=w, inc=(s.pe if cch == NCH - 1 else None))
                    for cch in range(NCH):
                        w = []
                        if cch == 0:
                            ft = s.psfree.get(id(ps_b))
                            if ft is not None:
                                w.append(ft)
                        mmG = p.op("tensor", lambda e, cch=cch, tsl=tsl, slot=slot: e.matmul(
                            ps_b[:], W12S[:, slot * NCH * 256 + cch * 256 + 128: slot * NCH * 256 + cch * 256 + 256],
                            SCB[:, cch * T:(cch + 1) * T][:, tsl],
                            start=(cch == 0), stop=(cch == NCH - 1), skip_group_check=True),
                            waits=w, inc=(s.pe if cch == NCH - 1 else None))
                    av = p.op("scalar", lambda e, ep=ep, i=i: e.activation(
                        VB[:], ps_a[:], AF.Identity, bias=B1SB[:, i * 44 + ep:i * 44 + ep + 1]),
                        waits=[mmA] + ([s.vb_free] if getattr(s, "vb_free", None) else []),
                        inc=s.act)
                    s.psfree[id(ps_a)] = av
                    gg = p.op("scalar", lambda e, ep=ep, i=i: e.activation(
                        GB[:], ps_b[:], AF.Gelu, bias=B1SB[:, i * 44 + 22 + ep:i * 44 + 22 + ep + 1]),
                        waits=[mmG] + ([s.gb_free] if getattr(s, "gb_free", None) else []),
                        inc=s.act)
                    s.psfree[id(ps_b)] = gg
                    hcomb = p.op("vector", lambda e, ep=ep: e.tensor_mul(
                        HB[:, ep * 512:(ep + 1) * 512], VB[:], GB[:]),
                        waits=[av, gg] + ([s.psfree.get(("hb", ep))] if s.psfree.get(("hb", ep)) else []),
                        inc=s.dve)
                    s.vb_free = hcomb
                    s.gb_free = hcomb
                    hb_toks.append(hcomb)
                    s.psfree[("w12", slot)] = hcomb
                for outc in range(NCH):
                    slot = 0
                    wtok = p.dma("gpsimd",
                                 W2S[:, slot * 22 * 128:(slot + 1) * 22 * 128],
                                 w2p[i, outc],
                                 waits=[s.psfree.get(("w2", slot))] if s.psfree.get(("w2", slot)) else [])
                    mmY = None
                    for ep in range(22):
                        w = [hb_toks[ep]]
                        if ep == 0:
                            w.append(wtok)
                            ft = s.psfree.get(id(ps_c))
                            if ft is not None:
                                w.append(ft)
                        mmY = p.op("tensor", lambda e, ep=ep, slot=slot: e.matmul(
                            ps_c[:], W2S[:, slot * 22 * 128 + ep * 128: slot * 22 * 128 + ep * 128 + 128],
                            HB[:, ep * 512:(ep + 1) * 512],
                            start=(ep == 0), stop=(ep == 21), skip_group_check=True),
                            waits=w, inc=(s.pe if ep == 21 else None))
                    yb2 = p.op("scalar", lambda e, outc=outc, i=i: e.activation(
                        TMPG[:], ps_c[:], AF.Identity, bias=B2SB[:, i * 8 + outc:i * 8 + outc + 1]),
                        waits=[mmY] + ([s.tmpg_free] if s.tmpg_free else []), inc=s.act)
                    s.psfree[id(ps_c)] = yb2
                    radd = p.op("vector", lambda e, outc=outc, tc=tc: e.tensor_add(
                        xpu(outc, tc * 512, 512), xpu(outc, tc * 512, 512), TMPG[:]),
                        waits=[yb2] + wr_gate(outc), inc=s.dve)
                    s.tmpg_free = radd
                    xp_w[outc] = radd
                    if tc == NTC - 1:
                        xp_r[outc] = []
                    s.psfree[("w2", slot)] = radd
                for ep in range(22):
                    s.psfree[("hb", ep)] = mmY

        ybt = {c: [xp_w[c]] for c in range(NCH)}
        s.xs_gate = None
        st_toks = emit_lnout_add_store(p, s, None, ybt, XC[:, 0:T],
                                       xt, xout, bsl, XS, TMPF,
                                       yb_of=lambda c: xpu(c, 0, T))
        store_last = st_toks[-1]
        for c in range(NCH):
            xp_r[c].append(st_toks[c])

    return store_last


def build_stage_b(n_sub=6):
    nc = bass.Bass("TRN2", target_bir_lowering=False)
    xt = nc.dram_tensor("xt", [NCH, 128, B_CORE * T], F32, kind="ExternalInput").ap()
    wqk = nc.dram_tensor("wqk", [n_sub, 8, 128, NCH * 384], F32, kind="ExternalInput").ap()
    wob = nc.dram_tensor("wob", [n_sub, 8, 128, NCH * 128], F32, kind="ExternalInput").ap()
    w1p = nc.dram_tensor("w1p", [n_sub, 22, 128, NCH * 256], F32, kind="ExternalInput").ap()
    w2p = nc.dram_tensor("w2p", [n_sub, 8, 128, 22 * 128], F32, kind="ExternalInput").ap()
    b1p = nc.dram_tensor("b1p", [128, n_sub * 44], F32, kind="ExternalInput").ap()
    b2p = nc.dram_tensor("b2p", [128, n_sub * 8], F32, kind="ExternalInput").ap()
    qsp = nc.dram_tensor("qsp", [128, n_sub], F32, kind="ExternalInput").ap()
    ksp = nc.dram_tensor("ksp", [128, n_sub], F32, kind="ExternalInput").ap()
    lngp = nc.dram_tensor("lngp", [128, n_sub * 8], F32, kind="ExternalInput").ap()
    lnbp = nc.dram_tensor("lnbp", [128, n_sub * 8], F32, kind="ExternalInput").ap()
    cosS_in = nc.dram_tensor("cosS", [128, T], BF16, kind="ExternalInput").ap()
    sinS_in = nc.dram_tensor("sinS", [128, T], BF16, kind="ExternalInput").ap()
    cosI_in = nc.dram_tensor("cosI", [128, T], BF16, kind="ExternalInput").ap()
    sinI_in = nc.dram_tensor("sinI", [128, T], BF16, kind="ExternalInput").ap()
    perm_in = nc.dram_tensor("perm", [128, 128], BF16, kind="ExternalInput").ap()
    idn_in = nc.dram_tensor("idn", [128, 128], BF16, kind="ExternalInput").ap()
    maska_in = nc.dram_tensor("maska", [128, 384], BF16, kind="ExternalInput").ap()
    mask0_in = nc.dram_tensor("mask0", [128, 2 * 256], BF16, kind="ExternalInput").ap()
    xout = nc.dram_tensor("xout", [NCH, 128, B_CORE * T], F32, kind="ExternalOutput").ap()

    with contextlib.ExitStack() as ctx:
        p = Prog(nc, ctx)
        p.dma_sem = {"sync": p.sem("s_dsync"), "gpsimd": p.sem("s_dgp")}
        p.eng_sems = (p.sem("s_pe"), p.sem("s_act"), p.sem("s_dve"))
        with contextlib.ExitStack() as c1:
            store_last = _emit_stage_b(nc, c1, p, xt, wqk, wob, w1p, w2p, b1p, b2p, qsp, ksp, lngp, lnbp, cosS_in, sinS_in, cosI_in, sinI_in, perm_in, idn_in, maska_in, mask0_in, xout, n_sub)
        p.op("sync", lambda e: e.nop(), waits=[store_last])
        with nc.Block() as block:
            p.emit(block)
    return nc

def _xpos_tables():
    half = np.arange(0, 64, 2, dtype=np.float64)
    inv_freq = 1.0 / 10000.0 ** (half / 64)
    t = np.arange(T, dtype=np.float64)
    freqs = np.concatenate([np.outer(t, inv_freq), np.outer(t, inv_freq)], 1)
    base_scale = (half + 0.4 * 64) / (1.4 * 64)
    power = (t - T // 2) / (256 // 2)
    scale = base_scale[None, :] ** power[:, None]
    scale = np.concatenate([scale, scale], 1)
    cosS = (np.cos(freqs) * scale).T
    sinS = (np.sin(freqs) * scale).T
    cosI = (np.cos(freqs) / scale).T
    sinI = (np.sin(freqs) / scale).T
    out = []
    for a in (cosS, sinS, cosI, sinI):
        out.append(np.concatenate([a, a], 0).astype(np.float32).astype(BF))
    return out


def _local_masks():
    ql = np.arange(128)[:, None]
    kl = np.arange(384)[None, :]
    maska = np.where((kl - ql >= 0) & (kl - ql <= 256), 0.0, -1e30).astype(np.float32)
    k0 = np.arange(256)[None, :]
    m0 = [np.where(k0 <= j * 128 + ql, 0.0, -1e30).astype(np.float32) for j in range(2)]
    return maska.astype(BF), np.concatenate(m0, 1).astype(BF)


def _prep_b_weights(inputs, n_sub=6):
    Wqkv = np.asarray(inputs["L0_ffn_Wqkv"], dtype=np.float32)
    Wo = np.asarray(inputs["L0_ffn_Wo"], dtype=np.float32)
    W1 = np.asarray(inputs["L0_ffn_W1"], dtype=np.float32)
    W2 = np.asarray(inputs["L0_ffn_W2"], dtype=np.float32)
    b1 = np.asarray(inputs["L0_ffn_b1"], dtype=np.float32)
    b2 = np.asarray(inputs["L0_ffn_b2"], dtype=np.float32)
    qs = np.asarray(inputs["L0_ffn_q_scale"], dtype=np.float32)
    ks = np.asarray(inputs["L0_ffn_k_scale"], dtype=np.float32)
    g = np.asarray(inputs["L0_ffn_ln_g"], dtype=np.float32)
    bb = np.asarray(inputs["L0_ffn_ln_b"], dtype=np.float32)
    DI, DP = 2730, 2816
    wqk = np.zeros((n_sub, 8, 128, NCH * 384), dtype=np.float32)
    for i in range(n_sub):
        for hp in range(8):
            pair = np.concatenate([
                Wqkv[i][:, hp * 128:(hp + 1) * 128],
                Wqkv[i][:, C + hp * 128: C + (hp + 1) * 128],
                Wqkv[i][:, 2 * C + hp * 128: 2 * C + (hp + 1) * 128]], axis=1)
            wqk[i, hp] = _chunked(pair)
    w1pad = np.zeros((n_sub, C, 2 * DP), dtype=np.float32)
    w1pad[:, :, 0:DI] = W1[:n_sub, :, 0:DI]
    w1pad[:, :, DP:DP + DI] = W1[:n_sub, :, DI:2 * DI]
    w1p = np.zeros((n_sub, 22, 128, NCH * 256), dtype=np.float32)
    for i in range(n_sub):
        for ep in range(22):
            pair = np.concatenate([
                w1pad[i][:, ep * 128:(ep + 1) * 128],
                w1pad[i][:, DP + ep * 128: DP + (ep + 1) * 128]], axis=1)
            w1p[i, ep] = _chunked(pair)
    w2full = np.zeros((n_sub, DP, C), dtype=np.float32)
    w2full[:, 0:DI, :] = W2[:n_sub]
    w2p = np.zeros((n_sub, 8, 128, 22 * 128), dtype=np.float32)
    for i in range(n_sub):
        for outc in range(8):
            w2p[i, outc] = _chunked(w2full[i][:, outc * 128:(outc + 1) * 128])
    b1pad = np.zeros((n_sub, 2 * DP), dtype=np.float32)
    b1pad[:, 0:DI] = b1[:n_sub, 0:DI]
    b1pad[:, DP:DP + DI] = b1[:n_sub, DI:2 * DI]
    b1h = np.zeros((128, n_sub * 44), dtype=np.float32)
    b2h = np.zeros((128, n_sub * 8), dtype=np.float32)
    qsh = np.zeros((128, n_sub), dtype=np.float32)
    ksh = np.zeros((128, n_sub), dtype=np.float32)
    gh = np.zeros((128, n_sub * 8), dtype=np.float32)
    bh = np.zeros((128, n_sub * 8), dtype=np.float32)
    for i in range(n_sub):
        for ep in range(22):
            b1h[:, i * 44 + ep] = b1pad[i, ep * 128:(ep + 1) * 128]
            b1h[:, i * 44 + 22 + ep] = b1pad[i, DP + ep * 128: DP + (ep + 1) * 128]
        for cchn in range(8):
            b2h[:, i * 8 + cchn] = b2[i, cchn * 128:(cchn + 1) * 128]
            gh[:, i * 8 + cchn] = g[i, cchn * 128:(cchn + 1) * 128]
            bh[:, i * 8 + cchn] = bb[i, cchn * 128:(cchn + 1) * 128]
        qsh[:, i] = np.tile(qs[i], 2)
        ksh[:, i] = np.tile(ks[i], 2)
    return dict(wqk=np.ascontiguousarray(wqk),
                wob=np.ascontiguousarray(np.stack(
                    [_slot8(Wo[i]) for i in range(n_sub)])),
                w1p=np.ascontiguousarray(w1p), w2p=np.ascontiguousarray(w2p),
                b1p=b1h, b2p=b2h, qsp=qsh, ksp=ksh, lngp=gh, lnbp=bh)


def run_stage_b(x_full, inputs, n_sub=6):
    nc = build_stage_b(n_sub=n_sub)
    wd = _prep_b_weights(inputs, n_sub)
    cosS, sinS, cosI, sinI = _xpos_tables()
    maska, mask0 = _local_masks()
    perm = _perm_rotate_half(64)
    idn = np.eye(128, dtype=np.float32).astype(BF)
    in_maps = []
    for core in range(N_CORES):
        xc = np.asarray(x_full[core * B_CORE:(core + 1) * B_CORE], dtype=np.float32)
        im = {"xt": _to_xt(xc), "cosS": cosS, "sinS": sinS, "cosI": cosI,
              "sinI": sinI, "perm": perm, "idn": idn, "maska": maska,
              "mask0": mask0}
        im.update(wd)
        in_maps.append(im)
    res = run_bass_kernel_spmd(nc, in_maps, core_ids=list(range(N_CORES)))
    outs = [_from_xt(r["xout"]) for r in res.results]
    return np.concatenate(outs, axis=0)


def _to_xt(x_core):
    xt = x_core.transpose(2, 0, 1).reshape(NCH, 128, B_CORE * T)
    return np.ascontiguousarray(xt)


def _from_xt(xt):
    return np.ascontiguousarray(
        np.asarray(xt, dtype=np.float32).reshape(NCH, 128, B_CORE, T)
        .transpose(2, 3, 0, 1).reshape(B_CORE, T, C))


def run_stage_a(x_full, Wq, Wk, Wv, Wo):
    nc = build_stage_a()
    cos, sin = _rope_tables_l0attn()
    consts = _consts_common()
    wq = _slot8(np.asarray(Wq, dtype=np.float32).T)
    wk = _slot8(np.asarray(Wk, dtype=np.float32).T)
    wv = _chunked(np.asarray(Wv, dtype=np.float32).T)
    wo = _slot8(np.asarray(Wo, dtype=np.float32).T)
    perm = _perm_rotate_half(64)
    in_maps = []
    for core in range(N_CORES):
        xc = np.asarray(x_full[core * B_CORE:(core + 1) * B_CORE], dtype=np.float32)
        im = {"xt": _to_xt(xc), "wqt": wq, "wkt": wk, "wvt": wv, "wot": wo,
              "cos0": cos, "sin0": sin, "perm": perm, "masks": consts["masks"]}
        in_maps.append(im)
    res = run_bass_kernel_spmd(nc, in_maps, core_ids=list(range(N_CORES)))
    outs = [_from_xt(r["xout"]) for r in res.results]
    return np.concatenate(outs, axis=0)



def build_fused(n_sub=6):
    nc = bass.Bass("TRN2", target_bir_lowering=False)
    D = lambda name, shape, dt=F32: nc.dram_tensor(name, shape, dt, kind="ExternalInput").ap()
    xt = D("xt", [NCH, 128, B_CORE * T])
    x1t = nc.dram_tensor("x1t", [NCH, 128, B_CORE * T], F32, kind="Internal").ap()
    x2t = nc.dram_tensor("x2t", [NCH, 128, B_CORE * T], F32, kind="Internal").ap()
    x3t = nc.dram_tensor("x3t", [NCH, 128, B_CORE * T], F32, kind="Internal").ap()
    xout = nc.dram_tensor("xout", [NCH, 128, B_CORE * T], F32, kind="ExternalOutput").ap()
    a = NS(wqt=D("a_wqt", [8, 128, NCH * 128]), wkt=D("a_wkt", [8, 128, NCH * 128]),
           wvt=D("a_wvt", [128, NCH * C]),
           wot=D("a_wot", [8, 128, NCH * 128]), cos0=D("a_cos0", [128, T], BF16),
           sin0=D("a_sin0", [128, T], BF16), perm=D("a_perm", [128, 128], BF16),
           masks=D("a_masks", [128, 896], BF16))
    b = NS(wqk=D("b_wqk", [n_sub, 8, 128, NCH * 384]),
           wob=D("b_wob", [n_sub, 8, 128, NCH * 128]),
           w1p=D("b_w1p", [n_sub, 22, 128, NCH * 256]),
           w2p=D("b_w2p", [n_sub, 8, 128, 22 * 128]),
           b1p=D("b_b1p", [128, n_sub * 44]), b2p=D("b_b2p", [128, n_sub * 8]),
           qsp=D("b_qsp", [128, n_sub]), ksp=D("b_ksp", [128, n_sub]),
           lngp=D("b_lngp", [128, n_sub * 8]), lnbp=D("b_lnbp", [128, n_sub * 8]),
           cosS=D("b_cosS", [128, T], BF16), sinS=D("b_sinS", [128, T], BF16),
           cosI=D("b_cosI", [128, T], BF16), sinI=D("b_sinI", [128, T], BF16),
           perm=D("b_perm", [128, 128], BF16), idn=D("b_idn", [128, 128], BF16),
           maska=D("b_maska", [128, 384], BF16), mask0=D("b_mask0", [128, 2 * 256], BF16))
    c = NS(waq=D("c_waq", [8, 128, NCH * 128]), wak=D("c_wak", [8, 128, NCH * 128]),
           wav=D("c_wav", [2, 128, NCH * 512]),
           wproj=D("c_wproj", [8, 128, NCH * 128]),
           lvin=D("c_lvin", [128, 4]), cos1=D("c_cos1", [128, T], BF16),
           sin1=D("c_sin1", [128, T], BF16), perm32=D("c_perm32", [128, 128], BF16),
           masks=D("c_masks", [128, 896], BF16))
    d = NS(u1=D("d_u1", [128, NCH * 256]), v1=D("d_v1", [128, 2 * 4096]),
           b1in=D("d_b1in", [128, 32]), u2=D("d_u2", [128, 32 * 256]),
           v2=D("d_v2", [128, 2 * C]), b2in=D("d_b2in", [128, 8]))

    with contextlib.ExitStack() as ctx:
        p = Prog(nc, ctx)
        p.dma_sem = {"sync": p.sem("s_dsync"), "gpsimd": p.sem("s_dgp")}
        p.eng_sems = (p.sem("s_pe"), p.sem("s_act"), p.sem("s_dve"))
        with contextlib.ExitStack() as c1:
            _emit_stage_a(nc, c1, p, xt, a.wqt, a.wkt, a.wvt, a.wot,
                          a.cos0, a.sin0, a.perm, a.masks, x1t)
        p.barrier()
        with contextlib.ExitStack() as c2:
            _emit_stage_b(nc, c2, p, x1t, b.wqk, b.wob, b.w1p, b.w2p, b.b1p,
                          b.b2p, b.qsp, b.ksp, b.lngp, b.lnbp, b.cosS, b.sinS,
                          b.cosI, b.sinI, b.perm, b.idn, b.maska, b.mask0,
                          x2t, n_sub)
        p.barrier()
        with contextlib.ExitStack() as c3:
            _emit_stage_c(nc, c3, p, x2t, c.waq, c.wak, c.wav, c.wproj,
                          c.lvin, c.cos1, c.sin1, c.perm32, c.masks, x3t)
        p.barrier()
        with contextlib.ExitStack() as c4:
            store_last = _emit_stage_d(nc, c4, p, x3t, d.u1, d.v1, d.b1in,
                                       d.u2, d.v2, d.b2in, xout)
        p.op("sync", lambda e: e.nop(), waits=[store_last])
        with nc.Block() as block:
            p.emit(block)
    return nc


def run_fused(inputs, n_sub=6):
    x = np.asarray(inputs["L0_attn_x"], dtype=np.float32)
    nc = build_fused(n_sub=n_sub)
    consts = _consts_common()
    cos0, sin0 = _rope_tables_l0attn()
    cos1, sin1 = _rope_tables_l1()
    cosS, sinS, cosI, sinI = _xpos_tables()
    maska, mask0 = _local_masks()
    wd = _prep_b_weights(inputs, n_sub)
    wa_c = _deinterleave_wa(inputs["L1_attn_W_attn"])
    lq1, lk1 = inputs["L1_attn_lambda_q1"], inputs["L1_attn_lambda_k1"]
    lq2, lk2 = inputs["L1_attn_lambda_q2"], inputs["L1_attn_lambda_k2"]
    lv = np.zeros((128, 4), dtype=np.float32)
    lv[0:32, 0] = np.asarray(lq1); lv[0:32, 1] = np.asarray(lk1)
    lv[0:32, 2] = np.asarray(lq2); lv[0:32, 3] = np.asarray(lk2)
    b1h = np.ascontiguousarray(np.asarray(inputs["L1_ffn_b1"], dtype=np.float32).reshape(32, 128).T)
    b2h = np.ascontiguousarray(np.asarray(inputs["L1_ffn_b2"], dtype=np.float32).reshape(8, 128).T)
    common = {
        "a_wqt": _slot8(np.asarray(inputs["L0_attn_Wq"], dtype=np.float32).T),
        "a_wkt": _slot8(np.asarray(inputs["L0_attn_Wk"], dtype=np.float32).T),
        "a_wvt": _chunked(np.asarray(inputs["L0_attn_Wv"], dtype=np.float32).T),
        "a_wot": _slot8(np.asarray(inputs["L0_attn_Wo"], dtype=np.float32).T),
        "a_cos0": cos0, "a_sin0": sin0, "a_perm": _perm_rotate_half(64),
        "a_masks": consts["masks"],
        "b_cosS": cosS, "b_sinS": sinS, "b_cosI": cosI, "b_sinI": sinI,
        "b_perm": _perm_rotate_half(64),
        "b_idn": np.eye(128, dtype=np.float32).astype(BF),
        "b_maska": maska, "b_mask0": mask0,
        "c_wproj": _slot8(np.asarray(inputs["L1_attn_W_proj"], dtype=np.float32)),
        "c_lvin": lv, "c_cos1": cos1, "c_sin1": sin1,
        "c_waq": _slot8(wa_c[:, 0:C]), "c_wak": _slot8(wa_c[:, C:2 * C]),
        "c_wav": np.ascontiguousarray(np.stack(
            [_chunked(wa_c[:, 2 * C + h * 512:2 * C + (h + 1) * 512]) for h in range(2)])),
        "c_perm32": _perm_rotate_half(32), "c_masks": consts["masks"],
        "d_u1": _chunked(inputs["L1_ffn_U1"]), "d_v1": _chunked(inputs["L1_ffn_V1"]),
        "d_b1in": b1h,
        "d_u2": _chunked(inputs["L1_ffn_U2"]), "d_v2": _chunked(inputs["L1_ffn_V2"]),
        "d_b2in": b2h,
    }
    for k, v in wd.items():
        common["b_" + k] = v
    in_maps = []
    for core in range(N_CORES):
        xc = x[core * B_CORE:(core + 1) * B_CORE]
        im = dict(common)
        im["xt"] = _to_xt(xc)
        in_maps.append(im)
    res = run_bass_kernel_spmd(nc, in_maps, core_ids=list(range(N_CORES)))
    outs = [_from_xt(r["xout"]) for r in res.results]
    return np.concatenate(outs, axis=0)

def kernel(**inputs):
    return run_fused(inputs, n_sub=6)


# version 06
# speedup vs baseline: 1.1411x; 1.1411x over previous
import contextlib
import math
import numpy as np
import ml_dtypes

import concourse.bass as bass
import concourse.mybir as mybir
from concourse.bass_utils import run_bass_kernel_spmd

F32 = mybir.dt.float32
BF16 = mybir.dt.bfloat16
AF = mybir.ActivationFunctionType
ALU = mybir.AluOpType
AX = mybir.AxisListType
BF = ml_dtypes.bfloat16

N_CORES = 8
B_FULL = 16
B_CORE = B_FULL // N_CORES
T = 2048
C = 1024
NCH = C // 128
NTC = T // 512

ENGINES = ("sync", "gpsimd", "tensor", "scalar", "vector")


class Prog:

    def __init__(self, nc, ctx):
        self.nc = nc
        self.ctx = ctx
        self.ops = {e: [] for e in ENGINES}
        self.semcnt = {}

    def sem(self, name):
        s = self.ctx.enter_context(self.nc.semaphore(name))
        self.semcnt[id(s)] = 0
        return s

    def op(self, eng, fn, waits=(), inc=None, amount=1):
        tok = None
        if inc is not None:
            self.semcnt[id(inc)] += amount
            tok = (inc, self.semcnt[id(inc)])
        self.ops[eng].append(
            (fn, tuple(w for w in waits if w is not None), inc, amount))
        return tok

    def dma(self, eng, out, in_, waits=(), inc=None, **kw):
        if inc is None:
            inc = self.dma_sem[eng]
        return self.op(eng, lambda e: e.dma_start(out=out, in_=in_, **kw),
                       waits=waits, inc=inc, amount=16)

    def barrier(self):
        toks = []
        for sem in self.dma_sem.values():
            if self.semcnt[id(sem)]:
                toks.append((sem, self.semcnt[id(sem)]))
        for sem in self.eng_sems:
            if self.semcnt[id(sem)]:
                toks.append((sem, self.semcnt[id(sem)]))
        for eng in ENGINES:
            self.op(eng, lambda e: e.nop(), waits=toks)

    def emit(self, block):
        def mk(name):
            def run(eng):
                waited = {}
                for fn, waits, inc, amount in self.ops[name]:
                    for (sem, cnt) in waits:
                        if waited.get(id(sem), 0) < cnt:
                            eng.wait_ge(sem, cnt)
                            waited[id(sem)] = cnt
                    inst = fn(eng)
                    if inc is not None:
                        inst.then_inc(inc, amount)
            return run
        block.sync(mk("sync"))
        block.gpsimd(mk("gpsimd"))
        block.tensor(mk("tensor"))
        block.scalar(mk("scalar"))
        block.vector(mk("vector"))


def _rope_tables_l0attn():
    inv_freq = 1.0 / (10000.0 ** (np.arange(0, 64, 2, dtype=np.float64) / 64))
    t = np.arange(T, dtype=np.float64)
    freqs = np.outer(t, inv_freq)
    emb = np.concatenate([freqs, freqs], axis=-1)
    cos = np.cos(emb).T.astype(np.float32)
    sin = np.sin(emb).T.astype(np.float32)
    return (np.concatenate([cos, cos], 0).astype(BF),
            np.concatenate([sin, sin], 0).astype(BF))


def _perm_rotate_half(dim_head):
    h = dim_head // 2
    P = np.zeros((128, 128), dtype=np.float32)
    for base in range(0, 128, dim_head):
        for m in range(h):
            P[base + m + h, base + m] = -1.0
            P[base + m, base + m + h] = 1.0
    return P.astype(BF)


def _tri_masks():
    k = np.arange(128)[:, None]
    i = np.arange(896)[None, :]
    return (i >= 384 + k).astype(np.float32).astype(BF)


def _consts_common():
    return {
        "ones128": np.ones((128, 1), dtype=BF),
        "ones64": np.ones((64, 1), dtype=BF),
        "onesr1": np.ones((1, 128), dtype=np.float32),
        "negr1": -np.ones((1, 128), dtype=np.float32),
        "onesr64": np.ones((1, 64), dtype=np.float32),
        "masks": _tri_masks(),
    }


class NS:
    def __init__(self, **kw):
        self.__dict__.update(kw)


def _slot8(w):
    return np.ascontiguousarray(
        np.stack([_chunked(w[:, j * 128:(j + 1) * 128]) for j in range(8)]))


def _chunked(w):
    w = np.asarray(w, dtype=np.float32)
    K, N = w.shape
    return np.ascontiguousarray(w.reshape(K // 128, 128, N).transpose(1, 0, 2)
                                .reshape(128, -1))


def tokmax(toks):
    best = {}
    for t in toks:
        if t is None:
            continue
        sem, cnt = t
        if best.get(id(sem), (None, -1))[1] < cnt:
            best[id(sem)] = (sem, cnt)
    return list(best.values())


GA = 0.044715
GS = 0.7978845608028654

SOFF_SUM = 0
SOFF_SQ = T
SOFF_MU2 = 2 * T
SOFF_REC = 2 * T
SOFF_MAXK = 2 * T + 512
STAT_W = 2 * T + 528


def emit_ln_rows(p, s, get_chunk, sqch, n_t=T):
    ntc = n_t // 512
    ST = s.stat
    z1 = p.op("vector", lambda e: e.memset(ST[0:1, 0:2 * n_t], 0.0),
              waits=tokmax(s.stat_users), inc=s.dve)
    s.stat_users = []
    acc_toks = [z1]
    s.chunk_consumed = {}
    for c in range(NCH):
        ap, ready = get_chunk(c)
        sq_tok = p.op("scalar", lambda e, ap=ap: e.activation(sqch, ap, AF.Square),
                      waits=list(ready) + ([s.sqch_free] if s.sqch_free else []),
                      inc=s.act)
        mm1 = mm2 = None
        for tc in range(ntc):
            tsl = slice(tc * 512, tc * 512 + 512)
            mm1 = p.op("tensor", lambda e, ap=ap, tsl=tsl: e.matmul(
                s.ps_s1[:], s.ones128[:], ap[:, tsl],
                start=True, stop=True, skip_group_check=True),
                waits=list(ready) + s.cgate + ([s.psfree[id(s.ps_s1)]] if s.psfree.get(id(s.ps_s1)) else []),
                inc=s.pe)
            a1 = p.op("vector", lambda e, tsl=tsl: e.tensor_add(
                ST[0:1, SOFF_SUM:SOFF_SUM + n_t][:, tsl],
                ST[0:1, SOFF_SUM:SOFF_SUM + n_t][:, tsl], s.ps_s1[:]),
                waits=[mm1, z1], inc=s.dve)
            s.psfree[id(s.ps_s1)] = a1
            mm2 = p.op("tensor", lambda e, tsl=tsl, sqch=sqch: e.matmul(
                s.ps_s2[:], s.ones128[:], sqch[:, tsl],
                start=True, stop=True, skip_group_check=True),
                waits=[sq_tok] + ([s.psfree[id(s.ps_s2)]] if s.psfree.get(id(s.ps_s2)) else []),
                inc=s.pe)
            a2 = p.op("vector", lambda e, tsl=tsl: e.tensor_add(
                ST[0:1, SOFF_SQ:SOFF_SQ + n_t][:, tsl],
                ST[0:1, SOFF_SQ:SOFF_SQ + n_t][:, tsl], s.ps_s2[:]),
                waits=[mm2, z1], inc=s.dve)
            s.psfree[id(s.ps_s2)] = a2
            acc_toks += [a1, a2]
        s.sqch_free = acc_toks[-1]
        s.chunk_consumed[c] = [mm1, sq_tok]
    bc_toks = []
    for tc in range(ntc):
        mu = ST[0:1, SOFF_SUM + tc * 512:SOFF_SUM + tc * 512 + 512]
        rs = ST[0:1, SOFF_SQ + tc * 512:SOFF_SQ + tc * 512 + 512]
        mu2 = ST[0:1, SOFF_MU2:SOFF_MU2 + 512]
        t_mu = p.op("vector", lambda e, mu=mu: e.tensor_scalar_mul(mu, mu, 1.0 / C),
                    waits=acc_toks, inc=s.dve)
        t_mu2 = p.op("scalar", lambda e, mu=mu, mu2=mu2: e.activation(mu2, mu, AF.Square),
                     waits=[t_mu] + ([s.mu2_free] if getattr(s, "mu2_free", None) else []),
                     inc=s.act)
        t_var = p.op("vector", lambda e, rs=rs, mu2=mu2: e.scalar_tensor_tensor(
            rs, rs, 1.0 / C, mu2, ALU.mult, ALU.subtract),
            waits=[t_mu2] + acc_toks, inc=s.dve)
        s.mu2_free = t_var
        t_ln = p.op("scalar", lambda e, rs=rs: e.activation(rs, rs, AF.Ln, bias=s.eps[0:1, 0:1]),
                    waits=[t_var, s.eps_tok], inc=s.act)
        t_rstd = p.op("scalar", lambda e, rs=rs: e.activation(rs, rs, AF.Exp, scale=-0.5),
                      waits=[t_ln], inc=s.act)
        t_mr = p.op("vector", lambda e, mu=mu, rs=rs: e.tensor_mul(mu, mu, rs),
                    waits=[t_rstd], inc=s.dve)
        b1 = p.op("tensor", lambda e, rs=rs: e.matmul(
            s.ps_c[:], s.onesr1[:], rs, start=True, stop=True, skip_group_check=True),
            waits=[t_rstd] + s.cgate + ([s.psfree[id(s.ps_c)]] if s.psfree.get(id(s.ps_c)) else []), inc=s.pe)
        b2 = p.op("tensor", lambda e, mu=mu: e.matmul(
            s.ps_d[:], s.negr1[:], mu, start=True, stop=True, skip_group_check=True),
            waits=[t_mr] + s.cgate + ([s.psfree[id(s.ps_d)]] if s.psfree.get(id(s.ps_d)) else []), inc=s.pe)
        c1 = p.op("scalar", lambda e, tc=tc: e.copy(
            s.rstdB[:, tc * 512:tc * 512 + 512], s.ps_c[:]), waits=[b1], inc=s.act)
        c2 = p.op("scalar", lambda e, tc=tc: e.copy(
            s.nmrB[:, tc * 512:tc * 512 + 512], s.ps_d[:]), waits=[b2], inc=s.act)
        s.psfree[id(s.ps_c)] = c1
        s.psfree[id(s.ps_d)] = c2
        bc_toks += [c1, c2]
    s.stat_users += bc_toks
    return bc_toks




def emit_lnin_streamed(p, s, xt_ap, bsl, dst, XC, sqch, TMPF, bgate, dst_of=None):
    if dst_of is None:
        dst_of = lambda c: dst[:, c * T:(c + 1) * T]
    def get_chunk(cc):
        slot = cc % 2
        if cc < 2:
            w = list(bgate) + s.cgate
        else:
            w = s.chunk_consumed[cc - 2]
        ld = p.dma("gpsimd", XC[:, slot * T:(slot + 1) * T],
                   xt_ap[cc, :, bsl], waits=w)
        return XC[:, slot * T:(slot + 1) * T], [ld]
    bc = emit_ln_rows(p, s, get_chunk, sqch)
    xln_toks = []
    for c in range(NCH):
        slot = c % 2
        w = bc + ([xln_toks[c - 2]] if c >= 2 else [])
        ld = p.dma("gpsimd", XC[:, slot * T:(slot + 1) * T],
                   xt_ap[c, :, bsl], waits=w)
        t2 = None
        for tc in range(NTC):
            tsl = slice(tc * 512, tc * 512 + 512)
            t1 = p.op("vector", lambda e, slot=slot, tsl=tsl: e.tensor_mul(
                TMPF[:], XC[:, slot * T:(slot + 1) * T][:, tsl], s.rstdB[:, tsl]),
                waits=[ld] + bc + ([s.tmpf_free] if s.tmpf_free else []),
                inc=s.dve)
            t2 = p.op("vector", lambda e, c=c, tsl=tsl: e.tensor_add(
                dst_of(c)[:, tsl], TMPF[:], s.nmrB[:, tsl]),
                waits=[t1], inc=s.dve)
            s.tmpf_free = t2
        xln_toks.append(t2)
    return xln_toks


def emit_lnout_add_store(p, s, yb, yb_toks, sqch, xt_ap, xout_ap, bsl, XS, TMPF,
                         yb_of=None):
    if yb_of is None:
        yb_of = lambda c: yb[:, c * T:(c + 1) * T]
    def get_chunk(cc):
        return yb_of(cc), yb_toks[cc]
    bc2 = emit_ln_rows(p, s, get_chunk, sqch)
    st_toks = []
    for c in range(NCH):
        ld = p.dma("sync", XS[:], xt_ap[c, :, bsl],
                   waits=([st_toks[c - 1]] if c >= 1 else
                          ([s.xs_gate] if getattr(s, "xs_gate", None) else [])))
        t3 = None
        for tc in range(NTC):
            tsl = slice(tc * 512, tc * 512 + 512)
            t1 = p.op("vector", lambda e, c=c, tsl=tsl: e.tensor_mul(
                TMPF[:], yb_of(c)[:, tsl], s.rstdB[:, tsl]),
                waits=bc2 + ([s.tmpf_free] if s.tmpf_free else []), inc=s.dve)
            t2 = p.op("vector", lambda e, tsl=tsl: e.tensor_add(
                TMPF[:], TMPF[:], s.nmrB[:, tsl]), waits=[t1], inc=s.dve)
            t3 = p.op("vector", lambda e, tsl=tsl: e.tensor_add(
                XS[:, tsl], XS[:, tsl], TMPF[:]), waits=[t2, ld], inc=s.dve)
            s.tmpf_free = t3
        st = p.dma("sync", xout_ap[c, :, bsl], XS[:], waits=[t3])
        st_toks.append(st)
    return st_toks


def _emit_stage_a(nc, ctx, p, xt, wqt, wkt, wvt, wot, cos0, sin0, perm_in, masks_in, xout):
    sb = lambda name, shape, dt: ctx.enter_context(nc.sbuf_tensor('a_' + name, shape, dt))
    psumt = lambda name, shape: ctx.enter_context(nc.psum_tensor('a_' + name, shape, F32))

    SCA = sb("SCA", [128, NCH * T], BF16)
    SCB = sb("SCB", [128, NCH * T], BF16)
    V65 = sb("V65", [128, 16 * 1040], BF16)
    QR2 = sb("QR2", [128, T], BF16)
    KR2 = sb("KR2", [128, T], BF16)
    QH = sb("QH", [65, T], BF16)
    KH = sb("KH", [65, T], BF16)
    PT = sb("PT", [128, 4 * 512], BF16)
    OT2 = sb("OT2", [64, 512], BF16)
    OTN = sb("OTN", [64, 512], BF16)
    XC = sb("XC", [128, 2 * T], BF16)
    XS = sb("XS", [128, T], F32)
    WBIG = sb("WBIG", [128, NCH * C], BF16)
    WSM = sb("WSM", [128, 2 * C], BF16)
    COS = sb("COS", [128, T], BF16)
    SIN = sb("SIN", [128, T], BF16)
    PERM = sb("PERM", [128, 128], BF16)
    ONES128 = sb("ONES128", [128, 1], BF16)
    ONES64 = sb("ONES64", [64, 1], BF16)
    ONESR1 = sb("ONESR1", [1, 128], F32)
    NEGR1 = sb("NEGR1", [1, 128], F32)
    ONESR64 = sb("ONESR64", [1, 64], F32)
    MASKS = sb("MASKS", [128, 896], BF16)
    STAT = sb("STAT", [1, STAT_W], F32)
    EPS = sb("EPS", [1, 8], F32)
    RSTDB = sb("RSTDB", [128, T], F32)
    NMRB = sb("NMRB", [128, T], BF16)
    TMPF = sb("TMPF", [128, 512], F32)
    TMPG = sb("TMPG", [128, 512], F32)

    ps_a = psumt("ps_a", [128, 512])
    ps_b = psumt("ps_b", [128, 512])
    ps_c = psumt("ps_c", [128, 512])
    ps_d = psumt("ps_d", [128, 512])
    ps_o = psumt("ps_o", [65, 512])
    ps_r = psumt("ps_r", [64, 512])
    ps_s1 = psumt("ps_s1", [1, 512])
    ps_s2 = psumt("ps_s2", [1, 512])

    s = NS()
    s.pe, s.act, s.dve = p.eng_sems

    s.ones128, s.onesr1, s.negr1 = ONES128, ONESR1, NEGR1
    s.stat = STAT
    s.rstdB, s.nmrB = RSTDB, NMRB
    s.ps_s1, s.ps_s2, s.ps_c, s.ps_d = ps_s1, ps_s2, ps_c, ps_d
    s.psfree = {}
    s.sqch_free = None
    s.stat_users = []
    s.tmpf_free = s.tmpg_free = None

    const_toks = []
    for dst, srcap in [(COS, cos0), (SIN, sin0), (PERM, perm_in),
                       (MASKS, masks_in)]:
        const_toks.append(p.dma("sync", dst[:], srcap))
    for tns, val in [(ONES128, 1.0), (ONES64, 1.0), (ONESR1, 1.0),
                     (NEGR1, -1.0), (ONESR64, 1.0)]:
        const_toks.append(p.op("vector", lambda e, tns=tns, val=val: e.memset(tns[:], val),
                               inc=s.dve))
    s.cgate = tokmax(const_toks)
    s.eps = EPS
    s.eps_tok = p.op("vector", lambda e: e.memset(EPS[:], 1e-5), inc=s.dve)
    khm1 = p.op("vector", lambda e: e.memset(KH[64:65, :], -1.0), inc=s.dve)
    ones_v = p.op("vector", lambda e: e.memset(
        V65.ap().rearrange("p (tt h o) -> p tt h o", tt=16, h=16)[:, :, :, 64:65],
        1.0), inc=s.dve)

    BPAD = 1.02
    SM_SCALE = 1.0 / math.sqrt(64)
    store_last = None

    for b in range(B_CORE):
        bsl = slice(b * T, (b + 1) * T)
        bgate = [store_last] if store_last else []

        wv_tok = p.dma("gpsimd", WBIG[:], wvt,
                       waits=bgate + ([s.psfree.get("wbig")] if s.psfree.get("wbig") else []))

        xln_toks = emit_lnin_streamed(p, s, xt, bsl, SCA, XC, KR2[:], TMPF, bgate)
        xln_last = xln_toks[-1]

        for tt in range(16):
            for half in range(2):
                pst = ps_a if half == 0 else ps_b
                mm_tok = None
                for cch in range(NCH):
                    w = []
                    if cch == 0:
                        w = [wv_tok, xln_last, ones_v]
                        ft = s.psfree.get(id(pst))
                        if ft is not None:
                            w.append(ft)
                    mm_tok = p.op("tensor", lambda e, cch=cch, tt=tt, half=half, pst=pst: e.matmul(
                        pst[:], SCA[:, cch * T + tt * 128: cch * T + tt * 128 + 128],
                        WBIG[:, cch * C + half * 512: cch * C + half * 512 + 512],
                        start=(cch == 0), stop=(cch == NCH - 1), skip_group_check=True),
                        waits=w, inc=(s.pe if cch == NCH - 1 else None))

                def evac_v(e, tt=tt, half=half, pst=pst):
                    src_ap = pst.ap().rearrange("p (h d) -> p h d", h=8)
                    dst = V65.ap().rearrange("p (t2 h o) -> p t2 h o", t2=16, h=16)[
                        :, tt, half * 8:(half + 1) * 8, 0:64]
                    return e.activation(dst, src_ap, AF.Copy)
                ev = p.op("scalar", evac_v, waits=[mm_tok], inc=s.act)
                s.psfree[id(pst)] = ev
                vlast = ev
        s.psfree["wbig"] = vlast

        o_toks = []
        prev_qn = None
        prerope = XC[:, T:2 * T]
        for hp in range(8):
            pr_toks = {}
            for (which, wdram, slot) in (("q", wqt, 0), ("k", wkt, 1)):
                wtok = p.dma("gpsimd", WSM[:, slot * C:(slot + 1) * C], wdram[hp],
                             waits=[s.psfree.get(("wsm", slot))] if s.psfree.get(("wsm", slot)) else [])
                banks = [ps_a, ps_b, ps_c, ps_d]
                mm_last = {}
                for tc in range(NTC):
                    for cch in range(NCH):
                        pst = banks[tc]
                        w = []
                        if cch == 0:
                            w = [wtok, xln_last]
                            ft = s.psfree.get(id(pst))
                            if ft is not None:
                                w.append(ft)
                        mm_last[tc] = p.op("tensor", lambda e, cch=cch, tc=tc, pst=pst, slot=slot: e.matmul(
                            pst[:], WSM[:, slot * C + cch * 128: slot * C + cch * 128 + 128],
                            SCA[:, cch * T + tc * 512: cch * T + tc * 512 + 512],
                            start=(cch == 0), stop=(cch == NCH - 1), skip_group_check=True),
                            waits=w, inc=(s.pe if cch == NCH - 1 else None))
                rope_toks = []
                for tc in range(NTC):
                    tsl = slice(tc * 512, tc * 512 + 512)
                    pst = banks[tc]
                    ev = p.op("scalar", lambda e, tsl=tsl, pst=pst: e.copy(prerope[:, tsl], pst[:]),
                              waits=[mm_last[tc]] + ([s.prerope_free] if getattr(s, "prerope_free", None) else []),
                              inc=s.act)
                    s.psfree[id(pst)] = ev
                    rot = p.op("tensor", lambda e, tsl=tsl, pst=pst: e.matmul(
                        pst[:], PERM[:], prerope[:, tsl], start=True, stop=True,
                        skip_group_check=True),
                        waits=[ev] + s.cgate, inc=s.pe)
                    t1 = p.op("vector", lambda e, tsl=tsl: e.tensor_mul(
                        TMPF[:], prerope[:, tsl], COS[:, tsl]),
                        waits=[ev] + s.cgate + ([s.tmpf_free] if s.tmpf_free else []), inc=s.dve)
                    t2 = p.op("vector", lambda e, tsl=tsl, pst=pst: e.tensor_mul(
                        TMPG[:], pst[:], SIN[:, tsl]),
                        waits=[rot] + ([s.tmpg_free] if s.tmpg_free else []),
                        inc=s.dve)
                    s.psfree[id(pst)] = t2
                    dstb = QR2 if which == "q" else KR2
                    t3 = p.op("vector", lambda e, tsl=tsl, dstb=dstb: e.tensor_add(
                        dstb[:, tsl], TMPF[:], TMPG[:]), waits=[t1, t2], inc=s.dve)
                    s.tmpf_free = t3
                    s.tmpg_free = t3
                    rope_toks.append(t3)
                s.psfree[("wsm", slot)] = rope_toks[-1]
                s.prerope_free = rope_toks[-1]
                pr_toks[which] = rope_toks

            for half in range(2):
                h = hp * 2 + half
                sqh = prerope[0:64, :]
                ksq = p.op("scalar", lambda e, half=half, sqh=sqh: e.activation(
                    sqh, KR2[64 * half:64 * half + 64, :], AF.Square),
                    waits=pr_toks["k"] + pr_toks["q"] + ([prev_qn] if prev_qn else []),
                    inc=s.act)
                kn_toks = []
                for tc in range(NTC):
                    mm = p.op("tensor", lambda e, tc=tc, sqh=sqh: e.matmul(
                        ps_s1[:], ONES64[:], sqh[:, tc * 512:(tc + 1) * 512],
                        start=True, stop=True, skip_group_check=True),
                        waits=[ksq] + s.cgate + ([s.psfree[id(s.ps_s1)]] if s.psfree.get(id(s.ps_s1)) else []), inc=s.pe)
                    kn = p.op("scalar", lambda e, tc=tc: e.activation(
                        STAT[0:1, SOFF_SUM + tc * 512:SOFF_SUM + tc * 512 + 512],
                        ps_s1[:], AF.Sqrt), waits=[mm], inc=s.act)
                    s.psfree[id(s.ps_s1)] = kn
                    kn_toks.append(kn)
                maxk = p.op("vector", lambda e: e.reduce_max(
                    STAT[0:1, SOFF_MAXK:SOFF_MAXK + 1],
                    STAT[0:1, SOFF_SUM:SOFF_SUM + T], axis=AX.X),
                    waits=kn_toks, inc=s.dve)
                qsq = p.op("scalar", lambda e, half=half, sqh=sqh: e.activation(
                    sqh, QR2[64 * half:64 * half + 64, :], AF.Square),
                    waits=kn_toks + [ksq], inc=s.act)
                qn_toks = []
                for tc in range(NTC):
                    mm = p.op("tensor", lambda e, tc=tc, sqh=sqh: e.matmul(
                        ps_s1[:], ONES64[:], sqh[:, tc * 512:(tc + 1) * 512],
                        start=True, stop=True, skip_group_check=True),
                        waits=[qsq, maxk] + s.cgate + ([s.psfree[id(s.ps_s1)]] if s.psfree.get(id(s.ps_s1)) else []),
                        inc=s.pe)
                    qn = p.op("scalar", lambda e, tc=tc: e.activation(
                        STAT[0:1, SOFF_SUM + tc * 512:SOFF_SUM + tc * 512 + 512],
                        ps_s1[:], AF.Sqrt), waits=[mm], inc=s.act)
                    s.psfree[id(s.ps_s1)] = qn
                    qn_toks.append(qn)
                prev_qn = qn_toks[-1]
                s.prerope_free = qn_toks[-1]
                bq = p.op("vector", lambda e: e.tensor_scalar(
                    STAT[0:1, SOFF_SQ:SOFF_SQ + T],
                    STAT[0:1, SOFF_SUM:SOFF_SUM + T],
                    STAT[0:1, SOFF_MAXK:SOFF_MAXK + 1], BPAD,
                    ALU.mult, ALU.mult),
                    waits=qn_toks + [maxk], inc=s.dve)
                prev_o = tokmax(o_toks)
                qmove = p.op("scalar", lambda e, half=half: e.copy(
                    QH[0:64, :], QR2[64 * half:64 * half + 64, :]),
                    waits=pr_toks["q"] + prev_o, inc=s.act)
                kmove = p.op("scalar", lambda e, half=half: e.copy(
                    KH[0:64, :], KR2[64 * half:64 * half + 64, :]),
                    waits=pr_toks["k"] + [khm1] + prev_o, inc=s.act)
                brow = p.op("scalar", lambda e: e.copy(
                    QH[64:65, :], STAT[0:1, SOFF_SQ:SOFF_SQ + T]),
                    waits=[bq, qmove], inc=s.act)
                s.stat_users += [bq, brow, maxk]

                for qch in range(4):
                    nkt = 4 * qch + 4
                    pv_tok = None
                    ex_toks = {}

                    def emit_st(kt, qch=qch):
                        pst = ps_a if kt % 2 == 0 else ps_b
                        ptslot = kt % 4
                        stw = [brow, kmove]
                        ft = s.psfree.get(id(pst))
                        if ft is not None:
                            stw.append(ft)
                        st = p.op("tensor", lambda e, kt=kt, qch=qch, pst=pst: e.matmul(
                            pst[:], KH[:, kt * 128:(kt + 1) * 128],
                            QH[:, qch * 512:(qch + 1) * 512],
                            start=True, stop=True, skip_group_check=True),
                            waits=stw, inc=s.pe)
                        expw = [st]
                        ptfree = s.psfree.get(("pt", ptslot))
                        if ptfree is not None:
                            expw.append(ptfree)
                        ex = p.op("scalar", lambda e, ptslot=ptslot, pst=pst: e.activation(
                            PT[:, ptslot * 512:(ptslot + 1) * 512], pst[:], AF.Exp,
                            scale=SM_SCALE),
                            waits=expw, inc=s.act)
                        s.psfree[id(pst)] = ex
                        if kt >= 4 * qch:
                            j = kt - 4 * qch
                            ex = p.op("vector", lambda e, ptslot=ptslot, j=j: e.tensor_mul(
                                PT[:, ptslot * 512:(ptslot + 1) * 512],
                                PT[:, ptslot * 512:(ptslot + 1) * 512],
                                MASKS[:, (3 - j) * 128:(3 - j) * 128 + 512]),
                                waits=[ex] + s.cgate, inc=s.dve)
                        ex_toks[kt] = ex

                    def emit_pv(kt):
                        nonlocal pv_tok
                        ptslot = kt % 4
                        pvw = [ex_toks[kt]]
                        if kt == 0:
                            ft = s.psfree.get(id(ps_o))
                            if ft is not None:
                                pvw.append(ft)
                            pvw.append(vlast)
                            pvw.append(ones_v)
                        pv_tok = p.op("tensor", lambda e, kt=kt, h=h, ptslot=ptslot: e.matmul(
                            ps_o[:], V65[:, kt * 1040 + h * 65: kt * 1040 + h * 65 + 65],
                            PT[:, ptslot * 512:(ptslot + 1) * 512],
                            start=(kt == 0), stop=(kt == nkt - 1), skip_group_check=True),
                            waits=pvw, inc=(s.pe if kt == nkt - 1 else None))
                        s.psfree[("pt", ptslot)] = pv_tok

                    emit_st(0)
                    for kt in range(1, nkt):
                        emit_st(kt)
                        emit_pv(kt - 1)
                    emit_pv(nkt - 1)
                    rec = p.op("vector", lambda e: e.reciprocal(
                        STAT[0:1, SOFF_REC:SOFF_REC + 512], ps_o[64:65, :]),
                        waits=[pv_tok] + ([s.rec_free] if getattr(s, "rec_free", None) else []),
                        inc=s.dve)
                    otmp = p.op("scalar", lambda e: e.copy(OT2[:], ps_o[0:64, :]),
                                waits=[pv_tok] + ([s.ot2_free] if getattr(s, "ot2_free", None) else []),
                                inc=s.act)
                    s.psfree[id(ps_o)] = otmp
                    rb = p.op("tensor", lambda e: e.matmul(
                        ps_r[:], ONESR64[:], STAT[0:1, SOFF_REC:SOFF_REC + 512],
                        start=True, stop=True, skip_group_check=True),
                        waits=[rec] + s.cgate + ([s.ps_r_free] if getattr(s, "ps_r_free", None) else []),
                        inc=s.pe)
                    s.rec_free = rb
                    s.mu2_free = rb
                    odn = p.op("vector", lambda e: e.tensor_mul(OTN[:], OT2[:], ps_r[:]),
                               waits=[rb, otmp] + ([s.otn_free] if getattr(s, "otn_free", None) else []),
                               inc=s.dve)
                    s.ps_r_free = odn
                    s.ot2_free = odn
                    oc = p.op("scalar", lambda e, h=h, qch=qch: e.copy(
                        SCB[:, (h // 2) * T + qch * 512:(h // 2) * T + qch * 512 + 512][
                            64 * (h % 2):64 * (h % 2) + 64, :], OTN[:]),
                        waits=[odn], inc=s.act)
                    s.otn_free = oc
                    s.stat_users += [rb]
                    o_toks.append(oc)
        o_last = tokmax(o_toks)

        yb_toks = {}
        for outc in range(NCH):
            slot = outc % 2
            wtok = p.dma("gpsimd", WSM[:, slot * C:(slot + 1) * C], wot[outc],
                         waits=[s.psfree.get(("wsm", slot))] if s.psfree.get(("wsm", slot)) else [])
            banks = [ps_a, ps_b, ps_c, ps_d]
            mm_last = {}
            for tc in range(NTC):
                for cch in range(NCH):
                    pst = banks[tc]
                    w = []
                    if cch == 0:
                        w = [wtok] + o_last
                        ft = s.psfree.get(id(pst))
                        if ft is not None:
                            w.append(ft)
                    mm_last[tc] = p.op("tensor", lambda e, cch=cch, tc=tc, pst=pst, slot=slot: e.matmul(
                        pst[:], WSM[:, slot * C + cch * 128: slot * C + cch * 128 + 128],
                        SCB[:, cch * T + tc * 512: cch * T + tc * 512 + 512],
                        start=(cch == 0), stop=(cch == NCH - 1), skip_group_check=True),
                        waits=w, inc=(s.pe if cch == NCH - 1 else None))
            evs = []
            for tc in range(NTC):
                tsl = slice(tc * 512, tc * 512 + 512)
                pst = banks[tc]
                ev = p.op("scalar", lambda e, outc=outc, tsl=tsl, pst=pst: e.copy(
                    SCA[:, outc * T:(outc + 1) * T][:, tsl], pst[:]),
                    waits=[mm_last[tc]], inc=s.act)
                s.psfree[id(pst)] = ev
                evs.append(ev)
            s.psfree[("wsm", slot)] = evs[-1]
            yb_toks[outc] = evs

        st_toks = emit_lnout_add_store(p, s, SCA, yb_toks, XC[:, 0:T],
                                       xt, xout, bsl, XS, TMPF)
        store_last = st_toks[-1]

    return store_last


def build_stage_a():
    nc = bass.Bass("TRN2", target_bir_lowering=False)
    xt = nc.dram_tensor("xt", [NCH, 128, B_CORE * T], F32, kind="ExternalInput").ap()
    wqt = nc.dram_tensor("wqt", [8, 128, NCH * 128], F32, kind="ExternalInput").ap()
    wkt = nc.dram_tensor("wkt", [8, 128, NCH * 128], F32, kind="ExternalInput").ap()
    wvt = nc.dram_tensor("wvt", [128, NCH * C], F32, kind="ExternalInput").ap()
    wot = nc.dram_tensor("wot", [8, 128, NCH * 128], F32, kind="ExternalInput").ap()
    cos0 = nc.dram_tensor("cos0", [128, T], BF16, kind="ExternalInput").ap()
    sin0 = nc.dram_tensor("sin0", [128, T], BF16, kind="ExternalInput").ap()
    perm_in = nc.dram_tensor("perm", [128, 128], BF16, kind="ExternalInput").ap()
    masks_in = nc.dram_tensor("masks", [128, 896], BF16, kind="ExternalInput").ap()
    xout = nc.dram_tensor("xout", [NCH, 128, B_CORE * T], F32, kind="ExternalOutput").ap()

    with contextlib.ExitStack() as ctx:
        p = Prog(nc, ctx)
        p.dma_sem = {"sync": p.sem("s_dsync"), "gpsimd": p.sem("s_dgp")}
        p.eng_sems = (p.sem("s_pe"), p.sem("s_act"), p.sem("s_dve"))
        with contextlib.ExitStack() as c1:
            store_last = _emit_stage_a(nc, c1, p, xt, wqt, wkt, wvt, wot, cos0, sin0, perm_in, masks_in, xout)
        p.op("sync", lambda e: e.nop(), waits=[store_last])
        with nc.Block() as block:
            p.emit(block)
    return nc

def _emit_stage_d(nc, ctx, p, xt, u1, v1, b1in, u2, v2, b2in, xout):
    sb = lambda name, shape, dt: ctx.enter_context(nc.sbuf_tensor('d_' + name, shape, dt))
    psumt = lambda name, shape: ctx.enter_context(nc.psum_tensor('d_' + name, shape, F32))

    SCA = sb("SCA", [128, NCH * T], BF16)
    PB = sb("PB", [128, 2 * T], BF16)
    QB = sb("QB", [128, 2 * T], BF16)
    HB = sb("HB", [128, 32 * 512], BF16)
    U1SB = sb("U1SB", [128, NCH * 256], BF16)
    V1SB = sb("V1SB", [128, 2 * 4096], BF16)
    U2SB = sb("U2SB", [128, 32 * 256], BF16)
    V2SB = sb("V2SB", [128, 2 * C], BF16)
    B1SB = sb("B1SB", [128, 32], F32)
    B2SB = sb("B2SB", [128, 8], F32)
    XC = sb("XC", [128, 2 * T], BF16)
    XS = sb("XS", [128, T], F32)
    ONES128 = sb("ONES128", [128, 1], BF16)
    ONESR1 = sb("ONESR1", [1, 128], F32)
    NEGR1 = sb("NEGR1", [1, 128], F32)
    STAT = sb("STAT", [1, STAT_W], F32)
    EPS = sb("EPS", [1, 8], F32)
    RSTDB = sb("RSTDB", [128, T], F32)
    NMRB = sb("NMRB", [128, T], BF16)
    TMPF = sb("TMPF", [128, 512], F32)
    VB = sb("VB", [128, 512], F32)
    SQ = sb("SQ", [128, 512], F32)
    SQCH = sb("SQCH", [128, T], BF16)

    ps_a = psumt("ps_a", [128, 512])
    ps_b = psumt("ps_b", [128, 512])
    ps_c = psumt("ps_c", [128, 512])
    ps_d = psumt("ps_d", [128, 512])
    ps_s1 = psumt("ps_s1", [1, 512])
    ps_s2 = psumt("ps_s2", [1, 512])

    s = NS()
    s.pe, s.act, s.dve = p.eng_sems
    s.ones128, s.onesr1, s.negr1 = ONES128, ONESR1, NEGR1
    s.stat = STAT
    s.rstdB, s.nmrB = RSTDB, NMRB
    s.ps_s1, s.ps_s2, s.ps_c, s.ps_d = ps_s1, ps_s2, ps_c, ps_d
    s.psfree = {}
    s.sqch_free = None
    s.stat_users = []
    s.tmpf_free = None
    s.eps = EPS

    const_toks = []
    for dst, srcin in [(B1SB, b1in), (B2SB, b2in)]:
        const_toks.append(p.dma("sync", dst[:], srcin))
    for tns, val in [(ONES128, 1.0), (ONESR1, 1.0), (NEGR1, -1.0)]:
        const_toks.append(p.op("vector", lambda e, tns=tns, val=val: e.memset(tns[:], val),
                               inc=s.dve))
    for dst, srcin in [(U1SB, u1), (U2SB, u2), (V1SB, v1), (V2SB, v2)]:
        const_toks.append(p.dma("gpsimd", dst[:], srcin))
    s.cgate = tokmax(const_toks)
    s.eps_tok = p.op("vector", lambda e: e.memset(EPS[:], 1e-5), inc=s.dve)

    store_last = None
    for b in range(B_CORE):
        bsl = slice(b * T, (b + 1) * T)
        bgate = [store_last] if store_last else []

        xln_toks = emit_lnin_streamed(p, s, xt, bsl, SCA, XC, SQCH[:], TMPF, bgate)

        pb_toks = []
        for e2 in range(2):
            for tc in range(NTC):
                tsl = slice(tc * 512, tc * 512 + 512)
                pst = ps_a if tc % 2 == 0 else ps_b
                mm_tok = None
                for cch in range(NCH):
                    w = []
                    if cch == 0:
                        w = s.cgate + [xln_toks[-1]]
                        ft = s.psfree.get(id(pst))
                        if ft is not None:
                            w.append(ft)
                    mm_tok = p.op("tensor", lambda e, cch=cch, e2=e2, tsl=tsl, pst=pst: e.matmul(
                        pst[:], U1SB[:, cch * 256 + e2 * 128: cch * 256 + e2 * 128 + 128],
                        SCA[:, cch * T:(cch + 1) * T][:, tsl],
                        start=(cch == 0), stop=(cch == NCH - 1), skip_group_check=True),
                        waits=w, inc=(s.pe if cch == NCH - 1 else None))
                ev = p.op("scalar", lambda e, e2=e2, tsl=tsl, pst=pst: e.copy(
                    PB[:, e2 * T:(e2 + 1) * T][:, tsl], pst[:]),
                    waits=[mm_tok] + bgate, inc=s.act)
                s.psfree[id(pst)] = ev
                pb_toks.append(ev)
        pb_last = pb_toks[-1]

        qb_toks = []
        hb_dumped = False
        for tc in range(NTC):
            tsl = slice(tc * 512, tc * 512 + 512)
            hb_toks = []
            for f in range(32):
                pst = ps_a if f % 2 == 0 else ps_b
                mm_tok = None
                for k2 in range(2):
                    w = []
                    if k2 == 0:
                        w = [pb_last]
                        ft = s.psfree.get(id(pst))
                        if ft is not None:
                            w.append(ft)
                    mm_tok = p.op("tensor", lambda e, k2=k2, f=f, tsl=tsl, pst=pst: e.matmul(
                        pst[:], V1SB[:, k2 * 4096 + f * 128: k2 * 4096 + f * 128 + 128],
                        PB[:, k2 * T:(k2 + 1) * T][:, tsl],
                        start=(k2 == 0), stop=(k2 == 1), skip_group_check=True),
                        waits=w, inc=(s.pe if k2 == 1 else None))
                vfree = getattr(s, "vb_free", None)
                t_v = p.op("scalar", lambda e, f=f, pst=pst: e.activation(
                    VB[:], pst[:], AF.Identity, bias=B1SB[:, f:f + 1]),
                    waits=[mm_tok] + ([vfree] if vfree else []), inc=s.act)
                t_sq = p.op("scalar", lambda e, f=f, pst=pst: e.activation(
                    SQ[:], pst[:], AF.Square, bias=B1SB[:, f:f + 1]),
                    waits=[mm_tok] + ([getattr(s, "sq_free", None)] if getattr(s, "sq_free", None) else []),
                    inc=s.act)
                s.psfree[id(pst)] = t_sq
                t_u2 = p.op("vector", lambda e: e.tensor_scalar(
                    SQ[:], SQ[:], GA, 1.0, ALU.mult, ALU.add),
                    waits=[t_sq], inc=s.dve)
                t_w = p.op("vector", lambda e: e.tensor_mul(SQ[:], SQ[:], VB[:]),
                           waits=[t_u2, t_v], inc=s.dve)
                t_t = p.op("scalar", lambda e: e.activation(SQ[:], SQ[:], AF.Tanh, scale=GS),
                           waits=[t_w], inc=s.act)
                t_u3 = p.op("vector", lambda e: e.tensor_scalar(
                    SQ[:], SQ[:], 0.5, 0.5, ALU.mult, ALU.add),
                    waits=[t_t], inc=s.dve)
                t_h = p.op("vector", lambda e, f=f: e.tensor_mul(
                    HB[:, f * 512:(f + 1) * 512], SQ[:], VB[:]),
                    waits=[t_u3] + ([s.psfree.get(("hb", f))] if s.psfree.get(("hb", f)) else []),
                    inc=s.dve)
                s.vb_free = t_h
                s.sq_free = t_h
                hb_toks.append(t_h)
            for outr in range(2):
                pst = ps_c
                mm_tok = None
                for f in range(32):
                    w = [hb_toks[f]]
                    if f == 0:
                        ft = s.psfree.get(id(s.ps_c))
                        if ft is not None:
                            w.append(ft)
                    mm_tok = p.op("tensor", lambda e, f=f, outr=outr, pst=pst: e.matmul(
                        pst[:], U2SB[:, f * 256 + outr * 128: f * 256 + outr * 128 + 128],
                        HB[:, f * 512:(f + 1) * 512],
                        start=(f == 0), stop=(f == 31), skip_group_check=True),
                        waits=w, inc=(s.pe if f == 31 else None))
                ev = p.op("scalar", lambda e, outr=outr, tsl=tsl, pst=pst: e.copy(
                    QB[:, outr * T:(outr + 1) * T][:, tsl], pst[:]),
                    waits=[mm_tok], inc=s.act)
                s.psfree[id(s.ps_c)] = ev
                qb_toks.append(ev)
                for f in range(32):
                    s.psfree[("hb", f)] = mm_tok
        qb_last = qb_toks[-1]

        yb_toks = {}
        for outc in range(NCH):
            evs = []
            for tc in range(NTC):
                tsl = slice(tc * 512, tc * 512 + 512)
                pst = ps_a if tc % 2 == 0 else ps_b
                mm_tok = None
                for r2 in range(2):
                    w = []
                    if r2 == 0:
                        w = [qb_last]
                        ft = s.psfree.get(id(pst))
                        if ft is not None:
                            w.append(ft)
                    mm_tok = p.op("tensor", lambda e, r2=r2, outc=outc, tsl=tsl, pst=pst: e.matmul(
                        pst[:], V2SB[:, r2 * C + outc * 128: r2 * C + outc * 128 + 128],
                        QB[:, r2 * T:(r2 + 1) * T][:, tsl],
                        start=(r2 == 0), stop=(r2 == 1), skip_group_check=True),
                        waits=w, inc=(s.pe if r2 == 1 else None))
                ev = p.op("scalar", lambda e, outc=outc, tsl=tsl, pst=pst: e.activation(
                    SCA[:, outc * T:(outc + 1) * T][:, tsl], pst[:], AF.Identity,
                    bias=B2SB[:, outc:outc + 1]),
                    waits=[mm_tok], inc=s.act)
                s.psfree[id(pst)] = ev
                evs.append(ev)
            yb_toks[outc] = evs

        st_toks = emit_lnout_add_store(p, s, SCA, yb_toks, XC[:, 0:T],
                                       xt, xout, bsl, XS, TMPF)
        store_last = st_toks[-1]

    return store_last


def build_stage_d():
    nc = bass.Bass("TRN2", target_bir_lowering=False)
    xt = nc.dram_tensor("xt", [NCH, 128, B_CORE * T], F32, kind="ExternalInput").ap()
    u1 = nc.dram_tensor("u1", [128, NCH * 256], F32, kind="ExternalInput").ap()
    v1 = nc.dram_tensor("v1", [128, 2 * 4096], F32, kind="ExternalInput").ap()
    b1in = nc.dram_tensor("b1in", [128, 32], F32, kind="ExternalInput").ap()
    u2 = nc.dram_tensor("u2", [128, 32 * 256], F32, kind="ExternalInput").ap()
    v2 = nc.dram_tensor("v2", [128, 2 * C], F32, kind="ExternalInput").ap()
    b2in = nc.dram_tensor("b2in", [128, 8], F32, kind="ExternalInput").ap()
    xout = nc.dram_tensor("xout", [NCH, 128, B_CORE * T], F32, kind="ExternalOutput").ap()


    with contextlib.ExitStack() as ctx:
        p = Prog(nc, ctx)
        p.dma_sem = {"sync": p.sem("s_dsync"), "gpsimd": p.sem("s_dgp")}
        p.eng_sems = (p.sem("s_pe"), p.sem("s_act"), p.sem("s_dve"))
        with contextlib.ExitStack() as c1:
            store_last = _emit_stage_d(nc, c1, p, xt, u1, v1, b1in, u2, v2, b2in, xout)
        p.op("sync", lambda e: e.nop(), waits=[store_last])
        with nc.Block() as block:
            p.emit(block)
    return nc

def run_stage_d(x_full, U1, V1, b1, U2, V2, b2):
    nc = build_stage_d()
    consts = _consts_common()
    b1h = np.ascontiguousarray(np.asarray(b1, dtype=np.float32).reshape(32, 128).T)
    b2h = np.ascontiguousarray(np.asarray(b2, dtype=np.float32).reshape(8, 128).T)
    in_maps = []
    for core in range(N_CORES):
        xc = np.asarray(x_full[core * B_CORE:(core + 1) * B_CORE], dtype=np.float32)
        im = {"xt": _to_xt(xc), "u1": _chunked(U1), "v1": _chunked(V1),
              "b1in": b1h, "u2": _chunked(U2), "v2": _chunked(V2), "b2in": b2h}
        in_maps.append(im)
    res = run_bass_kernel_spmd(nc, in_maps, core_ids=list(range(N_CORES)))
    outs = [_from_xt(r["xout"]) for r in res.results]
    return np.concatenate(outs, axis=0)




def _emit_stage_c(nc, ctx, p, xt, waq, wak, wav, wproj, lvin, cos1, sin1, perm_in, masks_in, xout):
    SM = 1.0 / math.sqrt(32)
    LAMBDA_INIT = 0.2
    sb = lambda name, shape, dt: ctx.enter_context(nc.sbuf_tensor('c_' + name, shape, dt))
    psumt = lambda name, shape, dt=F32: ctx.enter_context(nc.psum_tensor('c_' + name, shape, dt))

    SCA = sb("SCA", [128, NCH * T], BF16)
    SCB = sb("SCB", [128, NCH * T], BF16)
    V65 = sb("V65", [128, 16 * 1040], BF16)
    QR2a = sb("QR2a", [128, T], BF16)
    QR2b = sb("QR2b", [128, T], BF16)
    KR2a = sb("KR2a", [128, T], BF16)
    KR2b = sb("KR2b", [128, T], BF16)
    QH = sb("QH", [33, T], BF16)
    KH = sb("KH", [33, T], BF16)
    PT = sb("PT", [128, 2 * 512], BF16)
    OT2 = sb("OT2", [64, 512], BF16)
    A1N = sb("A1N", [64, T], BF16)
    A2T = sb("A2T", [64, 512], BF16)
    DIFU = sb("DIFU", [64, T], BF16)
    SQD = sb("SQD", [64, T], BF16)
    OTN = A2T
    XC = sb("XC", [128, 2 * T], BF16)
    XS = sb("XS", [128, T], F32)
    WVH = sb("WVH", [128, NCH * 512], BF16)
    WSM = sb("WSM", [128, 1 * C], BF16)
    COS = sb("COS", [128, T], BF16)
    SIN = sb("SIN", [128, T], BF16)
    PERM = sb("PERM", [128, 128], BF16)
    ONES128 = sb("ONES128", [128, 1], BF16)
    ONES32 = sb("ONES32", [32, 1], BF16)
    ONES32F = sb("ONES32F", [32, 1], F32)
    ONES64B = sb("ONES64B", [64, 1], BF16)
    ONESR1 = sb("ONESR1", [1, 128], F32)
    NEGR1 = sb("NEGR1", [1, 128], F32)
    ONESR64 = sb("ONESR64", [1, 64], F32)
    NEG08R64 = sb("NEG08R64", [1, 64], F32)

    LV = sb("LV", [128, 4], F32)
    LAMC = sb("LAMC", [64, 1], F32)
    MASKS = sb("MASKS", [128, 896], BF16)
    STAT = sb("STAT", [1, STAT_W], F32)
    EPS = sb("EPS", [1, 8], F32)
    RSTDB = sb("RSTDB", [128, T], F32)
    NMRB = sb("NMRB", [128, T], BF16)
    TMPF = sb("TMPF", [128, 512], F32)
    TMPG = XS

    ps_a = psumt("ps_a", [128, 512])
    ps_b = psumt("ps_b", [128, 512])
    ps_c = psumt("ps_c", [128, 512])
    ps_d = psumt("ps_d", [128, 512])
    ps_o = psumt("ps_o", [65, 512])
    ps_r = psumt("ps_r", [64, 512])
    ps_s1 = psumt("ps_s1", [1, 512])
    ps_s2 = psumt("ps_s2", [1, 512])

    s = NS()
    s.pe, s.act, s.dve = p.eng_sems

    s.ones128, s.onesr1, s.negr1 = ONES128, ONESR1, NEGR1
    s.stat = STAT
    s.rstdB, s.nmrB = RSTDB, NMRB
    s.ps_s1, s.ps_s2, s.ps_c, s.ps_d = ps_s1, ps_s2, ps_c, ps_d
    s.psfree = {}
    s.sqch_free = None
    s.stat_users = []
    s.tmpf_free = s.tmpg_free = None
    s.eps = EPS

    const_toks = []
    for dst, srcap in [(COS, cos1), (SIN, sin1), (PERM, perm_in),
                       (MASKS, masks_in), (LV, lvin)]:
        const_toks.append(p.dma("sync", dst[:], srcap))
    for tns, val in [(ONES128, 1.0), (ONES32, 1.0), (ONESR1, 1.0),
                     (NEGR1, -1.0), (ONESR64, 1.0), (NEG08R64, -0.8),
                     (ONES64B, 1.0), (ONES32F, 1.0)]:
        const_toks.append(p.op("vector", lambda e, tns=tns, val=val: e.memset(tns[:], val),
                               inc=s.dve))
    s.cgate = tokmax(const_toks)
    s.eps_tok = p.op("vector", lambda e: e.memset(EPS[:], 1e-5), inc=s.dve)
    khm1 = p.op("vector", lambda e: e.memset(KH[32:33, :], -1.0), inc=s.dve)
    ones_v = p.op("vector", lambda e: e.memset(
        V65.ap().rearrange("p (tt h o) -> p tt h o", tt=16, h=16)[:, :, :, 64:65],
        1.0), inc=s.dve)

    lmul = p.op("vector", lambda e: e.tensor_mul(LV[:, 0:1], LV[:, 0:1], LV[:, 1:2]),
                waits=s.cgate, inc=s.dve)
    lmul2 = p.op("vector", lambda e: e.tensor_mul(LV[:, 2:3], LV[:, 2:3], LV[:, 3:4]),
                 waits=s.cgate, inc=s.dve)
    d1 = p.op("tensor", lambda e: e.matmul(ps_s1[:, 0:1], ONES32F[:], LV[0:32, 0:1],
                                           start=True, stop=True, skip_group_check=True),
              waits=[lmul] + s.cgate, inc=s.pe)
    d1c = p.op("scalar", lambda e: e.activation(STAT[0:1, 0:1], ps_s1[:, 0:1], AF.Exp),
               waits=[d1], inc=s.act)
    d2 = p.op("tensor", lambda e: e.matmul(ps_s2[:, 0:1], ONES32F[:], LV[0:32, 2:3],
                                           start=True, stop=True, skip_group_check=True),
              waits=[lmul2] + s.cgate, inc=s.pe)
    d2c = p.op("scalar", lambda e: e.activation(STAT[0:1, 1:2], ps_s2[:, 0:1], AF.Exp),
               waits=[d2], inc=s.act)
    lsub = p.op("vector", lambda e: e.tensor_sub(STAT[0:1, 0:1], STAT[0:1, 0:1], STAT[0:1, 1:2]),
                waits=[d1c, d2c], inc=s.dve)
    ladd = p.op("vector", lambda e: e.tensor_scalar_add(STAT[0:1, 0:1], STAT[0:1, 0:1],
                                                        LAMBDA_INIT),
                waits=[lsub], inc=s.dve)
    lbc = p.op("tensor", lambda e: e.matmul(ps_r[:, 0:1], ONESR64[:], STAT[0:1, 0:1],
                                            start=True, stop=True, skip_group_check=True),
              waits=[ladd], inc=s.pe)
    lamc = p.op("scalar", lambda e: e.copy(LAMC[:], ps_r[:, 0:1]), waits=[lbc], inc=s.act)
    s.psfree[id(s.ps_s1)] = d1c
    s.psfree[id(s.ps_s2)] = d2c
    s.ps_r_free = lamc
    s.stat_users += [lbc, ladd]

    BPAD = 1.02
    store_last = None
    for b in range(B_CORE):
        bsl = slice(b * T, (b + 1) * T)
        bgate = [store_last] if store_last else []

        xln_toks = emit_lnin_streamed(p, s, xt, bsl, SCA, XC, KR2a[:], TMPF, bgate)
        xln_last = xln_toks[-1]

        for half in range(2):
            wv_tok = p.dma("gpsimd", WVH[:], wav[half],
                           waits=bgate + ([s.psfree.get("wvh")] if s.psfree.get("wvh") else []))
            for tt in range(16):
                pst = ps_a if tt % 2 == 0 else ps_b
                mm_tok = None
                for cch in range(NCH):
                    w = []
                    if cch == 0:
                        w = [wv_tok, xln_last, ones_v]
                        ft = s.psfree.get(id(pst))
                        if ft is not None:
                            w.append(ft)
                    mm_tok = p.op("tensor", lambda e, cch=cch, tt=tt, half=half, pst=pst: e.matmul(
                        pst[:], SCA[:, cch * T + tt * 128: cch * T + tt * 128 + 128],
                        WVH[:, cch * 512:(cch + 1) * 512],
                        start=(cch == 0), stop=(cch == NCH - 1), skip_group_check=True),
                        waits=w, inc=(s.pe if cch == NCH - 1 else None))

                def evac_v(e, tt=tt, half=half, pst=pst):
                    src_ap = pst.ap().rearrange("p (h d) -> p h d", h=8)
                    dst = V65.ap().rearrange("p (t2 h o) -> p t2 h o", t2=16, h=16)[
                        :, tt, half * 8:(half + 1) * 8, 0:64]
                    return e.activation(dst, src_ap, AF.Copy)
                ev = p.op("scalar", evac_v, waits=[mm_tok], inc=s.act)
                s.psfree[id(pst)] = ev
                vlast = ev
            s.psfree["wvh"] = vlast

        o_toks = []
        prerope = XC[:, T:2 * T]
        for sc in range(4):
            pr_done = {}
            for (buf, which, cc) in ((QR2a, "q", sc), (KR2a, "k", sc),
                                     (QR2b, "q", sc + 4), (KR2b, "k", sc + 4)):
                wdram = waq if which == "q" else wak
                slot = 0
                wtok = p.dma("gpsimd", WSM[:, slot * C:(slot + 1) * C], wdram[cc],
                             waits=[s.psfree.get(("wsm", slot))] if s.psfree.get(("wsm", slot)) else [])
                banks = [ps_a, ps_b, ps_c, ps_d]
                mm_last = {}
                for tc in range(NTC):
                    for cch in range(NCH):
                        pst = banks[tc]
                        w = []
                        if cch == 0:
                            w = [wtok, xln_last]
                            ft = s.psfree.get(id(pst))
                            if ft is not None:
                                w.append(ft)
                        mm_last[tc] = p.op("tensor", lambda e, cch=cch, tc=tc, pst=pst, slot=slot: e.matmul(
                            pst[:], WSM[:, slot * C + cch * 128: slot * C + cch * 128 + 128],
                            SCA[:, cch * T + tc * 512: cch * T + tc * 512 + 512],
                            start=(cch == 0), stop=(cch == NCH - 1), skip_group_check=True),
                            waits=w, inc=(s.pe if cch == NCH - 1 else None))
                rope_toks = []
                for tc in range(NTC):
                    tsl = slice(tc * 512, tc * 512 + 512)
                    pst = banks[tc]
                    ev = p.op("scalar", lambda e, tsl=tsl, pst=pst: e.copy(prerope[:, tsl], pst[:]),
                              waits=[mm_last[tc]] + ([s.prerope_free] if getattr(s, "prerope_free", None) else []),
                              inc=s.act)
                    s.psfree[id(pst)] = ev
                    rot = p.op("tensor", lambda e, tsl=tsl, pst=pst: e.matmul(
                        pst[:], PERM[:], prerope[:, tsl], start=True, stop=True,
                        skip_group_check=True),
                        waits=[ev] + s.cgate, inc=s.pe)
                    t1 = p.op("vector", lambda e, tsl=tsl: e.tensor_mul(
                        TMPF[:], prerope[:, tsl], COS[:, tsl]),
                        waits=[ev] + s.cgate + ([s.tmpf_free] if s.tmpf_free else []), inc=s.dve)
                    t2 = p.op("vector", lambda e, tsl=tsl, pst=pst: e.tensor_mul(
                        TMPG[:, 0:512], pst[:], SIN[:, tsl]),
                        waits=[rot] + ([s.tmpg_free] if s.tmpg_free else []),
                        inc=s.dve)
                    s.psfree[id(pst)] = t2
                    t3 = p.op("vector", lambda e, tsl=tsl, buf=buf: e.tensor_add(
                        buf[:, tsl], TMPF[:], TMPG[:, 0:512]), waits=[t1, t2], inc=s.dve)
                    s.tmpf_free = t3
                    s.tmpg_free = t3
                    rope_toks.append(t3)
                s.psfree[("wsm", slot)] = rope_toks[-1]
                s.prerope_free = rope_toks[-1]
                pr_done[id(buf)] = rope_toks

            for m in range(4):
                pair_res = {}
                for (pi, qbuf, kbuf) in ((0, QR2a, KR2a), (1, QR2b, KR2b)):
                    j = 4 * sc + m + 16 * pi
                    def mkmove(dstH, srcbuf, m=m, stg_extra=None):
                        base = 32 * m
                        prev_o = tokmax(o_toks)
                        if m < 3:
                            return p.op("scalar", lambda e, dstH=dstH, srcbuf=srcbuf, base=base: e.copy(
                                dstH[0:32, :], srcbuf[base:base + 32, :]),
                                waits=pr_done[id(srcbuf)] + prev_o + [khm1], inc=s.act)
                        hop = p.op("scalar", lambda e, srcbuf=srcbuf: e.copy(
                            SQD[0:64, :], srcbuf[64:128, :]),
                            waits=pr_done[id(srcbuf)] + prev_o +
                            ([s.sqd_free] if getattr(s, "sqd_free", None) else []),
                            inc=s.act)
                        mv = p.op("scalar", lambda e, dstH=dstH: e.copy(
                            dstH[0:32, :], SQD[32:64, :]), waits=[hop, khm1], inc=s.act)
                        s.sqd_free = mv
                        return mv
                    qmove = mkmove(QH, qbuf)
                    kmove = mkmove(KH, kbuf)
                    sqh = SQD[0:32, :]
                    ksq = p.op("scalar", lambda e, sqh=sqh: e.activation(
                        sqh, KH[0:32, :], AF.Square),
                        waits=[kmove, qmove] + ([s.sqd_free] if getattr(s, "sqd_free", None) else []) +
                        ([getattr(s, "prev_qn", None)] if getattr(s, "prev_qn", None) else []),
                        inc=s.act)
                    kn_toks = []
                    for tc in range(NTC):
                        mm = p.op("tensor", lambda e, tc=tc, sqh=sqh: e.matmul(
                            ps_s1[:], ONES32[:], sqh[:, tc * 512:(tc + 1) * 512],
                            start=True, stop=True, skip_group_check=True),
                            waits=[ksq] + s.cgate + ([s.psfree[id(s.ps_s1)]] if s.psfree.get(id(s.ps_s1)) else []),
                            inc=s.pe)
                        kn = p.op("scalar", lambda e, tc=tc: e.activation(
                            STAT[0:1, SOFF_SUM + tc * 512:SOFF_SUM + tc * 512 + 512],
                            ps_s1[:], AF.Sqrt), waits=[mm], inc=s.act)
                        s.psfree[id(s.ps_s1)] = kn
                        kn_toks.append(kn)
                    maxk = p.op("vector", lambda e: e.reduce_max(
                        STAT[0:1, SOFF_MAXK:SOFF_MAXK + 1],
                        STAT[0:1, SOFF_SUM:SOFF_SUM + T], axis=AX.X),
                        waits=kn_toks, inc=s.dve)
                    qsq = p.op("scalar", lambda e, sqh=sqh: e.activation(
                        sqh, QH[0:32, :], AF.Square),
                        waits=kn_toks + [ksq, qmove], inc=s.act)
                    qn_toks = []
                    for tc in range(NTC):
                        mm = p.op("tensor", lambda e, tc=tc, sqh=sqh: e.matmul(
                            ps_s1[:], ONES32[:], sqh[:, tc * 512:(tc + 1) * 512],
                            start=True, stop=True, skip_group_check=True),
                            waits=[qsq, maxk] + s.cgate + ([s.psfree[id(s.ps_s1)]] if s.psfree.get(id(s.ps_s1)) else []),
                            inc=s.pe)
                        qn = p.op("scalar", lambda e, tc=tc: e.activation(
                            STAT[0:1, SOFF_SUM + tc * 512:SOFF_SUM + tc * 512 + 512],
                            ps_s1[:], AF.Sqrt), waits=[mm], inc=s.act)
                        s.psfree[id(s.ps_s1)] = qn
                        qn_toks.append(qn)
                    s.prev_qn = qn_toks[-1]
                    s.sqd_free = qn_toks[-1]
                    bq = p.op("vector", lambda e: e.tensor_scalar(
                        STAT[0:1, SOFF_SQ:SOFF_SQ + T],
                        STAT[0:1, SOFF_SUM:SOFF_SUM + T],
                        STAT[0:1, SOFF_MAXK:SOFF_MAXK + 1], BPAD,
                        ALU.mult, ALU.mult),
                        waits=qn_toks + [maxk], inc=s.dve)
                    brow = p.op("scalar", lambda e: e.copy(
                        QH[32:33, :], STAT[0:1, SOFF_SQ:SOFF_SQ + T]),
                        waits=[bq, qmove], inc=s.act)
                    s.stat_users += [bq, brow, maxk]

                    pair_out = []
                    for qch in range(4):
                        nkt = 4 * qch + 4
                        pv_tok = None
                        ex_toks = {}

                        def emit_st(kt, qch=qch):
                            pst = ps_a if kt % 2 == 0 else ps_b
                            ptslot = kt % 2
                            stw = [brow, kmove]
                            ft = s.psfree.get(id(pst))
                            if ft is not None:
                                stw.append(ft)
                            st = p.op("tensor", lambda e, kt=kt, qch=qch, pst=pst: e.matmul(
                                pst[:], KH[:, kt * 128:(kt + 1) * 128],
                                QH[:, qch * 512:(qch + 1) * 512],
                                start=True, stop=True, skip_group_check=True),
                                waits=stw, inc=s.pe)
                            expw = [st]
                            ptfree = s.psfree.get(("pt", ptslot))
                            if ptfree is not None:
                                expw.append(ptfree)
                            ex = p.op("scalar", lambda e, ptslot=ptslot, pst=pst: e.activation(
                                PT[:, ptslot * 512:(ptslot + 1) * 512], pst[:], AF.Exp,
                                scale=SM),
                                waits=expw, inc=s.act)
                            s.psfree[id(pst)] = ex
                            if kt >= 4 * qch:
                                jj = kt - 4 * qch
                                ex = p.op("vector", lambda e, ptslot=ptslot, jj=jj: e.tensor_mul(
                                    PT[:, ptslot * 512:(ptslot + 1) * 512],
                                    PT[:, ptslot * 512:(ptslot + 1) * 512],
                                    MASKS[:, (3 - jj) * 128:(3 - jj) * 128 + 512]),
                                    waits=[ex] + s.cgate, inc=s.dve)
                            ex_toks[kt] = ex

                        def emit_pv(kt):
                            nonlocal pv_tok
                            ptslot = kt % 2
                            pvw = [ex_toks[kt]]
                            if kt == 0:
                                ft = s.psfree.get(id(ps_o))
                                if ft is not None:
                                    pvw.append(ft)
                                pvw.append(vlast)
                                pvw.append(ones_v)
                            vh = j % 16
                            pv_tok = p.op("tensor", lambda e, kt=kt, vh=vh, ptslot=ptslot: e.matmul(
                                ps_o[:], V65[:, kt * 1040 + vh * 65: kt * 1040 + vh * 65 + 65],
                                PT[:, ptslot * 512:(ptslot + 1) * 512],
                                start=(kt == 0), stop=(kt == nkt - 1), skip_group_check=True),
                                waits=pvw, inc=(s.pe if kt == nkt - 1 else None))
                            s.psfree[("pt", ptslot)] = pv_tok

                        emit_st(0)
                        for kt in range(1, nkt):
                            emit_st(kt)
                            emit_pv(kt - 1)
                        emit_pv(nkt - 1)
                        rec = p.op("vector", lambda e: e.reciprocal(
                            STAT[0:1, SOFF_REC:SOFF_REC + 512], ps_o[64:65, :]),
                            waits=[pv_tok] + ([s.rec_free] if getattr(s, "rec_free", None) else []),
                            inc=s.dve)
                        otmp = p.op("scalar", lambda e: e.copy(OT2[:], ps_o[0:64, :]),
                                    waits=[pv_tok] + ([s.ot2_free] if getattr(s, "ot2_free", None) else []),
                                    inc=s.act)
                        s.psfree[id(ps_o)] = otmp
                        rb = p.op("tensor", lambda e: e.matmul(
                            ps_r[:], ONESR64[:], STAT[0:1, SOFF_REC:SOFF_REC + 512],
                            start=True, stop=True, skip_group_check=True),
                            waits=[rec] + s.cgate + ([s.ps_r_free] if getattr(s, "ps_r_free", None) else []),
                            inc=s.pe)
                        s.rec_free = rb
                        s.mu2_free = rb
                        if pi == 0:
                            odn = p.op("vector", lambda e, qch=qch: e.tensor_mul(
                                A1N[:, qch * 512:(qch + 1) * 512], OT2[:], ps_r[:]),
                                waits=[rb, otmp] + ([s.psfree.get("a1n")] if qch == 0 and s.psfree.get("a1n") else []),
                                inc=s.dve)
                        else:
                            odn = p.op("vector", lambda e: e.tensor_mul(
                                A2T[:], OT2[:], ps_r[:]),
                                waits=[rb, otmp] + ([s.psfree.get("a2t")] if s.psfree.get("a2t") else []),
                                inc=s.dve)
                            odn = p.op("vector", lambda e, qch=qch: e.scalar_tensor_tensor(
                                DIFU[:, qch * 512:(qch + 1) * 512], A2T[:], LAMC[:],
                                A1N[:, qch * 512:(qch + 1) * 512],
                                ALU.mult, ALU.subtract),
                                waits=[odn, lamc] + ([s.psfree.get("difu")] if qch == 0 and s.psfree.get("difu") else []),
                                inc=s.dve)
                            s.psfree["a2t"] = odn
                        s.ps_r_free = odn
                        s.ot2_free = odn
                        s.stat_users += [rb]
                        pair_out.append(odn)
                    pair_res[pi] = pair_out
                s.psfree["a1n"] = pair_res[1][-1]

                h = 4 * sc + m
                dq = p.op("scalar", lambda e: e.activation(SQD[:], DIFU[:], AF.Square),
                          waits=pair_res[1] + ([s.sqd_free] if getattr(s, "sqd_free", None) else []),
                          inc=s.act)
                ms_toks = []
                for tc in range(NTC):
                    mm = p.op("tensor", lambda e, tc=tc: e.matmul(
                        ps_s2[:], ONES64B[:], SQD[:, tc * 512:(tc + 1) * 512],
                        start=True, stop=True, skip_group_check=True),
                        waits=[dq] + s.cgate + ([s.psfree[id(s.ps_s2)]] if s.psfree.get(id(s.ps_s2)) else []),
                        inc=s.pe)
                    ln1 = p.op("scalar", lambda e, tc=tc: e.activation(
                        STAT[0:1, SOFF_SUM + tc * 512:SOFF_SUM + tc * 512 + 512],
                        ps_s2[:], AF.Ln, scale=1.0 / 64, bias=s.eps[0:1, 0:1]),
                        waits=[mm, s.eps_tok], inc=s.act)
                    s.psfree[id(s.ps_s2)] = ln1
                    rs1 = p.op("scalar", lambda e, tc=tc: e.activation(
                        STAT[0:1, SOFF_SUM + tc * 512:SOFF_SUM + tc * 512 + 512],
                        STAT[0:1, SOFF_SUM + tc * 512:SOFF_SUM + tc * 512 + 512],
                        AF.Exp, scale=-0.5), waits=[ln1], inc=s.act)
                    rbb = p.op("tensor", lambda e, tc=tc: e.matmul(
                        ps_r[:], NEG08R64[:],
                        STAT[0:1, SOFF_SUM + tc * 512:SOFF_SUM + tc * 512 + 512],
                        start=True, stop=True, skip_group_check=True),
                        waits=[rs1] + s.cgate + ([s.ps_r_free] if s.ps_r_free else []),
                        inc=s.pe)
                    dn = p.op("vector", lambda e, tc=tc: e.tensor_mul(
                        OTN[:], DIFU[:, tc * 512:(tc + 1) * 512], ps_r[:]),
                        waits=[rbb] + ([s.psfree.get("a2t")] if s.psfree.get("a2t") else []),
                        inc=s.dve)
                    s.ps_r_free = dn
                    oc = p.op("scalar", lambda e, h=h, tc=tc: e.copy(
                        SCB[:, (h // 2) * T + tc * 512:(h // 2) * T + tc * 512 + 512][
                            64 * (h % 2):64 * (h % 2) + 64, :], OTN[:]),
                        waits=[dn], inc=s.act)
                    s.psfree["a2t"] = oc
                    s.stat_users += [rbb]
                    ms_toks.append(oc)
                    o_toks.append(oc)
                s.sqd_free = ms_toks[-1]
                s.psfree["difu"] = ms_toks[-1]
        o_last = tokmax(o_toks)

        yb_toks = {}
        for outc in range(NCH):
            slot = 0
            wtok = p.dma("gpsimd", WSM[:, slot * C:(slot + 1) * C], wproj[outc],
                         waits=[s.psfree.get(("wsm", slot))] if s.psfree.get(("wsm", slot)) else [])
            banks = [ps_a, ps_b, ps_c, ps_d]
            mm_last = {}
            for tc in range(NTC):
                for cch in range(NCH):
                    pst = banks[tc]
                    w = []
                    if cch == 0:
                        w = [wtok] + o_last
                        ft = s.psfree.get(id(pst))
                        if ft is not None:
                            w.append(ft)
                    mm_last[tc] = p.op("tensor", lambda e, cch=cch, tc=tc, pst=pst, slot=slot: e.matmul(
                        pst[:], WSM[:, slot * C + cch * 128: slot * C + cch * 128 + 128],
                        SCB[:, cch * T + tc * 512: cch * T + tc * 512 + 512],
                        start=(cch == 0), stop=(cch == NCH - 1), skip_group_check=True),
                        waits=w, inc=(s.pe if cch == NCH - 1 else None))
            evs = []
            for tc in range(NTC):
                tsl = slice(tc * 512, tc * 512 + 512)
                pst = banks[tc]
                ev = p.op("scalar", lambda e, outc=outc, tsl=tsl, pst=pst: e.copy(
                    SCA[:, outc * T:(outc + 1) * T][:, tsl], pst[:]),
                    waits=[mm_last[tc]], inc=s.act)
                s.psfree[id(pst)] = ev
                evs.append(ev)
            s.psfree[("wsm", slot)] = evs[-1]
            yb_toks[outc] = evs

        s.xs_gate = s.tmpg_free
        st_toks = emit_lnout_add_store(p, s, SCA, yb_toks, XC[:, 0:T],
                                       xt, xout, bsl, XS, TMPF)
        store_last = st_toks[-1]

    return store_last


def build_stage_c():
    nc = bass.Bass("TRN2", target_bir_lowering=False)
    xt = nc.dram_tensor("xt", [NCH, 128, B_CORE * T], F32, kind="ExternalInput").ap()
    waq = nc.dram_tensor("waq", [8, 128, NCH * 128], F32, kind="ExternalInput").ap()
    wak = nc.dram_tensor("wak", [8, 128, NCH * 128], F32, kind="ExternalInput").ap()
    wav = nc.dram_tensor("wav", [2, 128, NCH * 512], F32, kind="ExternalInput").ap()
    wproj = nc.dram_tensor("wproj", [8, 128, NCH * 128], F32, kind="ExternalInput").ap()
    lvin = nc.dram_tensor("lvin", [128, 4], F32, kind="ExternalInput").ap()
    cos1 = nc.dram_tensor("cos1", [128, T], BF16, kind="ExternalInput").ap()
    sin1 = nc.dram_tensor("sin1", [128, T], BF16, kind="ExternalInput").ap()
    perm_in = nc.dram_tensor("perm32", [128, 128], BF16, kind="ExternalInput").ap()
    masks_in = nc.dram_tensor("masks", [128, 896], BF16, kind="ExternalInput").ap()
    xout = nc.dram_tensor("xout", [NCH, 128, B_CORE * T], F32, kind="ExternalOutput").ap()

    SM = 1.0 / math.sqrt(32)
    LAMBDA_INIT = 0.2

    with contextlib.ExitStack() as ctx:
        p = Prog(nc, ctx)
        p.dma_sem = {"sync": p.sem("s_dsync"), "gpsimd": p.sem("s_dgp")}
        p.eng_sems = (p.sem("s_pe"), p.sem("s_act"), p.sem("s_dve"))
        with contextlib.ExitStack() as c1:
            store_last = _emit_stage_c(nc, c1, p, xt, waq, wak, wav, wproj, lvin, cos1, sin1, perm_in, masks_in, xout)
        p.op("sync", lambda e: e.nop(), waits=[store_last])
        with nc.Block() as block:
            p.emit(block)
    return nc

def _rope_tables_l1():
    theta = 10000.0 ** (-(np.arange(16, dtype=np.float64) * 2.0 / 32))
    t = np.arange(T, dtype=np.float64)
    ang = np.outer(t, theta)
    blk_c = np.concatenate([np.cos(ang), np.cos(ang)], 1).T
    blk_s = np.concatenate([np.sin(ang), np.sin(ang)], 1).T
    cos = np.concatenate([blk_c] * 4, 0).astype(np.float32)
    sin = np.concatenate([blk_s] * 4, 0).astype(np.float32)
    return cos.astype(BF), sin.astype(BF)


def _deinterleave_wa(W):
    W = np.asarray(W, dtype=np.float32)
    idx = np.concatenate([np.arange(0, 32, 2), np.arange(1, 32, 2)])
    out = W.copy()
    for part in range(2):
        base = part * C
        for j in range(32):
            cols = base + 32 * j + np.arange(32)
            out[:, cols] = W[:, base + 32 * j + idx]
    return np.ascontiguousarray(out)


def run_stage_c(x_full, W_attn, W_proj, lq1, lk1, lq2, lk2):
    nc = build_stage_c()
    consts = _consts_common()
    cos, sin = _rope_tables_l1()
    wa = _deinterleave_wa(W_attn)
    waq = _slot8(wa[:, 0:C])
    wak = _slot8(wa[:, C:2 * C])
    wav = np.ascontiguousarray(np.stack(
        [_chunked(wa[:, 2 * C + h * 512:2 * C + (h + 1) * 512]) for h in range(2)]))
    wp = _slot8(np.asarray(W_proj, dtype=np.float32))
    lv = np.zeros((128, 4), dtype=np.float32)
    lv[0:32, 0] = np.asarray(lq1); lv[0:32, 1] = np.asarray(lk1)
    lv[0:32, 2] = np.asarray(lq2); lv[0:32, 3] = np.asarray(lk2)
    perm = _perm_rotate_half(32)
    in_maps = []
    for core in range(N_CORES):
        xc = np.asarray(x_full[core * B_CORE:(core + 1) * B_CORE], dtype=np.float32)
        im = {"xt": _to_xt(xc), "waq": waq, "wak": wak, "wav": wav,
              "wproj": wp, "lvin": lv,
              "cos1": cos, "sin1": sin, "perm32": perm, "masks": consts["masks"]}
        in_maps.append(im)
    res = run_bass_kernel_spmd(nc, in_maps, core_ids=list(range(N_CORES)))
    outs = [_from_xt(r["xout"]) for r in res.results]
    return np.concatenate(outs, axis=0)




TG = T + 1


def _emit_stage_b(nc, ctx, p, xt, wqk, wob, w1p, w2p, b1p, b2p, qsp, ksp, lngp, lnbp, cosS_in, sinS_in, cosI_in, sinI_in, perm_in, idn_in, maska_in, mask0_in, xout, n_sub=6):
    sb = lambda name, shape, dt: ctx.enter_context(nc.sbuf_tensor('b_' + name, shape, dt))

    XP = sb("XP", [128, NCH * TG], BF16)
    SCB = sb("SCB", [128, NCH * T], BF16)
    VT2 = sb("VT2", [128, 16 * 128], BF16)
    VSTG = sb("VSTG", [128, 2 * 512], BF16)
    QR2 = sb("QR2", [128, T], BF16)
    KR2 = sb("KR2", [128, T], BF16)
    XC = sb("XC", [128, 2 * TG], BF16)
    PRAW = None
    SQP = sb("SQP", [128, T], BF16)
    RNB = sb("RNB", [128, T], BF16)
    PQ = sb("PQ", [128, 2 * 384], BF16)
    PTS = sb("PTS", [128, 2 * 384], BF16)
    HB = sb("HB", [128, 22 * 512], BF16)
    W12S = sb("W12S", [128, 1 * NCH * 256], BF16)
    W2S = sb("W2S", [128, 1 * 22 * 128], BF16)
    WQKS = sb("WQKS", [128, 1 * NCH * 384], BF16)
    WOS = sb("WOS", [128, 1 * NCH * 128], BF16)
    COSS = sb("COSS", [128, T], BF16)
    SINS = sb("SINS", [128, T], BF16)
    COSI = sb("COSI", [128, T], BF16)
    SINI = sb("SINI", [128, T], BF16)
    PERM = sb("PERM", [128, 128], BF16)
    IDN = sb("IDN", [128, 128], BF16)
    MASKA = sb("MASKA", [128, 384], BF16)
    MASK0 = sb("MASK0", [128, 2 * 256], BF16)
    B1SB = sb("B1SB", [128, n_sub * 44], F32)
    B2SB = sb("B2SB", [128, n_sub * 8], F32)
    QSC = sb("QSC", [128, n_sub], F32)
    KSC = sb("KSC", [128, n_sub], F32)
    LNG = sb("LNG", [128, n_sub * 8], F32)
    LNB = sb("LNB", [128, n_sub * 8], F32)
    ONES128 = sb("ONES128", [128, 1], BF16)
    ONES64 = sb("ONES64", [64, 1], BF16)
    ONESR1 = sb("ONESR1", [1, 128], F32)
    NEGR1 = sb("NEGR1", [1, 128], F32)
    ONESR64 = sb("ONESR64", [1, 64], F32)
    STAT = sb("STAT", [1, STAT_W], F32)
    EPS = sb("EPS", [1, 8], F32)
    RSTDB = sb("RSTDB", [128, T], F32)
    NMRB = sb("NMRB", [128, T], BF16)
    TMPF = sb("TMPF", [128, 512], F32)
    TMPG = sb("TMPG", [128, 512], F32)
    VB = sb("VB", [128, 512], F32)
    GB = sb("GB", [128, 512], F32)
    NM8 = sb("NM8", [128, 4], F32)
    SUMC = sb("SUMC", [128, 4], F32)
    XS = sb("XS", [128, T], F32)
    PRAW = XC[:, 0:T]
    SQX = XC[:, TG:2 * TG]

    psumt = lambda name, shape, dt=F32: ctx.enter_context(nc.psum_tensor('b_' + name, shape, dt))
    ps_a = psumt("ps_a", [128, 512])
    ps_b = psumt("ps_b", [128, 512])
    ps_c = psumt("ps_c", [128, 512])
    ps_d = ps_a
    ps_t = psumt("ps_t", [128, 512], BF16)
    ps_t2 = psumt("ps_t2", [128, 512], BF16)
    ps_o2 = psumt("ps_o2", [64, 128])
    ps_o2b = psumt("ps_o2b", [64, 128])
    ps_s1 = psumt("ps_s1", [1, 512])
    ps_s2 = ps_s1

    s = NS()
    s.pe, s.act, s.dve = p.eng_sems
    s.ones128, s.onesr1, s.negr1 = ONES128, ONESR1, NEGR1
    s.stat = STAT
    s.rstdB, s.nmrB = RSTDB, NMRB
    s.ps_s1, s.ps_s2, s.ps_c, s.ps_d = ps_s1, ps_s2, ps_c, ps_d
    s.psfree = {}
    s.sqch_free = None
    s.stat_users = []
    s.nm8_free = [None, None]
    s.vstg_free = [None, None]
    s.pq_free = [None, None]
    s.pts_free = [None, None]
    s.tmpf_free = s.tmpg_free = None
    s.eps = EPS

    const_toks = []
    for dst, srcap in [(COSS, cosS_in), (SINS, sinS_in), (COSI, cosI_in),
                       (SINI, sinI_in), (PERM, perm_in), (IDN, idn_in),
                       (MASKA, maska_in), (MASK0, mask0_in),
                       (B1SB, b1p), (B2SB, b2p), (QSC, qsp), (KSC, ksp),
                       (LNG, lngp), (LNB, lnbp)]:
        const_toks.append(p.dma("sync", dst[:], srcap))
    for tns, val in [(ONES128, 1.0), (ONES64, 1.0), (ONESR1, 1.0),
                     (NEGR1, -1.0), (ONESR64, 1.0)]:
        const_toks.append(p.op("vector", lambda e, tns=tns, val=val: e.memset(tns[:], val),
                               inc=s.dve))
    guard0 = p.op("vector", lambda e: e.memset(
        XP.ap().rearrange("p (c g) -> p c g", c=NCH)[:, :, 0:1], 0.0), inc=s.dve)
    const_toks.append(guard0)
    s.cgate = tokmax(const_toks)
    s.eps_tok = p.op("vector", lambda e: e.memset(EPS[:], 1e-5), inc=s.dve)

    def xpu(c, st, w):
        return XP[:, c * TG + 1 + st: c * TG + 1 + st + w]

    def xps(c, st, w):
        off = 1 if c < 4 else 0
        return XP[:, c * TG + off + st: c * TG + off + st + w]

    xp_w = {}
    xp_r = {c: [] for c in range(NCH)}

    def rd_gate(c):
        return [xp_w[c]] if xp_w.get(c) else []

    def wr_gate(c):
        g = rd_gate(c) + xp_r[c]
        return tokmax(g)

    store_last = None
    for b in range(B_CORE):
        bsl = slice(b * T, (b + 1) * T)
        bgate = [store_last] if store_last else []

        xln_toks = emit_lnin_streamed(p, s, xt, bsl, None, XC, SQP[:], TMPF,
                                      bgate, dst_of=lambda c: xpu(c, 0, T))
        for c in range(NCH):
            xp_w[c] = xln_toks[c]
            xp_r[c] = []

        for i in range(n_sub):
            o_toks = []
            for hp in range(8):
                slot = 0
                wtok = p.dma("gpsimd",
                             WQKS[:, slot * NCH * 384:(slot + 1) * NCH * 384],
                             wqk[i, hp],
                             waits=[s.psfree.get(("wqk", slot))] if s.psfree.get(("wqk", slot)) else [])
                banks = [ps_a, ps_b, ps_c, ps_a]
                mm_last = {}
                for tc in range(NTC):
                    for cch in range(NCH):
                        pst = banks[tc]
                        w = []
                        if cch == 0:
                            w = [wtok]
                            ft = s.psfree.get(id(pst))
                            if ft is not None:
                                w.append(ft)
                        w = w + rd_gate(cch)
                        mm_last[tc] = p.op("tensor", lambda e, cch=cch, tc=tc, pst=pst, slot=slot: e.matmul(
                            pst[:], WQKS[:, slot * NCH * 384 + cch * 384 + 256:
                                         slot * NCH * 384 + cch * 384 + 384],
                            xps(cch, tc * 512, 512),
                            start=(cch == 0), stop=(cch == NCH - 1), skip_group_check=True),
                            waits=w, inc=(s.pe if cch == NCH - 1 else None))
                for cch in range(NCH):
                    xp_r[cch].append(mm_last[NTC - 1])
                for tc in range(NTC):
                    pst = banks[tc]
                    pstt = ps_t if tc % 2 == 0 else ps_t2
                    vse = p.op("scalar", lambda e, pst=pst, tc=tc: e.copy(
                        VSTG[:, (tc % 2) * 512:(tc % 2) * 512 + 512], pst[:]),
                        waits=[mm_last[tc]] + ([s.vstg_free[tc % 2]] if s.vstg_free[tc % 2] else []),
                        inc=s.act)
                    s.psfree[id(pst)] = vse
                    trs = []
                    for kt in range(4):
                        trw = [vse]
                        if kt == 0:
                            ftt = s.psfree.get(id(pstt))
                            if ftt is not None:
                                trw.append(ftt)
                        tr = p.op("tensor", lambda e, kt=kt, pstt=pstt, tc=tc: e.transpose(
                            pstt[:, kt * 128:(kt + 1) * 128],
                            VSTG[:, (tc % 2) * 512 + kt * 128:(tc % 2) * 512 + (kt + 1) * 128],
                            IDN[:]),
                            waits=trw + s.cgate, inc=s.pe)
                        trs.append(tr)
                    vte = p.op("scalar", lambda e, tc=tc, pstt=pstt: e.copy(
                        VT2[:, tc * 512:(tc + 1) * 512], pstt[:]),
                        waits=trs + ([s.psfree.get("vt2")] if tc == 0 and s.psfree.get("vt2") else []),
                        inc=s.act)
                    s.psfree[id(pstt)] = vte
                    s.vstg_free[tc % 2] = vte
                    vlast = vte
                pr_toks = {}
                for which in ("q", "k"):
                    woff = 0 if which == "q" else 128
                    praw_toks = []
                    banks = [ps_a, ps_b, ps_c, ps_a]
                    mm_last = {}
                    for tc in range(NTC):
                        for cch in range(NCH):
                            pst = banks[tc]
                            w = []
                            if cch == 0:
                                w = [wtok]
                                ft = s.psfree.get(id(pst))
                                if ft is not None:
                                    w.append(ft)
                            w = w + rd_gate(cch)
                            mm_last[tc] = p.op("tensor", lambda e, cch=cch, tc=tc, pst=pst, slot=slot, woff=woff: e.matmul(
                                pst[:], WQKS[:, slot * NCH * 384 + cch * 384 + woff:
                                             slot * NCH * 384 + cch * 384 + woff + 128],
                                xps(cch, tc * 512, 512),
                                start=(cch == 0), stop=(cch == NCH - 1), skip_group_check=True),
                                waits=w, inc=(s.pe if cch == NCH - 1 else None))
                    for cch in range(NCH):
                        xp_r[cch].append(mm_last[NTC - 1])
                    for tc in range(NTC):
                        tsl = slice(tc * 512, tc * 512 + 512)
                        pst = banks[tc]
                        ev = p.op("scalar", lambda e, tsl=tsl, pst=pst: e.copy(PRAW[:, tsl], pst[:]),
                                  waits=[mm_last[tc]] + ([s.praw_free] if getattr(s, "praw_free", None) else []),
                                  inc=s.act)
                        s.psfree[id(pst)] = ev
                        praw_toks.append(ev)
                    stat_gate = tokmax(s.stat_users)
                    s.stat_users = []
                    sqt = p.op("scalar", lambda e: e.activation(SQP[:], PRAW, AF.Square),
                               waits=praw_toks + ([s.sqp_free] if getattr(s, "sqp_free", None) else []),
                               inc=s.act)
                    ln_toks = []
                    for half in range(2):
                        soff = SOFF_SUM if half == 0 else SOFF_SQ
                        for tc in range(NTC):
                            mm = p.op("tensor", lambda e, half=half, tc=tc: e.matmul(
                                ps_s1[:], ONES128[64 * half:64 * half + 64, 0:1],
                                SQP[64 * half:64 * half + 64, tc * 512:(tc + 1) * 512],
                                start=True, stop=True, skip_group_check=True),
                                waits=[sqt] + s.cgate + ([s.psfree[id(s.ps_s1)]] if s.psfree.get(id(s.ps_s1)) else []),
                                inc=s.pe)
                            l1 = p.op("scalar", lambda e, soff=soff, tc=tc: e.activation(
                                STAT[0:1, soff + tc * 512: soff + tc * 512 + 512],
                                ps_s1[:], AF.Ln),
                                waits=[mm, stat_gate[0] if stat_gate else None]
                                + (stat_gate[1:] if len(stat_gate) > 1 else []),
                                inc=s.act)
                            s.psfree[id(s.ps_s1)] = l1
                            ln_toks.append(l1)
                    e1 = p.op("scalar", lambda e: e.activation(
                        STAT[0:1, 0:2 * T], STAT[0:1, 0:2 * T], AF.Exp, scale=-0.5),
                        waits=ln_toks, inc=s.act)
                    e2 = p.op("vector", lambda e: e.tensor_scalar_min(
                        STAT[0:1, 0:2 * T], STAT[0:1, 0:2 * T], 1e12),
                        waits=[e1], inc=s.dve)
                    s.stat_users += [e2]
                    rn_toks = []
                    for half in range(2):
                        soff = SOFF_SUM if half == 0 else SOFF_SQ
                        for tc in range(NTC):
                            bmm = p.op("tensor", lambda e, soff=soff, tc=tc: e.matmul(
                                ps_c[:, 0:512][0:64, :], ONESR64[:],
                                STAT[0:1, soff + tc * 512: soff + tc * 512 + 512],
                                start=True, stop=True, skip_group_check=True),
                                waits=[e2] + s.cgate + ([s.psfree[id(s.ps_c)]] if s.psfree.get(id(s.ps_c)) else []),
                                inc=s.pe)
                            cp = p.op("scalar", lambda e, half=half, tc=tc: e.copy(
                                RNB[64 * half:64 * half + 64, tc * 512:(tc + 1) * 512],
                                ps_c[0:64, :]),
                                waits=[bmm], inc=s.act)
                            s.psfree[id(s.ps_c)] = cp
                            s.psfree[id(ps_c)] = cp
                            rn_toks.append(cp)
                    s.stat_users += rn_toks
                    sccol = QSC if which == "q" else KSC
                    ap1 = p.op("vector", lambda e, sccol=sccol, i=i: e.scalar_tensor_tensor(
                        SQP[:], PRAW, sccol[:, i:i + 1], RNB[:],
                        ALU.mult, ALU.mult),
                        waits=rn_toks + [sqt] + s.cgate, inc=s.dve)
                    s.praw_free = ap1
                    tcos = COSS if which == "q" else COSI
                    tsin = SINS if which == "q" else SINI
                    dstb = QR2 if which == "q" else KR2
                    rope_toks = []
                    for tc in range(NTC):
                        tsl = slice(tc * 512, tc * 512 + 512)
                        pst = banks[tc]
                        rotw = [ap1] + s.cgate
                        ft = s.psfree.get(id(pst))
                        if ft is not None:
                            rotw.append(ft)
                        rot = p.op("tensor", lambda e, tsl=tsl, pst=pst: e.matmul(
                            pst[:], PERM[:], SQP[:, tsl], start=True, stop=True,
                            skip_group_check=True),
                            waits=rotw, inc=s.pe)
                        t1 = p.op("vector", lambda e, tsl=tsl, tcos=tcos: e.tensor_mul(
                            TMPF[:], SQP[:, tsl], tcos[:, tsl]),
                            waits=[ap1] + s.cgate + ([s.tmpf_free] if s.tmpf_free else []),
                            inc=s.dve)
                        t2 = p.op("vector", lambda e, tsl=tsl, tsin=tsin, pst=pst: e.tensor_mul(
                            TMPG[:], pst[:], tsin[:, tsl]),
                            waits=[rot] + ([s.tmpg_free] if s.tmpg_free else []),
                            inc=s.dve)
                        s.psfree[id(pst)] = t2
                        t3 = p.op("vector", lambda e, tsl=tsl, dstb=dstb: e.tensor_add(
                            dstb[:, tsl], TMPF[:], TMPG[:]),
                            waits=[t1, t2] + ([s.qr_free[id(dstb)]] if getattr(s, "qr_free", {}).get(id(dstb)) else []),
                            inc=s.dve)
                        s.tmpf_free = t3
                        s.tmpg_free = t3
                        rope_toks.append(t3)
                    s.sqp_free = rope_toks[-1]
                    pr_toks[which] = rope_toks
                s.psfree[("wqk", slot)] = pr_toks["k"][-1]

                if not hasattr(s, "qr_free"):
                    s.qr_free = {}
                head_toks = []
                for half in range(2):
                    h = hp * 2 + half
                    wins = []
                    for w_i in range(8):
                        for j in range(2):
                            if w_i == 0:
                                kstart, kw = 0, 256
                            else:
                                kstart = (w_i - 1) * 256 + j * 128
                                kw = 384
                            wins.append((w_i, j, kstart, kw))

                    def phaseA(idx):
                        w_i, j, kstart, kw = wins[idx]
                        qstart = w_i * 256 + j * 128
                        par = idx % 2
                        pst = ps_a if par == 0 else ps_b
                        pq = PQ[:, par * 384: par * 384 + kw]
                        nm_r = NM8[:, 2 * par: 2 * par + 1]
                        sm_r = SUMC[:, 2 * par: 2 * par + 1]
                        sm_i = SUMC[:, 2 * par + 1: 2 * par + 2]
                        sw = pr_toks["q"] + pr_toks["k"]
                        ft = s.psfree.get(id(pst))
                        if ft is not None:
                            sw.append(ft)
                        smm = p.op("tensor", lambda e, half=half, qstart=qstart, kstart=kstart, kw=kw, pst=pst: e.matmul(
                            pst[:, 0:kw], QR2[64 * half:64 * half + 64, qstart:qstart + 128],
                            KR2[64 * half:64 * half + 64, kstart:kstart + kw],
                            start=True, stop=True, skip_group_check=True),
                            waits=sw, inc=s.pe)
                        mask_ap = (MASK0[:, j * 256:(j + 1) * 256] if w_i == 0
                                   else MASKA[:, 0:384])
                        madd = p.op("vector", lambda e, pst=pst, kw=kw, mask_ap=mask_ap: e.scalar_tensor_tensor(
                            pst[:, 0:kw], pst[:, 0:kw], 8.0, mask_ap, ALU.mult, ALU.add),
                            waits=[smm] + s.cgate, inc=s.dve)
                        nmx = p.op("vector", lambda e, pst=pst, kw=kw, nm_r=nm_r: e.reduce_max(
                            nm_r, pst[:, 0:kw], axis=AX.X, negate=True),
                            waits=[madd] + ([s.nm8_free[par]] if s.nm8_free[par] else []),
                            inc=s.dve)
                        exq = p.op("scalar", lambda e, pst=pst, kw=kw, pq=pq, nm_r=nm_r, sm_r=sm_r: e.activation(
                            pq, pst[:, 0:kw], AF.Exp, bias=nm_r, accum_out=sm_r),
                            waits=[nmx] + ([s.pq_free[par]] if s.pq_free[par] else []),
                            inc=s.act)
                        s.psfree[id(pst)] = exq
                        rcp = p.op("vector", lambda e, sm_r=sm_r, sm_i=sm_i: e.reciprocal(
                            sm_i, sm_r),
                            waits=[exq], inc=s.dve)
                        s.nm8_free[par] = rcp
                        pnorm = p.op("vector", lambda e, kw=kw, pq=pq, sm_i=sm_i: e.tensor_scalar(
                            pq, pq, sm_i, None, ALU.mult),
                            waits=[rcp], inc=s.dve)
                        return pnorm

                    def phaseB(idx, pnorm):
                        w_i, j, kstart, kw = wins[idx]
                        qstart = w_i * 256 + j * 128
                        nkt = kw // 128
                        par = idx % 2
                        pq = PQ[:, par * 384: par * 384 + kw]
                        pts = PTS[:, par * 384: par * 384 + kw]
                        pstt = ps_t if par == 0 else ps_t2
                        psto = ps_o2 if par == 0 else ps_o2b
                        tr_toks = []
                        for kt in range(nkt):
                            trw = [pnorm]
                            ftt = s.psfree.get(id(pstt))
                            if ftt is not None and kt == 0:
                                trw.append(ftt)
                            tr = p.op("tensor", lambda e, kt=kt, pstt=pstt, pq=pq: e.transpose(
                                pstt[:, kt * 128:(kt + 1) * 128],
                                pq[:, kt * 128:(kt + 1) * 128], IDN[:]),
                                waits=trw + s.cgate, inc=s.pe)
                            tr_toks.append(tr)
                        ptc = p.op("scalar", lambda e, kw=kw, pts=pts, pstt=pstt: e.copy(
                            pts, pstt[:, 0:kw]),
                            waits=tr_toks + ([s.pts_free[par]] if s.pts_free[par] else []),
                            inc=s.act)
                        s.psfree[id(pstt)] = ptc
                        s.pq_free[par] = tr_toks[-1]
                        pv_tok = None
                        for kt in range(nkt):
                            ktg = (kstart + kt * 128) // 128
                            w = [ptc]
                            if kt == 0:
                                ftv = s.psfree.get(id(psto))
                                if ftv is not None:
                                    w.append(ftv)
                                w.append(vlast)
                            pv_tok = p.op("tensor", lambda e, kt=kt, ktg=ktg, half=half, psto=psto, pts=pts: e.matmul(
                                psto[:], VT2[:, ktg * 128 + 64 * half: ktg * 128 + 64 * half + 64],
                                pts[:, kt * 128:(kt + 1) * 128],
                                start=(kt == 0), stop=(kt == nkt - 1),
                                skip_group_check=True),
                                waits=w, inc=(s.pe if kt == nkt - 1 else None))
                        s.pts_free[par] = pv_tok
                        oc = p.op("scalar", lambda e, h=h, qstart=qstart, psto=psto: e.copy(
                            SCB[:, (h // 2) * T + qstart:(h // 2) * T + qstart + 128][
                                64 * (h % 2):64 * (h % 2) + 64, :], psto[:]),
                            waits=[pv_tok], inc=s.act)
                        s.psfree[id(psto)] = oc
                        o_toks.append(oc)
                        head_toks.append(oc)

                    pn_prev = phaseA(0)
                    for idx in range(1, len(wins)):
                        pn = phaseA(idx)
                        phaseB(idx - 1, pn_prev)
                        pn_prev = pn
                    phaseB(len(wins) - 1, pn_prev)
                s.qr_free[id(QR2)] = head_toks[-1]
                s.qr_free[id(KR2)] = head_toks[-1]
                s.psfree["vt2"] = head_toks[-1]
            o_last = tokmax(o_toks)

            for outc in range(NCH):
                slot = 0
                wtok = p.dma("gpsimd",
                             WOS[:, slot * NCH * 128:(slot + 1) * NCH * 128],
                             wob[i, outc],
                             waits=[s.psfree.get(("wos", slot))] if s.psfree.get(("wos", slot)) else [])
                last_add = None
                banks = [ps_a, ps_b, ps_c, ps_a]
                mm_last = {}
                for tc in range(NTC):
                    pst = banks[tc]
                    for cch in range(NCH):
                        w = []
                        if cch == 0:
                            w = [wtok] + o_last
                            ft = s.psfree.get(id(pst))
                            if ft is not None:
                                w.append(ft)
                        mm_last[tc] = p.op("tensor", lambda e, cch=cch, tc=tc, pst=pst, slot=slot: e.matmul(
                            pst[:], WOS[:, slot * NCH * 128 + cch * 128: slot * NCH * 128 + cch * 128 + 128],
                            SCB[:, cch * T + tc * 512: cch * T + tc * 512 + 512],
                            start=(cch == 0), stop=(cch == NCH - 1), skip_group_check=True),
                            waits=w, inc=(s.pe if cch == NCH - 1 else None))
                    radd = p.op("vector", lambda e, outc=outc, tc=tc, pst=pst: e.tensor_add(
                        xpu(outc, tc * 512, 512), xpu(outc, tc * 512, 512), pst[:]),
                        waits=[mm_last[tc]] + wr_gate(outc), inc=s.dve)
                    s.psfree[id(pst)] = radd
                    last_add = radd
                xp_w[outc] = last_add
                xp_r[outc] = []
                s.psfree[("wos", slot)] = last_add

            stat_gate = tokmax(s.stat_users)
            s.stat_users = []
            z1 = p.op("vector", lambda e: e.memset(STAT[0:1, 0:2 * T], 0.0),
                      waits=stat_gate, inc=s.dve)
            acc_toks = [z1]
            for cch in range(NCH):
                sqg = p.op("scalar", lambda e, cch=cch: e.activation(
                    SQX, XP[:, cch * TG:(cch + 1) * TG], AF.Square),
                    waits=rd_gate(cch) + ([s.sqx_free] if getattr(s, "sqx_free", None) else []),
                    inc=s.act)
                xp_r[cch].append(sqg)
                mm1 = mm2 = None
                for tc in range(NTC):
                    tsl = slice(tc * 512, tc * 512 + 512)
                    off = 1 if cch < 4 else 0
                    mm1 = p.op("tensor", lambda e, cch=cch, tc=tc: e.matmul(
                        ps_s1[:], ONES128[:], xps(cch, tc * 512, 512),
                        start=True, stop=True, skip_group_check=True),
                        waits=rd_gate(cch) + s.cgate + ([s.psfree[id(s.ps_s1)]] if s.psfree.get(id(s.ps_s1)) else []),
                        inc=s.pe)
                    a1 = p.op("vector", lambda e, tsl=tsl: e.tensor_add(
                        STAT[0:1, SOFF_SUM:SOFF_SUM + T][:, tsl],
                        STAT[0:1, SOFF_SUM:SOFF_SUM + T][:, tsl], ps_s1[:]),
                        waits=[mm1, z1], inc=s.dve)
                    s.psfree[id(s.ps_s1)] = a1
                    mm2 = p.op("tensor", lambda e, cch=cch, tc=tc, off=off: e.matmul(
                        ps_s2[:], ONES128[:],
                        SQX[:, off + tc * 512: off + tc * 512 + 512],
                        start=True, stop=True, skip_group_check=True),
                        waits=[sqg] + ([s.psfree[id(s.ps_s2)]] if s.psfree.get(id(s.ps_s2)) else []),
                        inc=s.pe)
                    a2 = p.op("vector", lambda e, tsl=tsl: e.tensor_add(
                        STAT[0:1, SOFF_SQ:SOFF_SQ + T][:, tsl],
                        STAT[0:1, SOFF_SQ:SOFF_SQ + T][:, tsl], ps_s2[:]),
                        waits=[mm2, z1], inc=s.dve)
                    s.psfree[id(s.ps_s2)] = a2
                    acc_toks += [a1, a2]
                xp_r[cch].append(mm1)
                s.sqx_free = acc_toks[-1]
            bc_toks = []
            for tc in range(NTC):
                mu = STAT[0:1, SOFF_SUM + tc * 512:SOFF_SUM + tc * 512 + 512]
                rs = STAT[0:1, SOFF_SQ + tc * 512:SOFF_SQ + tc * 512 + 512]
                mu2 = STAT[0:1, SOFF_MU2:SOFF_MU2 + 512]
                t_mu = p.op("vector", lambda e, mu=mu: e.tensor_scalar_mul(mu, mu, 1.0 / C),
                            waits=acc_toks, inc=s.dve)
                t_mu2 = p.op("scalar", lambda e, mu=mu, mu2=mu2: e.activation(mu2, mu, AF.Square),
                             waits=[t_mu] + ([s.mu2_free] if getattr(s, "mu2_free", None) else []),
                             inc=s.act)
                t_var = p.op("vector", lambda e, rs=rs, mu2=mu2: e.scalar_tensor_tensor(
                    rs, rs, 1.0 / C, mu2, ALU.mult, ALU.subtract),
                    waits=[t_mu2] + acc_toks, inc=s.dve)
                s.mu2_free = t_var
                t_ln = p.op("scalar", lambda e, rs=rs: e.activation(rs, rs, AF.Ln, bias=s.eps[0:1, 0:1]),
                            waits=[t_var, s.eps_tok], inc=s.act)
                t_rstd = p.op("scalar", lambda e, rs=rs: e.activation(rs, rs, AF.Exp, scale=-0.5),
                              waits=[t_ln], inc=s.act)
                t_mr = p.op("vector", lambda e, mu=mu, rs=rs: e.tensor_mul(mu, mu, rs),
                            waits=[t_rstd], inc=s.dve)
                b1 = p.op("tensor", lambda e, rs=rs: e.matmul(
                    ps_c[:], ONESR1[:], rs, start=True, stop=True, skip_group_check=True),
                    waits=[t_rstd] + s.cgate + ([s.psfree[id(s.ps_c)]] if s.psfree.get(id(s.ps_c)) else []), inc=s.pe)
                b2 = p.op("tensor", lambda e, mu=mu: e.matmul(
                    ps_d[:], NEGR1[:], mu, start=True, stop=True, skip_group_check=True),
                    waits=[t_mr] + s.cgate + ([s.psfree[id(s.ps_d)]] if s.psfree.get(id(s.ps_d)) else []), inc=s.pe)
                c1 = p.op("scalar", lambda e, tc=tc: e.copy(
                    RSTDB[:, tc * 512:tc * 512 + 512], ps_c[:]), waits=[b1], inc=s.act)
                c2 = p.op("scalar", lambda e, tc=tc: e.copy(
                    NMRB[:, tc * 512:tc * 512 + 512], ps_d[:]), waits=[b2], inc=s.act)
                s.ps_c_free, s.psfree[id(s.ps_d)] = c1, c2
                bc_toks += [c1, c2]
            s.stat_users += bc_toks
            xsn_toks = {}
            for cch in range(NCH):
                evs = []
                for tc in range(NTC):
                    tsl = slice(tc * 512, tc * 512 + 512)
                    t1 = p.op("vector", lambda e, cch=cch, tc=tc, tsl=tsl: e.tensor_mul(
                        TMPF[:], xps(cch, tc * 512, 512), RSTDB[:, tsl]),
                        waits=bc_toks + rd_gate(cch) + ([s.tmpf_free] if s.tmpf_free else []),
                        inc=s.dve)
                    t2 = p.op("vector", lambda e, tsl=tsl: e.tensor_add(
                        TMPF[:], TMPF[:], NMRB[:, tsl]), waits=[t1], inc=s.dve)
                    s.tmpf_free = t2
                    t3 = p.op("scalar", lambda e, cch=cch, tsl=tsl, i=i: e.activation(
                        SCB[:, cch * T:(cch + 1) * T][:, tsl], TMPF[:], AF.Identity,
                        bias=LNB[:, i * 8 + cch:i * 8 + cch + 1],
                        scale=LNG[:, i * 8 + cch:i * 8 + cch + 1]),
                        waits=[t2] + s.cgate, inc=s.act)
                    s.tmpf_free = t3
                    evs.append(t3)
                    xp_r[cch].append(t1)
                xsn_toks[cch] = evs
            xsn_last = tokmax([t for l in xsn_toks.values() for t in l])

            for tc in range(NTC):
                tsl = slice(tc * 512, tc * 512 + 512)
                hb_toks = []
                for ep in range(22):
                    slot = 0
                    wtok = p.dma("gpsimd",
                                 W12S[:, slot * NCH * 256:(slot + 1) * NCH * 256],
                                 w1p[i, ep],
                                 waits=[s.psfree.get(("w12", slot))] if s.psfree.get(("w12", slot)) else [])
                    mmA = mmG = None
                    for cch in range(NCH):
                        w = []
                        if cch == 0:
                            w = [wtok] + xsn_last
                            ft = s.psfree.get(id(ps_a))
                            if ft is not None:
                                w.append(ft)
                        mmA = p.op("tensor", lambda e, cch=cch, tsl=tsl, slot=slot: e.matmul(
                            ps_a[:], W12S[:, slot * NCH * 256 + cch * 256: slot * NCH * 256 + cch * 256 + 128],
                            SCB[:, cch * T:(cch + 1) * T][:, tsl],
                            start=(cch == 0), stop=(cch == NCH - 1), skip_group_check=True),
                            waits=w, inc=(s.pe if cch == NCH - 1 else None))
                    for cch in range(NCH):
                        w = []
                        if cch == 0:
                            ft = s.psfree.get(id(ps_b))
                            if ft is not None:
                                w.append(ft)
                        mmG = p.op("tensor", lambda e, cch=cch, tsl=tsl, slot=slot: e.matmul(
                            ps_b[:], W12S[:, slot * NCH * 256 + cch * 256 + 128: slot * NCH * 256 + cch * 256 + 256],
                            SCB[:, cch * T:(cch + 1) * T][:, tsl],
                            start=(cch == 0), stop=(cch == NCH - 1), skip_group_check=True),
                            waits=w, inc=(s.pe if cch == NCH - 1 else None))
                    av = p.op("scalar", lambda e, ep=ep, i=i: e.activation(
                        VB[:], ps_a[:], AF.Identity, bias=B1SB[:, i * 44 + ep:i * 44 + ep + 1]),
                        waits=[mmA] + ([s.vb_free] if getattr(s, "vb_free", None) else []),
                        inc=s.act)
                    s.psfree[id(ps_a)] = av
                    gg = p.op("scalar", lambda e, ep=ep, i=i: e.activation(
                        GB[:], ps_b[:], AF.Gelu, bias=B1SB[:, i * 44 + 22 + ep:i * 44 + 22 + ep + 1]),
                        waits=[mmG] + ([s.gb_free] if getattr(s, "gb_free", None) else []),
                        inc=s.act)
                    s.psfree[id(ps_b)] = gg
                    hcomb = p.op("vector", lambda e, ep=ep: e.tensor_mul(
                        HB[:, ep * 512:(ep + 1) * 512], VB[:], GB[:]),
                        waits=[av, gg] + ([s.psfree.get(("hb", ep))] if s.psfree.get(("hb", ep)) else []),
                        inc=s.dve)
                    s.vb_free = hcomb
                    s.gb_free = hcomb
                    hb_toks.append(hcomb)
                    s.psfree[("w12", slot)] = hcomb
                for outc in range(NCH):
                    slot = 0
                    wtok = p.dma("gpsimd",
                                 W2S[:, slot * 22 * 128:(slot + 1) * 22 * 128],
                                 w2p[i, outc],
                                 waits=[s.psfree.get(("w2", slot))] if s.psfree.get(("w2", slot)) else [])
                    mmY = None
                    for ep in range(22):
                        w = [hb_toks[ep]]
                        if ep == 0:
                            w.append(wtok)
                            ft = s.psfree.get(id(ps_c))
                            if ft is not None:
                                w.append(ft)
                        mmY = p.op("tensor", lambda e, ep=ep, slot=slot: e.matmul(
                            ps_c[:], W2S[:, slot * 22 * 128 + ep * 128: slot * 22 * 128 + ep * 128 + 128],
                            HB[:, ep * 512:(ep + 1) * 512],
                            start=(ep == 0), stop=(ep == 21), skip_group_check=True),
                            waits=w, inc=(s.pe if ep == 21 else None))
                    yb2 = p.op("scalar", lambda e, outc=outc, i=i: e.activation(
                        TMPG[:], ps_c[:], AF.Identity, bias=B2SB[:, i * 8 + outc:i * 8 + outc + 1]),
                        waits=[mmY] + ([s.tmpg_free] if s.tmpg_free else []), inc=s.act)
                    s.psfree[id(ps_c)] = yb2
                    radd = p.op("vector", lambda e, outc=outc, tc=tc: e.tensor_add(
                        xpu(outc, tc * 512, 512), xpu(outc, tc * 512, 512), TMPG[:]),
                        waits=[yb2] + wr_gate(outc), inc=s.dve)
                    s.tmpg_free = radd
                    xp_w[outc] = radd
                    if tc == NTC - 1:
                        xp_r[outc] = []
                    s.psfree[("w2", slot)] = radd
                for ep in range(22):
                    s.psfree[("hb", ep)] = mmY

        ybt = {c: [xp_w[c]] for c in range(NCH)}
        s.xs_gate = None
        st_toks = emit_lnout_add_store(p, s, None, ybt, XC[:, 0:T],
                                       xt, xout, bsl, XS, TMPF,
                                       yb_of=lambda c: xpu(c, 0, T))
        store_last = st_toks[-1]
        for c in range(NCH):
            xp_r[c].append(st_toks[c])

    return store_last




def build_stage_b(n_sub=6):
    nc = bass.Bass("TRN2", target_bir_lowering=False)
    xt = nc.dram_tensor("xt", [NCH, 128, B_CORE * T], F32, kind="ExternalInput").ap()
    wqk = nc.dram_tensor("wqk", [n_sub, 8, 128, NCH * 384], F32, kind="ExternalInput").ap()
    wob = nc.dram_tensor("wob", [n_sub, 8, 128, NCH * 128], F32, kind="ExternalInput").ap()
    w1p = nc.dram_tensor("w1p", [n_sub, 22, 128, NCH * 256], F32, kind="ExternalInput").ap()
    w2p = nc.dram_tensor("w2p", [n_sub, 8, 128, 22 * 128], F32, kind="ExternalInput").ap()
    b1p = nc.dram_tensor("b1p", [128, n_sub * 44], F32, kind="ExternalInput").ap()
    b2p = nc.dram_tensor("b2p", [128, n_sub * 8], F32, kind="ExternalInput").ap()
    qsp = nc.dram_tensor("qsp", [128, n_sub], F32, kind="ExternalInput").ap()
    ksp = nc.dram_tensor("ksp", [128, n_sub], F32, kind="ExternalInput").ap()
    lngp = nc.dram_tensor("lngp", [128, n_sub * 8], F32, kind="ExternalInput").ap()
    lnbp = nc.dram_tensor("lnbp", [128, n_sub * 8], F32, kind="ExternalInput").ap()
    cosS_in = nc.dram_tensor("cosS", [128, T], BF16, kind="ExternalInput").ap()
    sinS_in = nc.dram_tensor("sinS", [128, T], BF16, kind="ExternalInput").ap()
    cosI_in = nc.dram_tensor("cosI", [128, T], BF16, kind="ExternalInput").ap()
    sinI_in = nc.dram_tensor("sinI", [128, T], BF16, kind="ExternalInput").ap()
    perm_in = nc.dram_tensor("perm", [128, 128], BF16, kind="ExternalInput").ap()
    idn_in = nc.dram_tensor("idn", [128, 128], BF16, kind="ExternalInput").ap()
    maska_in = nc.dram_tensor("maska", [128, 384], BF16, kind="ExternalInput").ap()
    mask0_in = nc.dram_tensor("mask0", [128, 2 * 256], BF16, kind="ExternalInput").ap()
    xout = nc.dram_tensor("xout", [NCH, 128, B_CORE * T], F32, kind="ExternalOutput").ap()

    with contextlib.ExitStack() as ctx:
        p = Prog(nc, ctx)
        p.dma_sem = {"sync": p.sem("s_dsync"), "gpsimd": p.sem("s_dgp")}
        p.eng_sems = (p.sem("s_pe"), p.sem("s_act"), p.sem("s_dve"))
        with contextlib.ExitStack() as c1:
            store_last = _emit_stage_b(nc, c1, p, xt, wqk, wob, w1p, w2p, b1p, b2p, qsp, ksp, lngp, lnbp, cosS_in, sinS_in, cosI_in, sinI_in, perm_in, idn_in, maska_in, mask0_in, xout, n_sub)
        p.op("sync", lambda e: e.nop(), waits=[store_last])
        with nc.Block() as block:
            p.emit(block)
    return nc

def _xpos_tables():
    half = np.arange(0, 64, 2, dtype=np.float64)
    inv_freq = 1.0 / 10000.0 ** (half / 64)
    t = np.arange(T, dtype=np.float64)
    freqs = np.concatenate([np.outer(t, inv_freq), np.outer(t, inv_freq)], 1)
    base_scale = (half + 0.4 * 64) / (1.4 * 64)
    power = (t - T // 2) / (256 // 2)
    scale = base_scale[None, :] ** power[:, None]
    scale = np.concatenate([scale, scale], 1)
    cosS = (np.cos(freqs) * scale).T
    sinS = (np.sin(freqs) * scale).T
    cosI = (np.cos(freqs) / scale).T
    sinI = (np.sin(freqs) / scale).T
    out = []
    for a in (cosS, sinS, cosI, sinI):
        out.append(np.concatenate([a, a], 0).astype(np.float32).astype(BF))
    return out


def _local_masks():
    ql = np.arange(128)[:, None]
    kl = np.arange(384)[None, :]
    maska = np.where((kl - ql >= 0) & (kl - ql <= 256), 0.0, -1e30).astype(np.float32)
    k0 = np.arange(256)[None, :]
    m0 = [np.where(k0 <= j * 128 + ql, 0.0, -1e30).astype(np.float32) for j in range(2)]
    return maska.astype(BF), np.concatenate(m0, 1).astype(BF)


def _prep_b_weights(inputs, n_sub=6):
    Wqkv = np.asarray(inputs["L0_ffn_Wqkv"], dtype=np.float32)
    Wo = np.asarray(inputs["L0_ffn_Wo"], dtype=np.float32)
    W1 = np.asarray(inputs["L0_ffn_W1"], dtype=np.float32)
    W2 = np.asarray(inputs["L0_ffn_W2"], dtype=np.float32)
    b1 = np.asarray(inputs["L0_ffn_b1"], dtype=np.float32)
    b2 = np.asarray(inputs["L0_ffn_b2"], dtype=np.float32)
    qs = np.asarray(inputs["L0_ffn_q_scale"], dtype=np.float32)
    ks = np.asarray(inputs["L0_ffn_k_scale"], dtype=np.float32)
    g = np.asarray(inputs["L0_ffn_ln_g"], dtype=np.float32)
    bb = np.asarray(inputs["L0_ffn_ln_b"], dtype=np.float32)
    DI, DP = 2730, 2816
    wqk = np.zeros((n_sub, 8, 128, NCH * 384), dtype=np.float32)
    for i in range(n_sub):
        for hp in range(8):
            pair = np.concatenate([
                Wqkv[i][:, hp * 128:(hp + 1) * 128],
                Wqkv[i][:, C + hp * 128: C + (hp + 1) * 128],
                Wqkv[i][:, 2 * C + hp * 128: 2 * C + (hp + 1) * 128]], axis=1)
            wqk[i, hp] = _chunked(pair)
    w1pad = np.zeros((n_sub, C, 2 * DP), dtype=np.float32)
    w1pad[:, :, 0:DI] = W1[:n_sub, :, 0:DI]
    w1pad[:, :, DP:DP + DI] = W1[:n_sub, :, DI:2 * DI]
    w1p = np.zeros((n_sub, 22, 128, NCH * 256), dtype=np.float32)
    for i in range(n_sub):
        for ep in range(22):
            pair = np.concatenate([
                w1pad[i][:, ep * 128:(ep + 1) * 128],
                w1pad[i][:, DP + ep * 128: DP + (ep + 1) * 128]], axis=1)
            w1p[i, ep] = _chunked(pair)
    w2full = np.zeros((n_sub, DP, C), dtype=np.float32)
    w2full[:, 0:DI, :] = W2[:n_sub]
    w2p = np.zeros((n_sub, 8, 128, 22 * 128), dtype=np.float32)
    for i in range(n_sub):
        for outc in range(8):
            w2p[i, outc] = _chunked(w2full[i][:, outc * 128:(outc + 1) * 128])
    b1pad = np.zeros((n_sub, 2 * DP), dtype=np.float32)
    b1pad[:, 0:DI] = b1[:n_sub, 0:DI]
    b1pad[:, DP:DP + DI] = b1[:n_sub, DI:2 * DI]
    b1h = np.zeros((128, n_sub * 44), dtype=np.float32)
    b2h = np.zeros((128, n_sub * 8), dtype=np.float32)
    qsh = np.zeros((128, n_sub), dtype=np.float32)
    ksh = np.zeros((128, n_sub), dtype=np.float32)
    gh = np.zeros((128, n_sub * 8), dtype=np.float32)
    bh = np.zeros((128, n_sub * 8), dtype=np.float32)
    for i in range(n_sub):
        for ep in range(22):
            b1h[:, i * 44 + ep] = b1pad[i, ep * 128:(ep + 1) * 128]
            b1h[:, i * 44 + 22 + ep] = b1pad[i, DP + ep * 128: DP + (ep + 1) * 128]
        for cchn in range(8):
            b2h[:, i * 8 + cchn] = b2[i, cchn * 128:(cchn + 1) * 128]
            gh[:, i * 8 + cchn] = g[i, cchn * 128:(cchn + 1) * 128]
            bh[:, i * 8 + cchn] = bb[i, cchn * 128:(cchn + 1) * 128]
        qsh[:, i] = np.tile(qs[i], 2)
        ksh[:, i] = np.tile(ks[i], 2)
    return dict(wqk=np.ascontiguousarray(wqk),
                wob=np.ascontiguousarray(np.stack(
                    [_slot8(Wo[i]) for i in range(n_sub)])),
                w1p=np.ascontiguousarray(w1p), w2p=np.ascontiguousarray(w2p),
                b1p=b1h, b2p=b2h, qsp=qsh, ksp=ksh, lngp=gh, lnbp=bh)


def run_stage_b(x_full, inputs, n_sub=6):
    nc = build_stage_b(n_sub=n_sub)
    wd = _prep_b_weights(inputs, n_sub)
    cosS, sinS, cosI, sinI = _xpos_tables()
    maska, mask0 = _local_masks()
    perm = _perm_rotate_half(64)
    idn = np.eye(128, dtype=np.float32).astype(BF)
    in_maps = []
    for core in range(N_CORES):
        xc = np.asarray(x_full[core * B_CORE:(core + 1) * B_CORE], dtype=np.float32)
        im = {"xt": _to_xt(xc), "cosS": cosS, "sinS": sinS, "cosI": cosI,
              "sinI": sinI, "perm": perm, "idn": idn, "maska": maska,
              "mask0": mask0}
        im.update(wd)
        in_maps.append(im)
    res = run_bass_kernel_spmd(nc, in_maps, core_ids=list(range(N_CORES)))
    outs = [_from_xt(r["xout"]) for r in res.results]
    return np.concatenate(outs, axis=0)


def _to_xt(x_core):
    xt = x_core.transpose(2, 0, 1).reshape(NCH, 128, B_CORE * T)
    return np.ascontiguousarray(xt)


def _from_xt(xt):
    return np.ascontiguousarray(
        np.asarray(xt, dtype=np.float32).reshape(NCH, 128, B_CORE, T)
        .transpose(2, 3, 0, 1).reshape(B_CORE, T, C))


def run_stage_a(x_full, Wq, Wk, Wv, Wo):
    nc = build_stage_a()
    cos, sin = _rope_tables_l0attn()
    consts = _consts_common()
    wq = _slot8(np.asarray(Wq, dtype=np.float32).T)
    wk = _slot8(np.asarray(Wk, dtype=np.float32).T)
    wv = _chunked(np.asarray(Wv, dtype=np.float32).T)
    wo = _slot8(np.asarray(Wo, dtype=np.float32).T)
    perm = _perm_rotate_half(64)
    in_maps = []
    for core in range(N_CORES):
        xc = np.asarray(x_full[core * B_CORE:(core + 1) * B_CORE], dtype=np.float32)
        im = {"xt": _to_xt(xc), "wqt": wq, "wkt": wk, "wvt": wv, "wot": wo,
              "cos0": cos, "sin0": sin, "perm": perm, "masks": consts["masks"]}
        in_maps.append(im)
    res = run_bass_kernel_spmd(nc, in_maps, core_ids=list(range(N_CORES)))
    outs = [_from_xt(r["xout"]) for r in res.results]
    return np.concatenate(outs, axis=0)



def build_fused(n_sub=6):
    nc = bass.Bass("TRN2", target_bir_lowering=False)
    D = lambda name, shape, dt=F32: nc.dram_tensor(name, shape, dt, kind="ExternalInput").ap()
    xt = D("xt", [NCH, 128, B_CORE * T])
    x1t = nc.dram_tensor("x1t", [NCH, 128, B_CORE * T], F32, kind="Internal").ap()
    x2t = nc.dram_tensor("x2t", [NCH, 128, B_CORE * T], F32, kind="Internal").ap()
    x3t = nc.dram_tensor("x3t", [NCH, 128, B_CORE * T], F32, kind="Internal").ap()
    xout = nc.dram_tensor("xout", [NCH, 128, B_CORE * T], F32, kind="ExternalOutput").ap()
    a = NS(wqt=D("a_wqt", [8, 128, NCH * 128]), wkt=D("a_wkt", [8, 128, NCH * 128]),
           wvt=D("a_wvt", [128, NCH * C]),
           wot=D("a_wot", [8, 128, NCH * 128]), cos0=D("a_cos0", [128, T], BF16),
           sin0=D("a_sin0", [128, T], BF16), perm=D("a_perm", [128, 128], BF16),
           masks=D("a_masks", [128, 896], BF16))
    b = NS(wqk=D("b_wqk", [n_sub, 8, 128, NCH * 384]),
           wob=D("b_wob", [n_sub, 8, 128, NCH * 128]),
           w1p=D("b_w1p", [n_sub, 22, 128, NCH * 256]),
           w2p=D("b_w2p", [n_sub, 8, 128, 22 * 128]),
           b1p=D("b_b1p", [128, n_sub * 44]), b2p=D("b_b2p", [128, n_sub * 8]),
           qsp=D("b_qsp", [128, n_sub]), ksp=D("b_ksp", [128, n_sub]),
           lngp=D("b_lngp", [128, n_sub * 8]), lnbp=D("b_lnbp", [128, n_sub * 8]),
           cosS=D("b_cosS", [128, T], BF16), sinS=D("b_sinS", [128, T], BF16),
           cosI=D("b_cosI", [128, T], BF16), sinI=D("b_sinI", [128, T], BF16),
           perm=D("b_perm", [128, 128], BF16), idn=D("b_idn", [128, 128], BF16),
           maska=D("b_maska", [128, 384], BF16), mask0=D("b_mask0", [128, 2 * 256], BF16))
    c = NS(waq=D("c_waq", [8, 128, NCH * 128]), wak=D("c_wak", [8, 128, NCH * 128]),
           wav=D("c_wav", [2, 128, NCH * 512]),
           wproj=D("c_wproj", [8, 128, NCH * 128]),
           lvin=D("c_lvin", [128, 4]), cos1=D("c_cos1", [128, T], BF16),
           sin1=D("c_sin1", [128, T], BF16), perm32=D("c_perm32", [128, 128], BF16),
           masks=D("c_masks", [128, 896], BF16))
    d = NS(u1=D("d_u1", [128, NCH * 256]), v1=D("d_v1", [128, 2 * 4096]),
           b1in=D("d_b1in", [128, 32]), u2=D("d_u2", [128, 32 * 256]),
           v2=D("d_v2", [128, 2 * C]), b2in=D("d_b2in", [128, 8]))

    with contextlib.ExitStack() as ctx:
        p = Prog(nc, ctx)
        p.dma_sem = {"sync": p.sem("s_dsync"), "gpsimd": p.sem("s_dgp")}
        p.eng_sems = (p.sem("s_pe"), p.sem("s_act"), p.sem("s_dve"))
        with contextlib.ExitStack() as c1:
            _emit_stage_a(nc, c1, p, xt, a.wqt, a.wkt, a.wvt, a.wot,
                          a.cos0, a.sin0, a.perm, a.masks, x1t)
        p.barrier()
        with contextlib.ExitStack() as c2:
            _emit_stage_b(nc, c2, p, x1t, b.wqk, b.wob, b.w1p, b.w2p, b.b1p,
                          b.b2p, b.qsp, b.ksp, b.lngp, b.lnbp, b.cosS, b.sinS,
                          b.cosI, b.sinI, b.perm, b.idn, b.maska, b.mask0,
                          x2t, n_sub)
        p.barrier()
        with contextlib.ExitStack() as c3:
            _emit_stage_c(nc, c3, p, x2t, c.waq, c.wak, c.wav, c.wproj,
                          c.lvin, c.cos1, c.sin1, c.perm32, c.masks, x3t)
        p.barrier()
        with contextlib.ExitStack() as c4:
            store_last = _emit_stage_d(nc, c4, p, x3t, d.u1, d.v1, d.b1in,
                                       d.u2, d.v2, d.b2in, xout)
        p.op("sync", lambda e: e.nop(), waits=[store_last])
        with nc.Block() as block:
            p.emit(block)
    return nc


def run_fused(inputs, n_sub=6):
    x = np.asarray(inputs["L0_attn_x"], dtype=np.float32)
    nc = build_fused(n_sub=n_sub)
    consts = _consts_common()
    cos0, sin0 = _rope_tables_l0attn()
    cos1, sin1 = _rope_tables_l1()
    cosS, sinS, cosI, sinI = _xpos_tables()
    maska, mask0 = _local_masks()
    wd = _prep_b_weights(inputs, n_sub)
    wa_c = _deinterleave_wa(inputs["L1_attn_W_attn"])
    lq1, lk1 = inputs["L1_attn_lambda_q1"], inputs["L1_attn_lambda_k1"]
    lq2, lk2 = inputs["L1_attn_lambda_q2"], inputs["L1_attn_lambda_k2"]
    lv = np.zeros((128, 4), dtype=np.float32)
    lv[0:32, 0] = np.asarray(lq1); lv[0:32, 1] = np.asarray(lk1)
    lv[0:32, 2] = np.asarray(lq2); lv[0:32, 3] = np.asarray(lk2)
    b1h = np.ascontiguousarray(np.asarray(inputs["L1_ffn_b1"], dtype=np.float32).reshape(32, 128).T)
    b2h = np.ascontiguousarray(np.asarray(inputs["L1_ffn_b2"], dtype=np.float32).reshape(8, 128).T)
    common = {
        "a_wqt": _slot8(np.asarray(inputs["L0_attn_Wq"], dtype=np.float32).T),
        "a_wkt": _slot8(np.asarray(inputs["L0_attn_Wk"], dtype=np.float32).T),
        "a_wvt": _chunked(np.asarray(inputs["L0_attn_Wv"], dtype=np.float32).T),
        "a_wot": _slot8(np.asarray(inputs["L0_attn_Wo"], dtype=np.float32).T),
        "a_cos0": cos0, "a_sin0": sin0, "a_perm": _perm_rotate_half(64),
        "a_masks": consts["masks"],
        "b_cosS": cosS, "b_sinS": sinS, "b_cosI": cosI, "b_sinI": sinI,
        "b_perm": _perm_rotate_half(64),
        "b_idn": np.eye(128, dtype=np.float32).astype(BF),
        "b_maska": maska, "b_mask0": mask0,
        "c_wproj": _slot8(np.asarray(inputs["L1_attn_W_proj"], dtype=np.float32)),
        "c_lvin": lv, "c_cos1": cos1, "c_sin1": sin1,
        "c_waq": _slot8(wa_c[:, 0:C]), "c_wak": _slot8(wa_c[:, C:2 * C]),
        "c_wav": np.ascontiguousarray(np.stack(
            [_chunked(wa_c[:, 2 * C + h * 512:2 * C + (h + 1) * 512]) for h in range(2)])),
        "c_perm32": _perm_rotate_half(32), "c_masks": consts["masks"],
        "d_u1": _chunked(inputs["L1_ffn_U1"]), "d_v1": _chunked(inputs["L1_ffn_V1"]),
        "d_b1in": b1h,
        "d_u2": _chunked(inputs["L1_ffn_U2"]), "d_v2": _chunked(inputs["L1_ffn_V2"]),
        "d_b2in": b2h,
    }
    for k, v in wd.items():
        common["b_" + k] = v
    in_maps = []
    for core in range(N_CORES):
        xc = x[core * B_CORE:(core + 1) * B_CORE]
        im = dict(common)
        im["xt"] = _to_xt(xc)
        in_maps.append(im)
    res = run_bass_kernel_spmd(nc, in_maps, core_ids=list(range(N_CORES)))
    outs = [_from_xt(r["xout"]) for r in res.results]
    return np.concatenate(outs, axis=0)

def kernel(**inputs):
    return run_fused(inputs, n_sub=6)
